# version 05
# baseline (speedup 1.0000x reference)
import numpy as np
import concourse.bass as bass
import concourse.mybir as mybir
from concourse.bass_utils import run_bass_kernel_spmd

F32 = mybir.dt.float32
BF16 = mybir.dt.bfloat16
AF = mybir.ActivationFunctionType
ALU = mybir.AluOpType
AX = mybir.AxisListType

NCORES = 8
D = 1024
CH = 128
MAGIC = float(np.float32(12582912.0))
ENGINES = ("sync", "scalar", "vector", "tensor", "gpsimd")


class Sem:
    def __init__(self, nc, name):
        self.h = nc.alloc_semaphore(name)
        self.v = 0
    def inc(self, k=1):
        self.v += k
        return self.v


class Builder:
    def __init__(self, nc):
        self.nc = nc
        self.items = {e: [] for e in ENGINES}
        self.sems = {}
    ALIASES = {
        "s_ld2": "s_ld", "s_prep2": "s_prep", "s_mod2": "s_mod", "s_mev2": "s_mev",
        "s_mpub2": "s_mpub", "s_ccm2": "s_ccm", "s_mld2": "s_mld", "s_gab2": "s_gab",
        "s_adv2": "s_adv",
        "s_cmm": "s_umm", "s_ytr": "s_vtr",
        "fev": "pnv", "fdmm": "fmm", "ffres": "s_res", "fpw0": "fsp0", "fpw1": "fsp1",
        "frl0": "fwl0", "frl1": "fwl1", "arpub": "arA", "arrb": "arA",
        "arpv": "pnxv", "arrpm": "pnxt", "pnq1": "pnc", "pnq2": "pnq",
        "pmx": "pnxt", "fmx": "pnxv",
        "gsp0": "fsp0", "gsp1": "fsp1", "hag": "fsp0", "hlo": "fwl0",
    }
    def S(self, name):
        name = self.ALIASES.get(name, name)
        if name not in self.sems:
            self.sems[name] = Sem(self.nc, name)
        return self.sems[name]
    def emit(self, eng, fn):
        self.items[eng].append(fn)
    def dma(self, eng, out, in_, sem, **kw):
        sem.inc(16)
        self.emit(eng, lambda e, out=out, in_=in_, h=sem.h, kw=kw:
                  e.dma_start(out=out, in_=in_, **kw).then_inc(h, 16))
        return sem.v
    def wait(self, eng, sem, val=None):
        v = sem.v if val is None else val
        if v > 0:
            self.emit(eng, lambda e, h=sem.h, v=v: e.wait_ge(h, v))
    def op(self, eng, method, *args, inc=None, **kw):
        if inc is not None:
            inc.inc(1)
        def fn(e, method=method, args=args, kw=kw, inc=inc):
            r = getattr(e, method)(*args, **kw)
            if inc is not None:
                r.then_inc(inc.h, 1)
        self.emit(eng, fn)
        return None if inc is None else inc.v
    def fop(self, eng, method, *args, **kw):
        f = self.S("fence_" + eng)
        self.op(eng, method, *args, inc=f, **kw)
        self.wait(eng, f, f.v)
    def cc(self, kind, op_, ins_, outs_, sem):
        sem.inc(1)
        self.emit("gpsimd", lambda e, kind=kind, op_=op_, i=ins_, o=outs_, h=sem.h:
                  e.collective_compute(kind, op_, replica_groups=[list(range(NCORES))],
                                       ins=i, outs=o).then_inc(h, 1))
        return sem.v


def build_full(S=16384, debug=(), phases=6):
    FF = 3072
    TCORE = S // NCORES
    NTILE = TCORE // 128
    NCH = S // CH
    OWNCH = NTILE
    NPIECE = min(4, TCORE // 128)
    PTOK = S // NPIECE
    PCORE = TCORE // NPIECE
    PTILE = NTILE // NPIECE
    VSPAN = min(512, PCORE)
    MODS = 6 * D // NCORES
    MFS = MODS // 128

    nc = bass.Bass("TRN2", target_bir_lowering=False)
    B = Builder(nc)
    S_, dma, wait, op, cc = B.S, B.dma, B.wait, B.op, B.cc
    fop = B.fop

    def din(name, shape, dt=F32):
        return nc.dram_tensor(name, shape, dt, kind="ExternalInput").ap()
    xin = din("xin", [TCORE, D])
    w0t = din("w0t", [D, 128])
    wkvvt = din("wkvvt", [D, 128])
    bkvv = din("bkvv", [128, 1])
    eta_h = din("eta", [1, 1])
    c0t = din("c0t", [128, 8])
    wmod0s = din("wmod0s", [D, MODS])
    c1t = din("c1t", [128, 8])
    wmod1s = din("wmod1s", [D, MODS])
    g0s = din("g0s", [3 * 128, D])
    u0s = din("u0s", [3 * 128, D])
    d0s = din("d0s", [128, 3 * D])
    g1s = din("g1s", [3 * 128, D])
    u1s = din("u1s", [3 * 128, D])
    d1s = din("d1s", [128, 3 * D])
    wqkvt = din("wqkvt", [D, 3 * D])
    woutt = din("woutt", [D, D])
    wl1t = din("wl1t", [1, 64])
    bl1c = din("bl1c", [64, 1])
    wl2t = din("wl2t", [64, 16])
    bl2c = din("bl2c", [16, 1])
    pos = din("pos", [1, TCORE])
    boutc = din("boutc", [128, 8])
    onescolbf = din("onescolbf", [128, 1], BF16)
    identbf = din("identbf", [128, 128], BF16)
    identf = din("identf", [128, 128], F32)
    onesf = din("onesf", [1, 128], F32)
    trium = din("trium", [128, 128], F32)

    douts = {}
    def dout(name, shape, dt=F32):
        douts[name] = nc.dram_tensor(name, shape, dt, kind="ExternalOutput").ap()
        return douts[name]
    dbg_y = dout("dbg_y", [128, S], BF16) if "y" in debug else None
    dbg_hfm = dout("dbg_hfm", [128, 8 * TCORE], BF16) if "h" in debug else None
    dbg_mod = dout("dbg_mod", [128, 48], F32) if "mod" in debug else None
    dbg_v = dout("dbg_v", [128, S], F32) if "v" in debug else None
    dbg_c1 = dout("dbg_c1", [128, 8 * 128 + 1024 + 256 + 128 + 128 + 1024 + 512], F32) if "c1" in debug else None
    dbg_res = dout("dbg_res", [128, 8 * TCORE], F32) if "res" in debug else None
    dbg_wq = dout("dbg_wq", [128, 256], BF16) if "wq" in debug else None
    dbg_h2 = dout("dbg_h2", [128, 8 * TCORE], BF16) if "h2" in debug else None
    dbg_x2 = dout("dbg_x2", [128, 8 * TCORE], F32) if "x2" in debug else None
    dbg_x3 = dout("dbg_x3", [128, 8 * TCORE], F32) if "x3" in debug else None

    agin_mod = nc.dram_tensor("agin_mod", [1, MODS], F32)
    agout_mod = nc.dram_tensor("agout_mod", [NCORES, MODS], F32, addr_space="Shared")
    agin_h = nc.dram_tensor("agin_h", [1, D * TCORE * 2], BF16)
    agout_h = nc.dram_tensor("agout_h", [NCORES, D * TCORE * 2], BF16, addr_space="Shared")
    agin_g = nc.dram_tensor("agin_g", [1, OWNCH * 2 * 128 * 128], BF16)
    agout_g = nc.dram_tensor("agout_g", [NCORES, OWNCH * 2 * 128 * 128], BF16, addr_space="Shared")
    agin_y = [nc.dram_tensor(f"agin_y{p}", [128, PTOK], BF16) for p in range(NPIECE)]
    agout_yall = nc.dram_tensor("agout_yall", [NPIECE * NCORES * 128, PTOK], BF16, addr_space="Shared")
    agout_y = [agout_yall.ap()[p * NCORES * 128:(p + 1) * NCORES * 128, :] for p in range(NPIECE)]
    WDQN = 3 * FF * 128
    agin_wdq = [nc.dram_tensor(f"agin_wdq{l}", [1, WDQN], BF16) for l in range(2)]
    agout_wdq = [nc.dram_tensor(f"agout_wdq{l}", [NCORES, WDQN], BF16, addr_space="Shared") for l in range(2)]
    MXW = [1, 2, 1, 1, 1, 2, 1, 1]
    agin_mx = [nc.dram_tensor(f"agin_mx{j}", [1, MXW[j]], F32) for j in range(8)]
    agout_mx = [nc.dram_tensor(f"agout_mx{j}", [NCORES, MXW[j]], F32, addr_space="Shared") for j in range(8)]
    dscr1 = nc.dram_tensor("dscr1", [FF, TCORE], BF16)
    dscr2 = nc.dram_tensor("dscr2", [FF, TCORE], BF16)
    out_h = nc.dram_tensor("out", [TCORE, D], F32, kind="ExternalOutput").ap() if phases >= 6 else None

    sb = lambda name, shape, dt=F32: nc.alloc_sbuf_tensor(name, shape, dt).ap()
    A = sb("arenaA", [128, S], F32)
    hfm = sb("hfm", [128, 8 * TCORE], BF16)
    if S >= 16384:
        htok = A.bitcast(BF16)[:, 2 * MODS * 8:2 * MODS * 8 + NTILE * D]
    else:
        htok = sb("htok", [128, NTILE * D], BF16)
    xtile = [sb(f"xtile{j}", [128, D]) for j in range(2)]
    xhat = sb("xhat", [128, D])
    tt1 = sb("tt1", [128, D])
    carep = sb("carep", [128, D])
    sarep = sb("sarep", [128, D])
    rows8 = sb("rows8", [64, 128])
    rowsflat = sb("rowsflat", [1, 16 * 128])
    stats = sb("stats", [128, 8 * NTILE])
    modT = sb("modT", [128, 96])
    csc = sb("csc", [128, 16])
    wmods = A[:, 0:8 * MODS] if S >= 16384 else sb("wmods", [128, 8 * MODS])
    cval = sb("cval", [128, 1])
    ngc = sb("ngc", [128, 1])
    ident_b = sb("ident_b", [128, 128], BF16)
    ones_cb = sb("ones_cb", [128, 1], BF16)
    ones_rb = sb("ones_rb", [1, 128], BF16)
    boutc_sb = sb("boutc_sb", [128, 8])
    bl1c_sb = sb("bl1c_sb", [64, 1])
    bl2c_sb = sb("bl2c_sb", [16, 1])
    wl2b = sb("wl2b", [64, 16], BF16)
    lamsc = sb("lamsc", [16, TCORE], BF16)
    rows2 = sb("rows2", [1, max(TCORE, 1024)])
    mxp = sb("mxp", [128, 8])
    mxr = sb("mxr", [1, 24])
    qcols = sb("qcols", [128, 48])
    ident_f = sb("ident_f", [128, 128], F32)
    ones_f = sb("ones_f", [1, 128], F32)
    triu_f = sb("triu_f", [128, 128], F32)
    w0t_sb = sb("w0t_sb", [128, 8 * 128])
    wkvvt_sb = sb("wkvvt_sb", [128, 8 * 128], BF16)
    bkvv_sb = sb("bkvv_sb", [128, 1])
    gup = sb("gup", [128, 128])
    nlo = sb("nlo", [128, 128], BF16)
    nup = sb("nup", [128, 128], BF16)
    ilq = sb("ilq", [128, 128], BF16)
    rbuf = sb("rbuf", [128, 128], BF16)
    gpub = sb("gpub", [128, OWNCH * 256], BF16)
    wbhi = sb("wbhi", [128, 8 * 128], BF16)
    wblo = sb("wblo", [128, 8 * 128], BF16)
    xf_c = [sb(f"xf_c{j}", [128, 8 * 128], BF16) for j in range(2)]
    xt_c = [sb(f"xt_c{j}", [128, D], BF16) for j in range(2)]
    gt_c = [sb(f"gt_c{j}", [128, 2 * 128], BF16) for j in range(2)]
    vxf = [sb(f"vxf{j}", [128, 8 * VSPAN], BF16) for j in range(2)]
    d_c = sb("d_c", [128, 128], BF16)
    u_c = sb("u_c", [128, 128], BF16)
    y_c = sb("y_c", [128, 128], BF16)
    yt_c2 = [sb(f"yt_c2{j}", [128, 128], BF16) for j in range(2)]
    vstage = sb("vstage", [128, VSPAN])

    pb = [nc.alloc_psum_tensor(f"pb{k}", [128, 512], F32).ap() for k in range(8)]
    pbb = [p.bitcast(BF16) for p in pb]
    def WP(k):
        return pb[k // 4][:, (k % 4) * 128:(k % 4) * 128 + 128]

    s_ld = S_("s_ld")
    dma("sync", ident_b[:], identbf[:, :], s_ld)
    dma("sync", ident_f[:], identf[:, :], s_ld)
    dma("sync", ones_f[:], onesf[:, :], s_ld)
    dma("sync", triu_f[:], trium[:, :], s_ld)
    dma("sync", w0t_sb[:].rearrange("p (k f) -> p k f", f=128), w0t.rearrange("(k p) f -> p k f", p=128), s_ld)
    s_gld = S_("s_gld")
    dma("gpsimd", wkvvt_sb[:].rearrange("p (k f) -> p k f", f=128), wkvvt.rearrange("(k p) f -> p k f", p=128), s_gld)
    dma("sync", bkvv_sb[:], bkvv[:, :], s_ld)
    dma("sync", ones_cb[:], onescolbf[:, :], s_ld)
    dma("sync", boutc_sb[:], boutc[:, :], s_ld)
    dma("sync", bl1c_sb[:], bl1c[:, :], s_ld)
    dma("sync", bl2c_sb[:], bl2c[:, :], s_ld)
    s_gld2 = S_("hag")
    dma("gpsimd", wl2b[:], wl2t[:, :], s_gld2)
    dma("sync", csc[:, 0:8], c0t[:, :], s_ld)
    dma("sync", wmods[:].rearrange("p (k f) -> p k f", f=768), wmod0s.rearrange("(k p) f -> p k f", p=128), s_ld)
    dma("sync", cval[0:1, 0:1], eta_h[:, :], s_ld)
    LD = s_ld.v

    s_prep = S_("s_prep")
    wait("scalar", s_ld, LD)
    op("scalar", "mul", cval[0:1, 0:1], cval[0:1, 0:1], float(2.0 / D))
    op("scalar", "copy", ones_rb[:], ones_f[:])
    op("scalar", "activation", csc[:, 0:8], csc[:, 0:8], AF.Silu, inc=s_prep)

    s_cv = S_("s_cv")
    wait("tensor", s_prep)
    op("tensor", "matmul", pb[7][:, 0:1], ones_f[0:1, :], cval[0:1, 0:1], start=True, stop=True, inc=s_cv)
    s_cv2 = S_("s_cv2")
    wait("scalar", s_cv)
    op("scalar", "copy", cval[:, 0:1], pb[7][:, 0:1], inc=s_cv2)
    wait("scalar", s_cv2, s_cv2.v)
    op("scalar", "mul", ngc[:, 0:1], cval[:, 0:1], -1.0, inc=s_cv2)
    CV2 = s_cv2.v

    s_mod = S_("s_mod")
    wait("tensor", s_cv2, CV2)
    for mf in range(MFS):
        for hc in range(8):
            op("tensor", "matmul", pb[7][:, 128 + mf:129 + mf],
               wmods[:, hc * MODS + mf * 128: hc * MODS + mf * 128 + 128],
               csc[:, hc:hc + 1],
               start=(hc == 0), stop=(hc == 7), inc=(s_mod if hc == 7 else None))
    s_mev = S_("s_mev")
    wait("scalar", s_mod, MFS)
    op("scalar", "copy", modT[:, 0:MFS], pb[7][:, 128:128 + MFS], inc=s_mev)
    s_mpub = S_("s_mpub")
    wait("gpsimd", s_mev)
    dma("gpsimd", agin_mod.ap().rearrange("o (p k) -> p (o k)", p=128), modT[:, 0:MFS], s_mpub)
    wait("gpsimd", s_mpub)
    s_ccm = S_("s_ccm")
    cc("AllGather", ALU.bypass, [agin_mod.ap()], [agout_mod.ap()], s_ccm)
    wait("gpsimd", s_ccm)
    s_mld = S_("s_mld")
    dma("gpsimd", modT[:, 0:48].rearrange("p (r k) -> p r k", k=MFS),
        agout_mod.ap().rearrange("r (p k) -> p r k", p=128), s_mld)
    MLD = s_mld.v

    s_adv = S_("s_adv")
    wait("vector", s_mld, MLD)
    op("vector", "tensor_scalar", modT[:, 8:16], modT[:, 8:16], 1.0, None, op0=ALU.add)
    op("vector", "tensor_scalar", modT[:, 32:40], modT[:, 32:40], 1.0, None, op0=ALU.add, inc=s_adv)
    s_gab = S_("s_gab")
    wait("scalar", s_mld, MLD)
    op("scalar", "activation", tt1[:, 0:8], modT[:, 16:24], AF.Abs)
    op("scalar", "activation", tt1[:, 8:16], modT[:, 16:24], AF.Sign)
    op("scalar", "activation", tt1[:, 16:24], modT[:, 40:48], AF.Abs)
    op("scalar", "activation", tt1[:, 24:32], modT[:, 40:48], AF.Sign, inc=s_gab)
    wait("vector", s_gab)
    fop("vector", "tensor_scalar", tt1[:, 0:8], tt1[:, 0:8], 0.25, None, op0=ALU.max)
    fop("vector", "tensor_scalar", tt1[:, 16:24], tt1[:, 16:24], 0.25, None, op0=ALU.max)
    op("vector", "tensor_tensor", modT[:, 16:24], tt1[:, 0:8], tt1[:, 8:16], op=ALU.mult)
    op("vector", "tensor_tensor", modT[:, 40:48], tt1[:, 16:24], tt1[:, 24:32], op=ALU.mult, inc=s_adv)
    ADV = s_adv.v
    if dbg_mod is not None:
        s_dmod = S_("s_dmod")
        wait("gpsimd", s_adv, ADV)
        dma("gpsimd", dbg_mod[:, :], modT[:, 0:48], s_dmod)

    agin_mod2 = nc.dram_tensor("agin_mod2", [1, MODS], F32)
    agout_mod2 = nc.dram_tensor("agout_mod2", [NCORES, MODS], F32, addr_space="Shared")
    s_ld2 = S_("s_ld2")
    wait("sync", s_mod, MFS)
    dma("sync", wmods[:].rearrange("p (k f) -> p k f", f=MODS), wmod1s.rearrange("(k p) f -> p k f", p=128), s_ld2)
    dma("sync", csc[:, 8:16], c1t[:, :], s_ld2)
    s_prep2 = S_("s_prep2")
    wait("scalar", s_ld2, s_ld2.v)
    op("scalar", "activation", csc[:, 8:16], csc[:, 8:16], AF.Silu, inc=s_prep2)
    s_mod2 = S_("s_mod2")
    wait("tensor", s_prep2, s_prep2.v)
    wait("tensor", s_mev, 1)
    for mf in range(MFS):
        for hc in range(8):
            op("tensor", "matmul", pb[7][:, 128 + mf:129 + mf],
               wmods[:, hc * MODS + mf * 128: hc * MODS + mf * 128 + 128],
               csc[:, 8 + hc:9 + hc],
               start=(hc == 0), stop=(hc == 7), inc=(s_mod2 if hc == 7 else None))
    s_mev2 = S_("s_mev2")
    wait("scalar", s_mod2, s_mod2.v)
    op("scalar", "copy", modT[:, 48:48 + MFS], pb[7][:, 128:128 + MFS], inc=s_mev2)
    s_mpub2 = S_("s_mpub2")
    wait("gpsimd", s_mev2)
    dma("gpsimd", agin_mod2.ap().rearrange("o (p k) -> p (o k)", p=128), modT[:, 48:48 + MFS], s_mpub2)
    wait("gpsimd", s_mpub2)
    s_ccm2 = S_("s_ccm2")
    cc("AllGather", ALU.bypass, [agin_mod2.ap()], [agout_mod2.ap()], s_ccm2)
    wait("gpsimd", s_ccm2)
    s_mld2 = S_("s_mld2")
    dma("gpsimd", modT[:, 48:96].rearrange("p (r k) -> p r k", k=MFS),
        agout_mod2.ap().rearrange("r (p k) -> p r k", p=128), s_mld2)
    s_adv2 = S_("s_adv2")
    wait("vector", s_mld2, s_mld2.v)
    op("vector", "tensor_scalar", modT[:, 56:64], modT[:, 56:64], 1.0, None, op0=ALU.add)
    op("vector", "tensor_scalar", modT[:, 80:88], modT[:, 80:88], 1.0, None, op0=ALU.add, inc=s_adv2)
    s_gab2 = S_("s_gab2")
    wait("scalar", s_mld2, s_mld2.v)
    wait("scalar", s_adv, ADV)
    op("scalar", "activation", tt1[:, 32:40], modT[:, 64:72], AF.Abs)
    op("scalar", "activation", tt1[:, 40:48], modT[:, 64:72], AF.Sign)
    op("scalar", "activation", tt1[:, 48:56], modT[:, 88:96], AF.Abs)
    op("scalar", "activation", tt1[:, 56:64], modT[:, 88:96], AF.Sign, inc=s_gab2)
    wait("vector", s_gab2, s_gab2.v)
    fop("vector", "tensor_scalar", tt1[:, 32:40], tt1[:, 32:40], 0.25, None, op0=ALU.max)
    fop("vector", "tensor_scalar", tt1[:, 48:56], tt1[:, 48:56], 0.25, None, op0=ALU.max)
    op("vector", "tensor_tensor", modT[:, 64:72], tt1[:, 32:40], tt1[:, 40:48], op=ALU.mult)
    op("vector", "tensor_tensor", modT[:, 88:96], tt1[:, 48:56], tt1[:, 56:64], op=ALU.mult, inc=s_adv2)
    ADV2 = s_adv2.v

    s_tr = S_("s_tr")
    wait("tensor", s_adv, ADV)
    op("tensor", "transpose", pb[6][0:8, 0:128], modT[:, 8:16], ident_f[:])
    op("tensor", "transpose", pb[6][0:8, 128:256], modT[:, 0:8], ident_f[:], inc=s_tr)
    s_tr2 = S_("s_tr2")
    wait("scalar", s_tr)
    op("scalar", "copy", rows8[0:8, 0:128], pb[6][0:8, 0:128])
    op("scalar", "copy", rows8[32:40, 0:128], pb[6][0:8, 128:256], inc=s_tr2)
    s_rflat = S_("s_rflat")
    wait("gpsimd", s_tr2)
    dma("gpsimd", rowsflat[0:1, 0:8 * 128], rows8[0:8, :], s_rflat)
    dma("gpsimd", rowsflat[0:1, 8 * 128:16 * 128], rows8[32:40, :], s_rflat)
    s_rep = S_("s_rep")
    s_repev = S_("s_repev")
    wait("tensor", s_rflat)
    wait("tensor", s_mev, 1)
    for k in range(8):
        if k > 0:
            wait("tensor", s_repev, s_repev.v)
        op("tensor", "matmul", pb[7][:, 0:128], ones_f[0:1, :], rowsflat[0:1, k * 128:(k + 1) * 128], start=True, stop=True)
        op("tensor", "matmul", pb[7][:, 128:256], ones_f[0:1, :], rowsflat[0:1, (8 + k) * 128:(9 + k) * 128], start=True, stop=True, inc=s_rep)
        wait("scalar", s_rep, s_rep.v)
        op("scalar", "copy", carep[:, k * 128:k * 128 + 128], pb[7][:, 0:128])
        op("scalar", "copy", sarep[:, k * 128:k * 128 + 128], pb[7][:, 128:256], inc=s_repev)
    REPDONE = s_repev.v

    s_hst = S_("s_hst")
    s_hsc = S_("s_hsc")
    s_hv = S_("s_hv")
    s_htr = S_("s_htr")
    s_hev = S_("s_hev")
    s_gmm = S_("s_gmm")
    s_ovec = S_("s_ovec")
    s_osq = S_("s_osq")
    s_oev = S_("s_oev")
    s_hp = [S_("s_hp0")]
    s_gpubd = S_("s_gpubd")
    s_agdone = S_("s_agdone")

    s_xp = [S_("s_xp0"), S_("s_xp1")]
    dma("sync", xtile[0][:], xin[0:128, :], s_xp[0])
    for i in range(NTILE):
        buf = xtile[i % 2]
        if i + 1 < NTILE:
            if i >= 1:
                wait("sync", s_hsc, s_hsc.v)
                wait("sync", s_xp[(i + 1) % 2], s_xp[(i + 1) % 2].v)
            dma("sync", xtile[(i + 1) % 2][:], xin[(i + 1) * 128:(i + 2) * 128, :], s_xp[(i + 1) % 2])
        sc = stats[:, i * 8:(i + 1) * 8]
        wait("vector", s_xp[i % 2], 16 * (i // 2 + 1))
        if i == 0:
            wait("vector", s_repev, REPDONE)
            wait("vector", s_adv2, ADV2)
        op("vector", "tensor_reduce", sc[:, 0:1], buf[:], axis=AX.X, op=ALU.add)
        fop("vector", "tensor_tensor", tt1[:], buf[:], buf[:], op=ALU.mult)
        fop("vector", "tensor_reduce", sc[:, 1:2], tt1[:], axis=AX.X, op=ALU.add)
        fop("vector", "tensor_scalar", sc[:, 2:3], sc[:, 0:1], float(1.0 / D), None, op0=ALU.mult)
        fop("vector", "tensor_scalar", sc[:, 3:4], sc[:, 1:2], float(1.0 / D), None, op0=ALU.mult)
        fop("vector", "tensor_tensor", sc[:, 4:5], sc[:, 2:3], sc[:, 2:3], op=ALU.mult)
        fop("vector", "tensor_tensor", sc[:, 4:5], sc[:, 3:4], sc[:, 4:5], op=ALU.subtract)
        fop("vector", "tensor_scalar", sc[:, 4:5], sc[:, 4:5], 1e-5, None, op0=ALU.add)
        op("vector", "reciprocal", sc[:, 5:6], sc[:, 4:5], inc=s_hst)
        wait("scalar", s_hst, s_hst.v)
        op("scalar", "activation", sc[:, 6:7], sc[:, 5:6], AF.Sqrt, inc=s_hsc)
        wait("vector", s_hsc, s_hsc.v)
        fop("vector", "tensor_tensor", sc[:, 7:8], sc[:, 2:3], sc[:, 6:7], op=ALU.mult)
        op("vector", "tensor_scalar", sc[:, 7:8], sc[:, 7:8], -1.0, None, op0=ALU.mult, inc=s_hst)
        wait("scalar", s_hst, s_hst.v)
        op("scalar", "activation", xhat[:], buf[:], AF.Identity, scale=sc[:, 6:7], bias=sc[:, 7:8], inc=s_hsc)
        wait("vector", s_hsc, s_hsc.v)
        fop("vector", "tensor_tensor", tt1[:], xhat[:], carep[:], op=ALU.mult)
        op("vector", "tensor_tensor", htok[:, i * D:(i + 1) * D], tt1[:], sarep[:], op=ALU.add, inc=s_hv)
        wait("tensor", s_hv, s_hv.v)
        for kk in range(4):
            if not (i == 0 and kk == 0):
                wait("tensor", s_hev, s_hev.v)
            op("tensor", "transpose", pbb[6][:, 0:128], htok[:, i * D + 2 * kk * 128: i * D + (2 * kk + 1) * 128], ident_b[:])
            op("tensor", "transpose", pbb[6][:, 128:256], htok[:, i * D + (2 * kk + 1) * 128: i * D + (2 * kk + 2) * 128], ident_b[:], inc=s_htr)
            wait("scalar", s_htr, s_htr.v)
            op("scalar", "copy", hfm[:, 2 * kk * TCORE + i * 128: 2 * kk * TCORE + (i + 1) * 128], pbb[6][:, 0:128])
            op("scalar", "copy", hfm[:, (2 * kk + 1) * TCORE + i * 128: (2 * kk + 1) * TCORE + (i + 1) * 128], pbb[6][:, 128:256], inc=s_hev)
        ofs_tok = D * TCORE
        wait("gpsimd", s_hv, s_hv.v)
        dma("gpsimd",
            agin_h.ap()[0:1, ofs_tok + i * 128 * D: ofs_tok + (i + 1) * 128 * D]
            .rearrange("o (t f) -> (o t) f", f=D),
            htok[:, i * D:(i + 1) * D], s_hp[0])
        if i + 1 == NTILE:
            wait("gpsimd", s_hev, s_hev.v)
            dma("gpsimd",
                agin_h.ap()[0:1, 0:D * TCORE].rearrange("o (k pp t) -> (o pp) k t", pp=128, t=TCORE),
                hfm.rearrange("pp (k tc) -> pp k tc", tc=TCORE)[:, :, :],
                s_hp[0])
            wait("gpsimd", s_hp[0], 16 * (NTILE + 1))
            cc("AllGather", ALU.bypass, [agin_h.ap()], [agout_h.ap()], s_agdone)

        wait("tensor", s_hev, s_hev.v)
        wait("tensor", s_cv2, CV2)
        wait("tensor", s_ovec, s_ovec.v)
        for k in range(8):
            op("tensor", "matmul", pb[2][:, 0:128],
               hfm[:, k * TCORE + i * 128: k * TCORE + (i + 1) * 128],
               hfm[:, k * TCORE + i * 128: k * TCORE + (i + 1) * 128],
               start=(k == 0), stop=(k == 7), inc=(s_gmm if k == 7 else None))
        wait("vector", s_gmm, s_gmm.v)
        wait("vector", s_adv, ADV)
        fop("vector", "tensor_tensor", gup[:], pb[2][:, 0:128], triu_f[:], op=ALU.mult)
        B.sems["s_ovec"].inc(1) if False else None
        op("vector", "tensor_copy", gpub[:, i * 256 + 128: i * 256 + 256], gup[:], inc=s_ovec)
        fop("vector", "tensor_scalar", nup[:], gup[:], ngc[:, 0:1], None, op0=ALU.mult)
        op("vector", "tensor_tensor", rbuf[:], nup[:], ident_b[:], op=ALU.add, inc=s_ovec)
        wait("tensor", s_ovec, s_ovec.v)
        op("tensor", "transpose", pbb[3][:, 0:128], nup[:], ident_b[:], inc=s_osq)
        wait("scalar", s_osq, s_osq.v)
        op("scalar", "copy", nlo[:], pbb[3][:, 0:128], inc=s_oev)
        for j in range(6):
            wait("tensor", s_oev, s_oev.v)
            op("tensor", "matmul", pb[3][:, 0:128], nlo[:], nup[:], start=True, stop=True)
            op("tensor", "matmul", pb[3][:, 128:256], nup[:], nlo[:], start=True, stop=True, inc=s_osq)
            wait("scalar", s_osq, s_osq.v)
            op("scalar", "copy", nup[:], pb[3][:, 0:128])
            op("scalar", "copy", nlo[:], pb[3][:, 128:256], inc=s_oev)
            wait("vector", s_oev, s_oev.v)
            op("vector", "tensor_tensor", ilq[:], nlo[:], ident_b[:], op=ALU.add, inc=s_ovec)
            wait("tensor", s_ovec, s_ovec.v)
            op("tensor", "matmul", pb[4][:, 0:128], ilq[:], rbuf[:], start=True, stop=True, inc=s_osq)
            wait("scalar", s_osq, s_osq.v)
            op("scalar", "copy", rbuf[:], pb[4][:, 0:128], inc=s_oev)
        wait("vector", s_oev, s_oev.v)
        op("vector", "tensor_scalar", gpub[:, i * 256: i * 256 + 128], rbuf[:], cval[:, 0:1], None,
           op0=ALU.mult, inc=s_ovec)

    wait("gpsimd", s_ovec, s_ovec.v)
    dma("gpsimd", agin_g.ap().rearrange("o (pp f) -> (o pp) f", pp=128), gpub[:, :], s_gpubd)
    wait("gpsimd", s_gpubd, s_gpubd.v)
    cc("AllGather", ALU.bypass, [agin_g.ap()], [agout_g.ap()], s_agdone)
    AG_G = s_agdone.v

    if dbg_hfm is not None:
        s_dh = S_("s_dh")
        wait("gpsimd", s_hev, s_hev.v)
        for _dc in range(0, 8 * TCORE, 2048):
            dma("gpsimd", dbg_hfm[:, _dc:_dc + 2048], hfm[:, _dc:_dc + 2048], s_dh)

    s_wi = S_("s_wi")
    wait("tensor", s_ld, LD)
    for k in range(8):
        op("tensor", "matmul", WP(k), ident_f[:], w0t_sb[:, k * 128:(k + 1) * 128],
           start=(k % 4 == 0), stop=(k % 4 == 3), skip_group_check=True, inc=(s_wi if k == 7 else None))

    s_vx2 = [S_("s_vx20"), S_("s_vx21")]
    s_vmm = S_("s_vmm")
    s_vev = S_("s_vev")
    s_vtr = S_("s_vtr")
    s_vta = S_("s_vta")
    s_xf2 = [S_("s_xf20"), S_("s_xf21")]
    s_xt2 = [S_("ssp0"), S_("ssp1")]
    s_wcast = S_("s_wcast")
    s_wlo = S_("s_wlo")
    s_zmm = S_("s_zmm")
    s_dv = S_("s_dv")
    s_umm = S_("s_umm")
    s_uev = S_("s_uev")
    s_cmm = S_("s_cmm")
    s_yevac = S_("s_yevac")
    s_ytr = S_("s_ytr")
    s_ytev = S_("s_ytev")
    s_upd = S_("s_upd")
    s_yp2 = [S_("s_yp20"), S_("s_yp21")]
    YINC = 32 if dbg_y is not None else 16

    def emit_vspan(v):
        r = (v * VSPAN) // TCORE
        tok0 = (v * VSPAN) % TCORE
        buf = vxf[v % 2]
        svx = s_vx2[v % 2]
        wait("sync", s_agdone, 1)
        if v >= 2:
            wait("sync", s_vmm, v - 1)
        wait("sync", svx, svx.v)
        dma("sync", buf[:],
            agout_h.ap()[r:r + 1, 0:D * TCORE]
            .rearrange("o (k pp t) -> (o pp) k t", pp=128, t=TCORE)[:, :, tok0:tok0 + VSPAN], svx)
        wait("tensor", svx, svx.v)
        wait("tensor", B.sems["s_gld"], 16)
        wait("tensor", B.sems["s_repev"], B.sems["s_repev"].v)
        wait("tensor", s_ytev, s_ytev.v)
        wait("tensor", s_vta, s_vta.v)
        wait("tensor", s_vev, s_vev.v)
        for k in range(8):
            op("tensor", "matmul", pb[7][:, 0:VSPAN], wkvvt_sb[:, k * 128:(k + 1) * 128],
               buf[:, k * VSPAN:(k + 1) * VSPAN],
               start=(k == 0), stop=(k == 7), inc=(s_vmm if k == 7 else None))
        wait("scalar", s_vmm, s_vmm.v)
        op("scalar", "activation", vstage[:], pb[7][:, 0:VSPAN], AF.Identity,
           scale=1.0, bias=bkvv_sb[:, 0:1], inc=s_vev)
        wait("tensor", s_vev, s_vev.v)
        for tq in range(VSPAN // 128):
            if tq > 0:
                wait("tensor", s_vta, s_vta.v)
            op("tensor", "transpose", pb[6][:, 0:128], vstage[:, tq * 128:(tq + 1) * 128], ident_f[:], inc=s_vtr)
            wait("scalar", s_vtr, s_vtr.v)
            op("scalar", "copy", A[:, (v * VSPAN // 128 + tq) * 128:(v * VSPAN // 128 + tq + 1) * 128],
               pb[6][:, 0:128], inc=s_vta)

    def emit_chunk(c):
        par = c % 2
        zb = pb[2 + par]
        ub = pb[4 + par]
        xf, xt, gt = xf_c[par], xt_c[par], gt_c[par]
        r = (c * CH) // TCORE
        tokp = (c * CH) % TCORE

        sxf = s_xf2[par]
        if "s_dc1" in B.sems and c == 1:
            for _e in ("scalar", "vector", "tensor"):
                wait(_e, B.sems["s_dc1"], B.sems["s_dc1"].v)
        if "s_dc1b" in B.sems and c == 2:
            for _e in ("scalar", "vector", "tensor"):
                wait(_e, B.sems["s_dc1b"], B.sems["s_dc1b"].v)
        wait("sync", s_agdone, 1)
        if c == 0:
            wait("sync", s_agdone, AG_G)
        if c >= 2:
            wait("sync", s_upd, c - 1)
        wait("sync", sxf, sxf.v)
        if "s_dc1" in B.sems:
            wait("sync", B.sems["s_dc1"], B.sems["s_dc1"].v)
        dma("sync", xf[:],
            agout_h.ap()[r:r + 1, 0:D * TCORE]
            .rearrange("o (k pp t) -> (o pp) k t", pp=128, t=TCORE)[:, :, tokp:tokp + 128], sxf)
        sxt = s_xt2[par]
        wait("scalar", sxt, sxt.v)
        if c == 0:
            wait("scalar", s_agdone, 1)
        if c >= 2:
            wait("scalar", s_upd, c - 1)
        dma("scalar", xt[:],
            agout_h.ap()[r:r + 1, D * TCORE:2 * D * TCORE]
            .rearrange("o (t f) -> (o t) f", f=D)[tokp:tokp + 128, :], sxt)
        dma("sync", gt[:],
            agout_g.ap()[c // OWNCH:c // OWNCH + 1, :]
            .rearrange("o (pp f) -> (o pp) f", pp=128)[:, (c % OWNCH) * 256:((c % OWNCH) + 1) * 256], sxf)
        XL = sxf.v
        XTL = sxt.v

        if c > 0:
            wait("scalar", s_upd, c)
        else:
            wait("scalar", s_wi, 1)
        op("scalar", "activation", wbhi[:, 0:512], pb[0][:, 0:512], AF.Copy)
        op("scalar", "activation", wbhi[:, 512:1024], pb[1][:, 0:512], AF.Copy, inc=s_wcast)
        wait("vector", s_wcast, s_wcast.v)
        op("vector", "scalar_tensor_tensor", wblo[:, 0:512], pb[0][:, 0:512], 1.0, wbhi[:, 0:512],
           op0=ALU.mult, op1=ALU.subtract)
        op("vector", "scalar_tensor_tensor", wblo[:, 512:1024], pb[1][:, 0:512], 1.0, wbhi[:, 512:1024],
           op0=ALU.mult, op1=ALU.subtract, inc=s_wlo)

        wait("tensor", sxf, XL)
        wait("tensor", sxt, XTL)
        wait("tensor", s_wlo, s_wlo.v)
        if c >= 2:
            wait("tensor", s_yevac, c - 1)
        for k in range(8):
            op("tensor", "matmul", zb[:, 0:128], xf[:, k * 128:(k + 1) * 128], wbhi[:, k * 128:(k + 1) * 128],
               start=(k == 0), stop=False)
        for k in range(8):
            op("tensor", "matmul", zb[:, 0:128], xf[:, k * 128:(k + 1) * 128], wblo[:, k * 128:(k + 1) * 128],
               start=False, stop=(k == 7), inc=(s_zmm if k == 7 else None))
        wait("vector", s_zmm, s_zmm.v)
        wait("vector", s_vta, s_vta.v)
        op("vector", "tensor_tensor", d_c[:], A[:, c * 128:(c + 1) * 128], zb[:, 0:128], op=ALU.subtract, inc=s_dv)
        wait("tensor", s_dv, s_dv.v)
        op("tensor", "matmul", ub[:, 0:128], gt[:, 0:128], d_c[:], start=True, stop=True, inc=s_umm)
        wait("scalar", s_umm, s_umm.v)
        op("scalar", "copy", u_c[:], ub[:, 0:128], inc=s_uev)
        wait("tensor", s_uev, s_uev.v)
        op("tensor", "matmul", zb[:, 0:128], gt[:, 128:256], u_c[:], start=False, stop=False,
           skip_group_check=True, inc=s_cmm)
        for k in range(8):
            op("tensor", "matmul", WP(k), xt[:, k * 128:(k + 1) * 128], u_c[:],
               start=False, stop=(c == NCH - 1), skip_group_check=True,
               inc=(s_upd if k == 7 else None))
        wait("scalar", s_cmm, s_cmm.v)
        if c >= 2:
            wait("scalar", s_yp2[par], YINC * (c // 2))
        op("scalar", "copy", y_c[:], zb[:, 0:128], inc=s_yevac)
        wait("tensor", s_yevac, c + 1)
        wait("tensor", s_vta, s_vta.v)
        op("tensor", "transpose", pbb[6][:, 0:128], y_c[:], ident_b[:], inc=s_ytr)
        wait("scalar", s_ytr, s_ytr.v)
        op("scalar", "copy", yt_c2[par][:], pbb[6][:, 0:128], inc=s_ytev)
        if dbg_c1 is not None and c == 0:
            s_dc1 = S_("s_dc1")
            wait("gpsimd", s_uev, s_uev.v)
            wait("gpsimd", s_wlo, s_wlo.v)
            o = 0
            dma("gpsimd", dbg_c1[:, 0:1024].bitcast(BF16)[:, 0:1024], xf[:], s_dc1)
            dma("gpsimd", dbg_c1[:, 1024:1536].bitcast(BF16)[:, 0:1024], xt[:], s_dc1)
            dma("gpsimd", dbg_c1[:, 1536:1664].bitcast(BF16)[:, 0:256], gt[:], s_dc1)
            dma("gpsimd", dbg_c1[:, 1664:1728].bitcast(BF16)[:, 0:128], d_c[:], s_dc1)
            dma("gpsimd", dbg_c1[:, 1728:1792].bitcast(BF16)[:, 0:128], u_c[:], s_dc1)
            dma("gpsimd", dbg_c1[:, 1792:2304].bitcast(BF16)[:, 0:1024], wbhi[:], s_dc1)
        if dbg_c1 is not None and c == 1:
            s_dc1b = S_("s_dc1b")
            wait("gpsimd", s_wlo, s_wlo.v)
            dma("gpsimd", dbg_c1[:, 2304:2816].bitcast(BF16)[:, 0:1024], wbhi[:], s_dc1b)
        wait("gpsimd", s_ytev, s_ytev.v)
        wait("gpsimd", s_yp2[par], s_yp2[par].v)
        pY = (c * CH) // PTOK
        dma("gpsimd", agin_y[pY].ap()[:, (c * CH) % PTOK: (c * CH) % PTOK + 128], yt_c2[par][:], s_yp2[par])
        if dbg_y is not None:
            dma("gpsimd", dbg_y[:, c * 128:(c + 1) * 128], yt_c2[par][:], s_yp2[par])

    SP_PER_PIECE = PTOK // VSPAN
    CH_PER_PIECE = PTOK // CH
    for p in range(NPIECE):
        for v in range(SP_PER_PIECE):
            emit_vspan(p * SP_PER_PIECE + v)
        for j in range(CH_PER_PIECE):
            emit_chunk(p * CH_PER_PIECE + j)

    if dbg_v is not None:
        s_dv2 = S_("s_dv2")
        wait("gpsimd", s_vta, s_vta.v)
        for _dc in range(0, S, 2048):
            dma("gpsimd", dbg_v[:, _dc:_dc + 2048], A[:, _dc:_dc + 2048], s_dv2)

    s_agy = S_("s_agy")
    for p in range(NPIECE):
        nch_done = (p + 1) * CH_PER_PIECE
        wait("gpsimd", s_yp2[0], YINC * ((nch_done + 1) // 2))
        wait("gpsimd", s_yp2[1], YINC * (nch_done // 2))
        cc("AllGather", ALU.bypass, [agin_y[p].ap()], [agout_y[p]], s_agy)

    if "yag" in debug:
        dbg_yag = dout("dbg_yag", [128, 8 * TCORE], BF16)
        s_dyag = S_("s_dyag")
        myp = (0 * 1)
        i_core_tok0 = 0
        wait("gpsimd", s_agy, s_agy.v)
        for rr in range(8):
            dma("gpsimd", dbg_yag[:, rr * TCORE:(rr + 1) * TCORE],
                agout_y[0][rr * 128:(rr + 1) * 128, 0:TCORE], s_dyag)

    NSPAN = S // VSPAN
    if phases >= 2:
        PCC = PTOK // TCORE
        pid = nc.partition_id()
        s_xr = [S_("s_xr0"), S_("s_xr1")]
        s_rtr = S_("s_rtr")
        s_rev = [S_("s_rev0"), S_("s_rev1")]
        xrv = {}
        wait("tensor", s_uev, NCH)
        for i in range(NTILE):
            buf = xtile[i % 2]
            sx = s_xr[i % 2]
            wait("sync", sx, sx.v)
            if i >= 2:
                wait("sync", s_rtr, xrv[i - 2])
            dma("sync", buf[:], xin[i * 128:(i + 1) * 128, :], sx)
            for kk in range(4):
                wait("tensor", sx, sx.v)
                wait("tensor", s_rev[0], s_rev[0].v)
                op("tensor", "transpose", pb[5][:, 0:128],
                   buf[:, 2 * kk * 128:(2 * kk + 1) * 128], ident_f[:])
                op("tensor", "transpose", pb[5][:, 128:256],
                   buf[:, (2 * kk + 1) * 128:(2 * kk + 2) * 128], ident_f[:], inc=s_rtr)
                wait("scalar", s_rtr, s_rtr.v)
                if i == 0 and kk == 0:
                    wait("scalar", s_dv, NCH)
                    if dbg_v is not None:
                        wait("scalar", B.sems["s_dv2"], B.sems["s_dv2"].v)
                op("scalar", "copy",
                   A[:, 2 * kk * TCORE + i * 128: 2 * kk * TCORE + (i + 1) * 128],
                   pb[5][:, 0:128])
                op("scalar", "copy",
                   A[:, (2 * kk + 1) * TCORE + i * 128: (2 * kk + 1) * TCORE + (i + 1) * 128],
                   pb[5][:, 128:256], inc=s_rev[0])
            xrv[i] = s_rtr.v

        ybuf = [gpub[:, 0:TCORE], gpub[:, TCORE:2 * TCORE]]
        s_yb = [S_("s_yb0"), S_("s_yb1")]
        s_res = S_("s_res")
        ybv = {}
        for k in range(8 if "nop2b" not in debug else 0):
            j = k % 2
            sy = s_yb[j]
            if k == 0:
                wait("gpsimd", s_agy, NPIECE)
                wait("gpsimd", s_gpubd, s_gpubd.v)
            wait("gpsimd", sy, sy.v)
            if k >= 2:
                wait("gpsimd", s_res, ybv[k - 2])
            dma("gpsimd", ybuf[j],
                agout_yall.ap()[bass.ds((pid // PCC) * NCORES * 128 + k * 128, 128),
                                bass.ds((pid % PCC) * TCORE, TCORE)], sy)
            wait("vector", sy, sy.v)
            if k == 0:
                wait("vector", s_rev[0], s_rev[0].v)
                wait("vector", s_rev[1], s_rev[1].v)
            op("vector", "scalar_tensor_tensor", A[:, k * TCORE:(k + 1) * TCORE],
               ybuf[j], modT[:, 16 + k:17 + k], A[:, k * TCORE:(k + 1) * TCORE],
               op0=ALU.mult, op1=ALU.add, inc=s_res)
            ybv[k] = s_res.v
        if dbg_res is not None:
            s_dres = S_("s_dres")
            wait("gpsimd", s_res, 8)
            for _dc in range(0, 8 * TCORE, 2048):
                dma("gpsimd", dbg_res[:, _dc:_dc + 2048], A[:, _dc:_dc + 2048], s_dres)

    if phases >= 2 and "nop0d" not in debug:
        if S >= 16384:
            wq_in = vxf[0].bitcast(F32)[:, 0:D]
            wq_sg = vxf[0].bitcast(F32)[:, D:D + 64]
            wq_bf = vxf[1][:, 0:D]
            wq_st = vxf[1][:, D:D + 256]
            wqdbg = vxf[1][:, 512:768]
        else:
            wq_in = sb("wq_in", [128, D])
            wq_sg = sb("wq_sg", [128, 64])
            wq_bf = sb("wq_bf", [128, D], BF16)
            wq_st = sb("wq_st", [128, 256], BF16)
            wqdbg = sb("wqdbg", [128, 256], BF16)
        s_wql = [S_("s_wql0"), S_("s_wql1")]
        s_wqv = S_("s_wqv")
        s_wqt = S_("s_wqt")
        s_wqe = [S_("s_wqe0"), S_("s_wqe1")]
        s_wqp = [S_("s_wqp0"), S_("s_wqp1")]
        s_agw = S_("s_agw")
        NG = 32
        wq_tiles = []
        for (gsl, usl, dsl) in ((g0s, u0s, d0s), (g1s, u1s, d1s)):
            tl = []
            for t3 in range(3):
                tl.append((gsl[t3 * 128:(t3 + 1) * 128, :], t3 * D * 128))
            for t3 in range(3):
                tl.append((usl[t3 * 128:(t3 + 1) * 128, :], (3 + t3) * D * 128))
            for s3 in range(3):
                tl.append((dsl[:, s3 * D:(s3 + 1) * D], (6 + s3) * D * 128))
            wq_tiles.append(tl)

        wait("tensor", s_ytev, s_ytev.v)
        wait("tensor", s_vta, s_vta.v)
        g32 = lambda ap: ap.rearrange("p (g k) -> p g k", k=32)
        sgb = lambda c0: wq_sg[:, c0:c0 + NG].rearrange("p (g o) -> p g o", o=1).to_broadcast((128, NG, 32))
        tc = 0
        for lsel in range(2):
            for (srcap, pubofs) in wq_tiles[lsel]:
                par = tc % 2
                svl = s_wql[par]
                wait("sync", svl, svl.v)
                wait("sync", s_wqv, s_wqv.v)
                if tc == 0:
                    wait("sync", s_vmm, NSPAN)
                dma("sync", wq_in[:, :], srcap, svl)
                wait("vector", svl, svl.v)
                fop("vector", "tensor_reduce", wq_sg[:, 0:NG].rearrange("p (g o) -> p g o", o=1),
                    g32(wq_in), axis=AX.X, op=ALU.max, apply_absolute_value=True)
                fop("vector", "tensor_scalar", wq_sg[:, 0:NG], wq_sg[:, 0:NG],
                    float(1.0 / 7.0), 1e-12, op0=ALU.mult, op1=ALU.max)
                fop("vector", "reciprocal", wq_sg[:, NG:2 * NG], wq_sg[:, 0:NG])
                fop("vector", "tensor_tensor", g32(wq_in), g32(wq_in), sgb(NG), op=ALU.mult)
                fop("vector", "tensor_scalar", wq_in[:, :], wq_in[:, :], MAGIC, None, op0=ALU.add)
                fop("vector", "tensor_scalar", wq_in[:, :], wq_in[:, :], MAGIC, None, op0=ALU.subtract)
                wait("vector", s_wqt, s_wqt.v)
                op("vector", "tensor_tensor", g32(wq_bf), g32(wq_in), sgb(0), op=ALU.mult, inc=s_wqv)
                for bb in range(4):
                    wait("tensor", s_wqv, s_wqv.v)
                    wait("tensor", s_wqe[0], s_wqe[0].v)
                    wait("tensor", s_wqe[1], s_wqe[1].v)
                    op("tensor", "transpose", pbb[6][:, 0:128],
                       wq_bf[:, 2 * bb * 128:(2 * bb + 1) * 128], ident_b[:])
                    op("tensor", "transpose", pbb[6][:, 128:256],
                       wq_bf[:, (2 * bb + 1) * 128:(2 * bb + 2) * 128], ident_b[:], inc=s_wqt)
                    wait("scalar", s_wqt, s_wqt.v)
                    wait("scalar", s_wqp[0], s_wqp[0].v)
                    wait("scalar", s_wqp[1], s_wqp[1].v)
                    op("scalar", "copy", wq_st[:, 0:128], pbb[6][:, 0:128], inc=s_wqe[0])
                    op("scalar", "copy", wq_st[:, 128:256], pbb[6][:, 128:256], inc=s_wqe[1])
                    for half in range(2):
                        b = 2 * bb + half
                        wait("gpsimd", s_wqe[half], s_wqe[half].v)
                        wait("gpsimd", s_wqp[half], s_wqp[half].v)
                        dma("gpsimd",
                            agin_wdq[lsel].ap()[0:1, pubofs + b * 16384: pubofs + (b + 1) * 16384]
                            .rearrange("o (r q) -> (o r) q", q=128),
                            wq_st[:, half * 128:half * 128 + 128], s_wqp[half])
                tc += 1
            wait("gpsimd", s_wqp[0], s_wqp[0].v)
            wait("gpsimd", s_wqp[1], s_wqp[1].v)
            cc("AllGather", ALU.bypass, [agin_wdq[lsel].ap()], [agout_wdq[lsel].ap()], s_agw)

        if dbg_wq is not None:
            s_dwq = S_("s_dwq")
            s_wqd = S_("s_wqd")
            wait("sync", s_agw, 2)
            dma("sync", wqdbg[:, 0:128],
                agout_wdq[0].ap()[0:1, 0:16384].rearrange("o (p f) -> (o p) f", f=128), s_wqd)
            dma("sync", wqdbg[:, 128:256],
                agout_wdq[0].ap()[0:1, 6 * D * 128: 6 * D * 128 + 16384]
                .rearrange("o (p f) -> (o p) f", f=128), s_wqd)
            wait("gpsimd", s_wqd, s_wqd.v)
            dma("gpsimd", dbg_wq[:, :], wqdbg[:, :], s_dwq)

    if phases >= 3:
        TB = min(512, TCORE)
        NTB = TCORE // TB
        FBLK = FF // 128
        s_b3 = S_("s_b3")

        def emit_armax(j, srcs, psem):
            s_pub = S_("arpub"); s_cc = S_("arcc"); s_rb = S_("arrb")
            s_pv = S_("arpv"); s_rpm = S_("arrpm")
            wait("gpsimd", psem, psem.v)
            for m, sap in enumerate(srcs):
                wait("gpsimd", s_pub, s_pub.v)
                dma("gpsimd", agin_mx[j].ap()[0:1, m:m + 1], sap, s_pub)
            wait("gpsimd", s_pub, s_pub.v)
            cc("AllGather", ALU.bypass, [agin_mx[j].ap()], [agout_mx[j].ap()], s_cc)
            wait("gpsimd", s_cc, s_cc.v)
            for m in range(len(srcs)):
                wait("gpsimd", s_rb, s_rb.v)
                if m > 0:
                    wait("gpsimd", s_pv, s_pv.v)
                dma("gpsimd", mxr[0:1, 0:NCORES],
                    agout_mx[j].ap()[:, m:m + 1].rearrange("r o -> o r"), s_rb,
                    allow_slow_non_contiguous=True)
                wait("vector", s_rb, s_rb.v)
                wait("vector", s_rpm, s_rpm.v)
                fop("vector", "tensor_reduce", mxr[0:1, 8:9],
                    mxr[0:1, 0:NCORES].rearrange("o (g k) -> o g k", g=1),
                    axis=AX.X, op=ALU.max)
                fop("vector", "tensor_scalar", mxr[0:1, 9:10], mxr[0:1, 8:9],
                    float(1.0 / 32767.0), 1e-12, op0=ALU.mult, op1=ALU.max)
                fop("vector", "reciprocal", mxr[0:1, 10:11], mxr[0:1, 9:10])
                op("vector", "tensor_scalar", mxr[0:1, 11:12], mxr[0:1, 9:10],
                   -MAGIC, None, op0=ALU.mult, inc=s_pv)
                cb0 = 6 * j + 3 * m
                wait("tensor", s_pv, s_pv.v)
                wait("tensor", s_b3, s_b3.v)
                op("tensor", "matmul", pb[3][:, 0:1], ones_f[0:1, :], mxr[0:1, 9:10],
                   start=True, stop=True)
                op("tensor", "matmul", pb[3][:, 1:2], ones_f[0:1, :], mxr[0:1, 10:11],
                   start=True, stop=True)
                op("tensor", "matmul", pb[3][:, 2:3], ones_f[0:1, :], mxr[0:1, 11:12],
                   start=True, stop=True, inc=s_rpm)
                wait("scalar", s_rpm, s_rpm.v)
                op("scalar", "copy", qcols[:, cb0:cb0 + 3], pb[3][:, 0:3], inc=s_b3)
            return s_b3, s_b3.v

        def emit_ptmax(srccol, psem, bank):
            s_t = S_("pnxt"); s_vv = S_("pnxv")
            wait("tensor", psem, psem.v)
            wait("tensor", s_b3, s_b3.v)
            op("tensor", "transpose", bank[0:1, 0:128], mxp[:, srccol:srccol + 1],
               ident_f[:], inc=s_t)
            wait("vector", s_t, s_t.v)
            op("vector", "tensor_reduce", mxr[0:1, 12 + srccol:13 + srccol],
               bank[0:1, 0:128].rearrange("o (g k) -> o g k", g=1),
               axis=AX.X, op=ALU.max, inc=s_vv)
            return s_vv, s_vv.v

        def emit_prenorm(ccol, scol, qdq_j, ready):
            s_c = S_("pnc"); s_q = S_("pnq")
            s_m = S_("pnm")
            s_xv = S_("pnxv"); s_xs = S_("pnxs"); s_xt = S_("pnxt")
            s_v = S_("pnv"); s_a = S_("pna")
            for eng, sem, cnt in ready:
                wait(eng, sem, cnt)
            stfree = {}
            for tb in range(NTB):
                for k in range(8):
                    slot = k % 2
                    n = tb * 8 + k
                    rb = A[:, k * TCORE + tb * TB: k * TCORE + tb * TB + TB]
                    sta = xf_c[0][:, slot * TB: slot * TB + TB]
                    stb = xf_c[1][:, slot * TB: slot * TB + TB]
                    if n >= 2:
                        wait("scalar", s_m, stfree[slot])
                        wait("vector", s_m, stfree[slot])
                    op("scalar", "copy", sta, rb, inc=s_c)
                    op("vector", "tensor_tensor", stb, rb, rb, op=ALU.mult, inc=s_q)
                    wait("tensor", s_c, s_c.v)
                    wait("tensor", s_q, s_q.v)
                    if k == 0 and tb > 0:
                        wait("tensor", s_xv, s_xv.v)
                        wait("tensor", s_xs, s_xs.v)
                    op("tensor", "matmul", pb[7][0:1, 0:TB], ones_cb[:, 0:1], sta,
                       start=(k == 0), stop=(k == 7), inc=s_m)
                    op("tensor", "matmul", pb[7][32:33, 0:TB], ones_cb[:, 0:1], stb,
                       start=(k == 0), stop=(k == 7), inc=s_m)
                    stfree[slot] = s_m.v
                r0 = pb[7][0:1, 0:TB]; r32 = pb[7][32:33, 0:TB]
                mu = rows2[0:1, 0:TB]; e2 = rows2[0:1, 512:512 + TB]
                wait("scalar", s_m, s_m.v)
                op("scalar", "copy", mu, r0)
                op("scalar", "copy", e2, r32, inc=s_xs)
                wait("vector", s_xs, s_xs.v)
                fop("vector", "tensor_scalar", mu, mu, float(1.0 / D), None, op0=ALU.mult)
                fop("vector", "tensor_scalar", e2, e2, float(1.0 / D), None, op0=ALU.mult)
                fop("vector", "tensor_tensor", rowsflat[0:1, 0:TB], mu, mu, op=ALU.mult)
                fop("vector", "tensor_tensor", e2, e2, rowsflat[0:1, 0:TB], op=ALU.subtract)
                op("vector", "tensor_scalar", e2, e2, 1e-5, None, op0=ALU.add, inc=s_xv)
                wait("scalar", s_xv, s_xv.v)
                op("scalar", "activation", rowsflat[0:1, 0:TB], e2, AF.Sqrt, inc=s_xs)
                wait("vector", s_xs, s_xs.v)
                fop("vector", "reciprocal", e2, rowsflat[0:1, 0:TB])
                fop("vector", "tensor_tensor", rowsflat[0:1, 512:512 + TB], mu, e2, op=ALU.mult)
                fop("vector", "tensor_scalar", rowsflat[0:1, 512:512 + TB],
                    rowsflat[0:1, 512:512 + TB], -1.0, None, op0=ALU.mult)
                op("vector", "tensor_copy", rowsflat[0:1, 0:TB], e2, inc=s_xv)
                wait("tensor", s_xv, s_xv.v)
                if tb > 0:
                    wait("tensor", s_v, s_v.v)
                op("tensor", "matmul", pb[4][:, 0:TB], ones_f[0:1, :],
                   rowsflat[0:1, 0:TB], start=True, stop=True)
                op("tensor", "matmul", pb[5][:, 0:TB], ones_f[0:1, :],
                   rowsflat[0:1, 512:512 + TB], start=True, stop=True, inc=s_xt)
                t2free = {}
                for k in range(8):
                    slot = k % 2
                    n = tb * 8 + k
                    rb = A[:, k * TCORE + tb * TB: k * TCORE + tb * TB + TB]
                    t2 = xhat[:, slot * TB: slot * TB + TB] if TB == 512 else \
                         xhat[:, slot * TB: slot * TB + TB]
                    wait("vector", s_xt, s_xt.v)
                    if k >= 2:
                        wait("vector", s_a, t2free[slot])
                    fop("vector", "tensor_tensor", t2, rb, pb[4][:, 0:TB], op=ALU.mult)
                    op("vector", "tensor_tensor", t2, t2, pb[5][:, 0:TB], op=ALU.add, inc=s_v)
                    if qdq_j is not None:
                        wait("vector", s_v, s_v.v)
                        if n == 0:
                            fop("vector", "tensor_reduce", mxp[:, 0:1], t2, axis=AX.X,
                                op=ALU.max, apply_absolute_value=True)
                        else:
                            fop("vector", "tensor_reduce", mxp[:, 1:2], t2, axis=AX.X,
                                op=ALU.max, apply_absolute_value=True)
                            fop("vector", "tensor_tensor", mxp[:, 0:1], mxp[:, 0:1],
                                mxp[:, 1:2], op=ALU.max)
                    wait("scalar", s_v, s_v.v)
                    op("scalar", "activation",
                       hfm[:, k * TCORE + tb * TB: k * TCORE + tb * TB + TB],
                       t2, AF.Identity, scale=modT[:, ccol + k:ccol + k + 1],
                       bias=modT[:, scol + k:scol + k + 1], inc=s_a)
                    t2free[slot] = s_a.v
            if qdq_j is None:
                return s_a, s_a.v
            s_mxf = S_("pnxv")
            op("vector", "tensor_copy", mxp[:, 1:2], mxp[:, 0:1], inc=s_mxf)
            psem, _ = emit_ptmax(0, s_mxf, pb[3])
            qsem, qcnt = emit_armax(qdq_j, [mxr[0:1, 12:13]], psem)
            cb0 = 6 * qdq_j
            s_q1 = S_("pnq1"); s_q2 = S_("pnq2")
            hafinal = s_a.v
            qfree = {}
            for tb in range(NTB):
                for k in range(8):
                    slot = (tb * 8 + k) % 2
                    n = tb * 8 + k
                    hb = hfm[:, k * TCORE + tb * TB: k * TCORE + tb * TB + TB]
                    t2 = xhat[:, slot * TB: slot * TB + TB]
                    if n == 0:
                        wait("scalar", qsem, qcnt)
                        wait("scalar", s_a, hafinal)
                    if n >= 2:
                        wait("scalar", s_q2, qfree[slot])
                    op("scalar", "activation", t2, hb, AF.Identity,
                       scale=qcols[:, cb0 + 1:cb0 + 2], bias=np.float32(MAGIC), inc=s_q1)
                    wait("vector", s_q1, s_q1.v)
                    op("vector", "tensor_scalar", hb, t2, MAGIC, qcols[:, cb0:cb0 + 1],
                       op0=ALU.subtract, op1=ALU.mult, inc=s_q2)
                    qfree[slot] = s_q2.v
            return s_q2, s_q2.v

        def emit_ffn(lsel, gbcol, jug, jp2, jp3, s_hq, hqcnt):
            NSB = NTB
            wstage = [carep.bitcast(BF16), sarep.bitcast(BF16)]
            ustg = [wbhi[:, 0:TB], wbhi[:, 512:512 + TB] if TB == 512 else wbhi[:, TB:2 * TB]]
            gstg = [wblo[:, 0:TB], wblo[:, 512:512 + TB] if TB == 512 else wblo[:, TB:2 * TB]]
            s_wl = [S_("fwl0"), S_("fwl1")]
            s_mm = S_("fmm"); s_ev = S_("fev"); s_mx = S_("fmx")
            s_sp = [S_("fsp0"), S_("fsp1")]
            s_sp2 = [S_("ssp0"), S_("ssp1")]
            mmv = {}
            evfree = {}
            wsfree = {}
            spfree = [(s_sp[0].v, s_sp2[0].v), (s_sp[1].v, s_sp2[1].v)]
            MM0 = s_mm.v
            for fb in range(FBLK):
                rank, t3 = fb // 3, fb % 3
                par = fb % 2
                ws = wstage[par]; sw = s_wl[par]
                wait("sync", sw, sw.v)
                if fb == 0:
                    wait("sync", s_agw, lsel + 1)
                    wait("sync", s_mm, MM0)
                if fb >= 2:
                    wait("sync", s_mm, wsfree[fb - 2])
                for q in range(8):
                    dma("sync", ws[:, q * 128:(q + 1) * 128],
                        agout_wdq[lsel].ap()[rank:rank + 1,
                            t3 * D * 128 + q * 16384: t3 * D * 128 + (q + 1) * 16384]
                        .rearrange("o (p f) -> (o p) f", f=128), sw)
                for q in range(8):
                    dma("sync", ws[:, 1024 + q * 128:1024 + (q + 1) * 128],
                        agout_wdq[lsel].ap()[rank:rank + 1,
                            (3 + t3) * D * 128 + q * 16384: (3 + t3) * D * 128 + (q + 1) * 16384]
                        .rearrange("o (p f) -> (o p) f", f=128), sw)
                wait("tensor", sw, sw.v)
                if fb == 0:
                    wait("tensor", s_hq, hqcnt)
                    wait("tensor", s_wqe[0], s_wqe[0].v)
                    wait("tensor", s_wqe[1], s_wqe[1].v)
                for tb in range(NSB):
                    if fb > 0:
                        uv, gv = evfree[(fb - 1, tb)]
                        wait("tensor", s_ev, uv)
                    for q in range(8):
                        op("tensor", "matmul", pb[tb][:, 0:TB],
                           ws[:, 1024 + q * 128:1024 + (q + 1) * 128],
                           hfm[:, q * TCORE + tb * TB: q * TCORE + tb * TB + TB],
                           start=(q == 0), stop=(q == 7))
                    if fb > 0:
                        wait("tensor", s_ev, evfree[(fb - 1, tb)][1])
                    for q in range(8):
                        op("tensor", "matmul", pb[4 + tb][:, 0:TB],
                           ws[:, q * 128:(q + 1) * 128],
                           hfm[:, q * TCORE + tb * TB: q * TCORE + tb * TB + TB],
                           start=(q == 0), stop=(q == 7), inc=(s_mm if q == 7 else None))
                    mmv[(fb, tb)] = s_mm.v
                wsfree[fb] = s_mm.v
                for tb in range(NSB):
                    sl = (fb * NSB + tb) % 2
                    first = (fb == 0 and tb == 0)
                    wait("vector", s_mm, mmv[(fb, tb)])
                    wait("vector", s_sp[sl], spfree[sl][0])
                    wait("vector", s_sp2[sl], spfree[sl][1])
                    op("vector", "tensor_scalar", ustg[sl], pb[tb][:, 0:TB], 1.0, None,
                       op0=ALU.mult, inc=s_ev)
                    u_ev = s_ev.v
                    op("vector", "tensor_reduce", mxp[:, 1:2], pb[tb][:, 0:TB],
                       axis=AX.X, op=ALU.max, apply_absolute_value=True, inc=s_ev)
                    uv = s_ev.v
                    wait("vector", s_ev, s_ev.v)
                    if first:
                        fop("vector", "tensor_copy", mxp[:, 2:3], mxp[:, 1:2])
                    else:
                        fop("vector", "tensor_tensor", mxp[:, 2:3], mxp[:, 2:3],
                            mxp[:, 1:2], op=ALU.max)
                    op("vector", "tensor_scalar", gstg[sl], pb[4 + tb][:, 0:TB], 1.0, None,
                       op0=ALU.mult, inc=s_ev)
                    g_ev = s_ev.v
                    op("vector", "tensor_reduce", mxp[:, 1:2], pb[4 + tb][:, 0:TB],
                       axis=AX.X, op=ALU.max, apply_absolute_value=True, inc=s_ev)
                    gv = s_ev.v
                    wait("vector", s_ev, s_ev.v)
                    if first:
                        fop("vector", "tensor_copy", mxp[:, 3:4], mxp[:, 1:2])
                    else:
                        fop("vector", "tensor_tensor", mxp[:, 3:4], mxp[:, 3:4],
                            mxp[:, 1:2], op=ALU.max)
                    evfree[(fb, tb)] = (uv, gv)
                    wait("gpsimd", s_ev, u_ev)
                    wait("gpsimd", s_sp[sl], s_sp[sl].v)
                    dma("gpsimd", dscr2.ap()[fb * 128:(fb + 1) * 128, tb * TB:tb * TB + TB],
                        ustg[sl], s_sp[sl])
                    wait("scalar", s_ev, g_ev)
                    wait("scalar", s_sp2[sl], s_sp2[sl].v)
                    dma("scalar", dscr1.ap()[fb * 128:(fb + 1) * 128, tb * TB:tb * TB + TB],
                        gstg[sl], s_sp2[sl])
                    spfree[sl] = (s_sp[sl].v, s_sp2[sl].v)
            ALLSP = [s_sp[0].v, s_sp[1].v]
            ALLMM = s_mm.v
            s_mxf = S_("fmx")
            op("vector", "tensor_copy", mxp[:, 1:2], mxp[:, 2:3], inc=s_mxf)
            pA, _ = emit_ptmax(2, s_mxf, pb[3])
            op("vector", "tensor_copy", mxp[:, 1:2], mxp[:, 3:4], inc=s_mxf)
            pB, _ = emit_ptmax(3, s_mxf, pb[2])
            qsem, qcnt = emit_armax(jug, [mxr[0:1, 14:15], mxr[0:1, 15:16]], pB)
            cbu, cbg = 6 * jug, 6 * jug + 3

            gbufs = [w0t_sb.bitcast(BF16)[:, 0:TCORE], gpub[:, 0:TCORE]]
            tbufs = [xt_c[0].bitcast(F32)[:, 0:TB], xt_c[1].bitcast(F32)[:, 0:TB]]
            sgbufs = [xhat[:, 0:TB], xhat[:, 512:512 + TB] if TB == 512 else xhat[:, TB:2 * TB]]
            w32 = wkvvt_sb.bitcast(F32)
            gqbufs = ([vstage[:, 0:TB], w32[:, 0:TB]] if VSPAN >= TB
                      else [w32[:, 0:TB], w32[:, TB:2 * TB]])
            p2stg = [xf_c[0][:, 0:TB], xf_c[0][:, 512:512 + TB] if TB == 512 else xf_c[0][:, TB:2 * TB]]
            s_rl = [S_("frl0"), S_("frl1")]
            s_ts = S_("fts"); s_vq = S_("fvq"); s_pv = S_("fpv")
            s_pw = [S_("fpw0"), S_("fpw1")]
            tbfree = {}; sgfree = {}; gqfree = {}; p2free = {}; gbfree = {}
            pwfree = [s_pw[0].v, s_pw[1].v]
            for fb in range(FBLK):
                par = fb % 2
                gb = gbufs[par]; sg = s_rl[par]
                wait("sync", sg, sg.v)
                if fb == 0:
                    wait("sync", s_sp2[0], s_sp2[0].v)
                    wait("sync", s_sp2[1], s_sp2[1].v)
                    wait("sync", s_res, 8)
                if fb >= 2:
                    wait("sync", s_ts, gbfree[par])
                dma("sync", gb, dscr1.ap()[fb * 128:(fb + 1) * 128, :], sg)
                for sbk in range(NSB):
                    sl = (fb * NSB + sbk) % 2
                    n = fb * NSB + sbk
                    tbf = tbufs[sl]; sgb = sgbufs[sl]; gq = gqbufs[sl]; p2s = p2stg[sl]
                    gblk = gb[:, sbk * TB: sbk * TB + TB]
                    wait("scalar", sg, sg.v)
                    if n == 0:
                        wait("scalar", qsem, qcnt)
                    if n >= 2:
                        wait("scalar", s_vq, tbfree[sl])
                    op("scalar", "activation", tbf, gblk, AF.Identity,
                       scale=qcols[:, cbg + 1:cbg + 2], bias=np.float32(MAGIC), inc=s_ts)
                    wait("scalar", s_ts, s_ts.v)
                    if n >= 2:
                        wait("scalar", s_pv, sgfree[sl])
                    op("scalar", "activation", sgb, tbf, AF.Sigmoid,
                       scale=qcols[:, cbg:cbg + 1], bias=qcols[:, cbg + 2:cbg + 3], inc=s_ts)
                    wait("vector", s_ts, s_ts.v)
                    if n >= 2:
                        wait("vector", s_pv, gqfree[sl])
                    op("vector", "tensor_scalar", gq, tbf, MAGIC, qcols[:, cbg:cbg + 1],
                       op0=ALU.subtract, op1=ALU.mult, inc=s_vq)
                    tbfree[sl] = s_vq.v
                    fop("vector", "tensor_scalar", sgb, sgb, 192.0,
                        float(192.0 + 65535.0 / 65536.0), op0=ALU.add, op1=ALU.min)
                    wait("vector", s_pw[sl], pwfree[sl])
                    wait("vector", s_vq, s_vq.v)
                    op("vector", "scalar_tensor_tensor", p2s, sgb, 192.0, gq,
                       op0=ALU.subtract, op1=ALU.mult, inc=s_pv)
                    sgfree[sl] = s_pv.v
                    gqfree[sl] = s_pv.v
                    wait("vector", s_pv, s_pv.v)
                    op("vector", "tensor_reduce", mxp[:, 1:2], p2s, axis=AX.X,
                       op=ALU.max, apply_absolute_value=True, inc=s_pv)
                    wait("vector", s_pv, s_pv.v)
                    if n == 0:
                        fop("vector", "tensor_copy", mxp[:, 4:5], mxp[:, 1:2])
                    else:
                        fop("vector", "tensor_tensor", mxp[:, 4:5], mxp[:, 4:5],
                            mxp[:, 1:2], op=ALU.max)
                    wait("gpsimd", s_pv, s_pv.v)
                    wait("gpsimd", s_pw[sl], s_pw[sl].v)
                    dma("gpsimd", dscr1.ap()[fb * 128:(fb + 1) * 128, sbk * TB:sbk * TB + TB],
                        p2s, s_pw[sl])
                    pwfree[sl] = s_pw[sl].v
                gbfree[par] = s_ts.v
            P2W = [s_pw[0].v, s_pw[1].v]
            s_mxf2 = S_("fmx")
            op("vector", "tensor_copy", mxp[:, 1:2], mxp[:, 4:5], inc=s_mxf2)
            pC, _ = emit_ptmax(4, s_mxf2, pb[3])
            qsem2, qcnt2 = emit_armax(jp2, [mxr[0:1, 16:17]], pC)
            cbp = 6 * jp2

            upbufs = [carep.bitcast(BF16)[:, 0:TCORE], sarep.bitcast(BF16)[:, 0:TCORE]]
            tbufs2 = [wbhi.bitcast(F32)[:, 0:TB], wblo.bitcast(F32)[:, 0:TB]]
            tb2free = {}; p3free = {}; uqfree = {}; g2free = {}; rlfree = {}
            pw3free = [s_pw[0].v, s_pw[1].v]
            s_ur = [S_("vrl0"), S_("vrl1")]
            for fb in range(FBLK):
                par = fb % 2
                p2b = gbufs[par]; upb = upbufs[par]
                sp = s_rl[par]
                su = s_ur[par]
                wait("sync", sp, sp.v)
                if fb == 0:
                    wait("sync", s_pw[0], P2W[0])
                    wait("sync", s_pw[1], P2W[1])
                if fb >= 2:
                    wait("sync", s_ts, rlfree[par])
                dma("sync", p2b, dscr1.ap()[fb * 128:(fb + 1) * 128, :], sp)
                wait("scalar", su, su.v)
                if fb == 0:
                    wait("scalar", s_sp[0], ALLSP[0])
                    wait("scalar", s_sp[1], ALLSP[1])
                    wait("scalar", s_mm, ALLMM)
                if fb >= 2:
                    wait("scalar", s_ts, rlfree[par])
                dma("scalar", upb, dscr2.ap()[fb * 128:(fb + 1) * 128, :], su)
                for sbk in range(NSB):
                    sl = (fb * NSB + sbk) % 2
                    n = fb * NSB + sbk
                    tbf = tbufs[sl]; tbf2 = tbufs2[sl]
                    g2 = gqbufs[sl]; uq = sgbufs[sl]; p3s = p2stg[sl]
                    wait("scalar", sp, sp.v)
                    wait("scalar", su, su.v)
                    if n == 0:
                        wait("scalar", qsem2, qcnt2)
                        wait("scalar", s_vq, s_vq.v)
                        wait("scalar", s_pv, s_pv.v)
                    if n >= 2:
                        wait("scalar", s_vq, tb2free[sl])
                    op("scalar", "activation", tbf, p2b[:, sbk * TB: sbk * TB + TB],
                       AF.Identity, scale=qcols[:, cbp + 1:cbp + 2],
                       bias=np.float32(MAGIC), inc=s_ts)
                    op("scalar", "activation", tbf2, upb[:, sbk * TB: sbk * TB + TB],
                       AF.Identity, scale=qcols[:, cbu + 1:cbu + 2],
                       bias=np.float32(MAGIC), inc=s_ts)
                    wait("vector", s_ts, s_ts.v)
                    if n >= 2:
                        wait("vector", s_pv, g2free[sl])
                    op("vector", "tensor_scalar", g2, tbf, MAGIC, qcols[:, cbp:cbp + 1],
                       op0=ALU.subtract, op1=ALU.mult, inc=s_vq)
                    op("vector", "tensor_scalar", uq, tbf2, MAGIC, qcols[:, cbu:cbu + 1],
                       op0=ALU.subtract, op1=ALU.mult, inc=s_vq)
                    tb2free[sl] = s_vq.v
                    wait("vector", s_pw[sl], pw3free[sl])
                    wait("vector", s_vq, s_vq.v)
                    op("vector", "tensor_tensor", p3s, g2, uq, op=ALU.mult, inc=s_pv)
                    g2free[sl] = s_pv.v
                    wait("vector", s_pv, s_pv.v)
                    op("vector", "tensor_reduce", mxp[:, 1:2], p3s, axis=AX.X,
                       op=ALU.max, apply_absolute_value=True, inc=s_pv)
                    wait("vector", s_pv, s_pv.v)
                    if n == 0:
                        fop("vector", "tensor_copy", mxp[:, 5:6], mxp[:, 1:2])
                    else:
                        fop("vector", "tensor_tensor", mxp[:, 5:6], mxp[:, 5:6],
                            mxp[:, 1:2], op=ALU.max)
                    wait("gpsimd", s_pv, s_pv.v)
                    wait("gpsimd", s_pw[sl], s_pw[sl].v)
                    dma("gpsimd", dscr2.ap()[fb * 128:(fb + 1) * 128, sbk * TB:sbk * TB + TB],
                        p3s, s_pw[sl])
                    pw3free[sl] = s_pw[sl].v
                rlfree[par] = s_ts.v
            P3W = [s_pw[0].v, s_pw[1].v]
            s_mxf3 = S_("fmx")
            op("vector", "tensor_copy", mxp[:, 1:2], mxp[:, 5:6], inc=s_mxf3)
            pD, _ = emit_ptmax(5, s_mxf3, pb[3])
            qsem3, qcnt3 = emit_armax(jp3, [mxr[0:1, 17:18]], pD)
            cbo = 6 * jp3

            wdstage = [xf_c[0][:, 0:1024], xf_c[1][:, 0:1024]]
            p3rl = [wblo[:, 0:TB], wblo[:, 512:512 + TB] if TB == 512 else wblo[:, TB:2 * TB]]
            obufs = [wbhi[:, 0:TB], wbhi[:, 512:512 + TB] if TB == 512 else wbhi[:, TB:2 * TB]]
            s_dw = [S_("fwl0"), S_("fwl1")]
            s_or = [S_("frl0"), S_("frl1")]
            s_vr = [S_("vrl0"), S_("vrl1")]
            s_dmm = S_("fdmm"); s_fres = S_("ffres")
            dmmv = {}; tqfree = {}; obfree = {}; fresv = {}
            EVFULL = s_ev.v
            VQFULL = s_vq.v
            PVFULL = s_pv.v
            TSFULL = s_ts.v
            for tb in range(NSB):
                for fb in range(FBLK):
                    par = fb % 2
                    n = tb * FBLK + fb
                    wd = wdstage[par]; p3b = p3rl[par]; ob = obufs[par]; tbf = tbufs[par]
                    sw = s_dw[par]; so = s_or[par]
                    wait("sync", sw, sw.v)
                    if n == 0:
                        wait("sync", s_pv, PVFULL)
                    if n >= 2:
                        wait("sync", s_dmm, dmmv[n - 2])
                    for j in range(8):
                        dma("sync", wd[:, j * 128:(j + 1) * 128],
                            agout_wdq[lsel].ap()[j:j + 1,
                                6 * D * 128 + fb * 16384: 6 * D * 128 + (fb + 1) * 16384]
                            .rearrange("o (p f) -> (o p) f", f=128), sw)
                    sv = s_vr[par]
                    wait("scalar", sv, sv.v)
                    if n == 0:
                        wait("scalar", s_pw[0], P3W[0])
                        wait("scalar", s_pw[1], P3W[1])
                        wait("scalar", s_vq, VQFULL)
                        wait("scalar", s_ts, TSFULL)
                    if n >= 2:
                        wait("scalar", s_dmm, dmmv[n - 2])
                        wait("scalar", s_vq, tqfree[par])
                    dma("scalar", p3b,
                        dscr2.ap()[fb * 128:(fb + 1) * 128, tb * TB:tb * TB + TB], sv)
                    wait("scalar", sv, sv.v)
                    if n == 0:
                        wait("scalar", qsem3, qcnt3)
                    if n >= 2:
                        wait("scalar", s_vq, tqfree[par])
                    op("scalar", "activation", tbf, p3b, AF.Identity,
                       scale=qcols[:, cbo + 1:cbo + 2], bias=np.float32(MAGIC), inc=s_ts)
                    wait("vector", s_ts, s_ts.v)
                    if n >= 2:
                        wait("vector", s_dmm, obfree[par])
                    op("vector", "tensor_scalar", ob, tbf, MAGIC, qcols[:, cbo:cbo + 1],
                       op0=ALU.subtract, op1=ALU.mult, inc=s_vq)
                    tqfree[par] = s_vq.v
                    wait("tensor", sw, sw.v)
                    wait("tensor", s_vq, s_vq.v)
                    if fb == 0:
                        if tb == 0:
                            wait("tensor", s_ev, EVFULL)
                        else:
                            wait("tensor", s_fres, fresv[tb - 1])
                    for j in range(8):
                        op("tensor", "matmul", pb[j][:, 0:TB], wd[:, j * 128:(j + 1) * 128],
                           ob, start=(fb == 0), stop=(fb == FBLK - 1),
                           inc=(s_dmm if j == 7 else None))
                    dmmv[n] = s_dmm.v
                    obfree[par] = s_dmm.v
                wait("vector", s_dmm, s_dmm.v)
                for j in range(8):
                    op("vector", "scalar_tensor_tensor",
                       A[:, j * TCORE + tb * TB: j * TCORE + tb * TB + TB],
                       pb[j][:, 0:TB], modT[:, gbcol + j:gbcol + j + 1],
                       A[:, j * TCORE + tb * TB: j * TCORE + tb * TB + TB],
                       op0=ALU.mult, op1=ALU.add, inc=s_fres)
                fresv[tb] = s_fres.v
            return s_fres, s_fres.v

        pn_ready = [("scalar", s_res, 8), ("vector", s_res, 8),
                    ("tensor", s_vev, NSPAN),
                    ("tensor", s_uev, NCH),
                    ("tensor", s_rev[0], s_rev[0].v), ("tensor", s_rev[1], s_rev[1].v),
                    ("scalar", s_hp[0], 16 * (NTILE + 1))]
        if dbg_hfm is not None:
            pn_ready.append(("scalar", B.sems["s_dh"], B.sems["s_dh"].v))
        s_hq0, hq0 = emit_prenorm(32, 24, 0, pn_ready)
        if dbg_h2 is not None:
            s_dbg = S_("s_dbg")
            wait("gpsimd", s_hq0, hq0)
            for _dc in range(0, 8 * TCORE, 2048):
                dma("gpsimd", dbg_h2[:, _dc:_dc + 2048], hfm[:, _dc:_dc + 2048], s_dbg)
        if "s_dbg" in B.sems:
            wait("vector", B.sems["s_dbg"], B.sems["s_dbg"].v)
        s_f0, f0cnt = emit_ffn(0, 40, 1, 2, 3, s_hq0, hq0)
        if dbg_x2 is not None:
            s_dbg = S_("s_dbg")
            wait("gpsimd", s_f0, f0cnt)
            for _dc in range(0, 8 * TCORE, 2048):
                dma("gpsimd", dbg_x2[:, _dc:_dc + 2048], A[:, _dc:_dc + 2048], s_dbg)

    if phases >= 4:
        NKB = TCORE // 128
        HED = 16
        s_at = S_("hat"); s_av = S_("hav"); s_as = S_("has")
        s_ag2 = S_("hag"); s_hl = S_("hlo")
        if "s_dbg" in B.sems:
            wait("vector", B.sems["s_dbg"], B.sems["s_dbg"].v)
        s_qw = [S_("gsp0"), S_("gsp1")]

        pn_ready1 = [("scalar", s_f0, f0cnt), ("vector", s_f0, f0cnt),
                     ("tensor", s_f0, f0cnt),
                     ("scalar", B.sems["fmm"], B.sems["fmm"].v),
                     ("vector", B.sems["fmm"], B.sems["fmm"].v)]
        s_h3, h3cnt = emit_prenorm(56, 48, None, pn_ready1)

        h1b = gpub[0:64, 0:TCORE]
        vbuf = gpub[0:64, 0:TCORE]
        kbuf = tt1.bitcast(BF16)[0:64, 0:TCORE]
        sqb64 = tt1.bitcast(BF16)[0:64, 0:TCORE]
        pbuf = vxf[0]
        kaug = vxf[1][0:65, 0:TCORE]
        qaug = vxf[1][0:65, 4 * VSPAN:4 * VSPAN + TCORE]
        vtokaug = w0t_sb.bitcast(BF16)[:, 0:NKB * 65]
        lamr0 = xhat.bitcast(BF16)[0:1, 0:TCORE]

        wait("sync", s_hl, s_hl.v)
        wait("sync", s_f0, f0cnt)
        wait("sync", B.sems["pnxt"], B.sems["pnxt"].v)
        dma("sync", rows2[0:1, 0:64], wl1t[:, :], s_hl)
        dma("sync", rowsflat[0:1, 0:TCORE], pos[:, :], s_hl)
        wait("scalar", s_hl, s_hl.v)
        wait("scalar", s_f0, f0cnt)
        wait("scalar", B.sems["pna"], B.sems["pna"].v)
        wait("scalar", B.sems["pnq"], B.sems["pnq"].v)
        fop("scalar", "copy", lamr0[0:1, 0:64], rows2[0:1, 0:64])
        fop("scalar", "copy", rows2[0:1, 64:128], lamr0[0:1, 0:64])
        op("scalar", "activation", rowsflat[0:1, 0:TCORE], rowsflat[0:1, 0:TCORE],
           AF.Ln, inc=s_as)
        for qb in range(NTB):
            wait("tensor", s_as, s_as.v)
            if qb == 0:
                wait("tensor", s_f0, f0cnt)
                wait("tensor", B.sems["pnxv"], B.sems["pnxv"].v)
                wait("tensor", B.sems["pnxs"], B.sems["pnxs"].v)
                wait("tensor", B.sems["pnv"], B.sems["pnv"].v)
                wait("tensor", s_b3, s_b3.v)
                wait("tensor", B.sems["s_gld"], B.sems["s_gld"].v)
            op("tensor", "matmul", pb[6][0:64, 0:TB], rows2[0:1, 64:128],
               rowsflat[0:1, qb * TB:qb * TB + TB], start=True, stop=True, inc=s_at)
            wait("scalar", s_at, s_at.v)
            op("scalar", "activation", h1b[:, qb * TB:qb * TB + TB], pb[6][0:64, 0:TB],
               AF.Relu, scale=1.0, bias=bl1c_sb[:, 0:1], inc=s_as)
            wait("tensor", s_as, s_as.v)
            if qb == 0:
                wait("tensor", S_("hag"), 16)
            op("tensor", "matmul", pb[7][0:16, 0:TB], wl2b[:, :],
               h1b[:, qb * TB:qb * TB + TB], start=True, stop=True, inc=s_at)
            wait("scalar", s_at, s_at.v)
            op("scalar", "activation", xhat[0:16, 512:512 + TB] if TB == 512 else
               xhat[0:16, TB:2 * TB], pb[7][0:16, 0:TB],
               AF.Sigmoid, scale=1.0, bias=bl2c_sb[:, 0:1], inc=s_as)
            wait("vector", s_as, s_as.v)
            op("vector", "tensor_scalar", lamsc[:, qb * TB:qb * TB + TB],
               xhat[0:16, 512:512 + TB] if TB == 512 else xhat[0:16, TB:2 * TB],
               float(64 ** -0.5), None, op0=ALU.mult, inc=s_av)
        LAMDONE = s_av.v

        qw = [carep.bitcast(BF16)[:, 0:1024], sarep.bitcast(BF16)[:, 0:1024]]
        qstg = [wbhi[:, 0:TB], wbhi[:, 512:512 + TB] if TB == 512 else wbhi[:, TB:2 * TB]]
        s_qm = S_("fmm"); s_qe = S_("pnv"); s_qs = [S_("fsp0"), S_("fsp1")]
        qmv = {}
        qevv = {}
        for fb3 in range(24 if "noqkv" not in debug else 0):
            par = fb3 % 2
            w = qw[par]; sw = s_qw[par]
            wait("gpsimd", sw, sw.v)
            if fb3 >= 2:
                wait("gpsimd", s_qm, qmv[fb3 - 2])
            for q in range(8):
                dma("gpsimd", w[:, q * 128:(q + 1) * 128],
                    wqkvt[q * 128:(q + 1) * 128, fb3 * 128:(fb3 + 1) * 128], sw)
            wait("tensor", sw, sw.v)
            if fb3 == 0:
                wait("tensor", s_h3, h3cnt)
            for tb in range(NTB):
                if fb3 > 0:
                    wait("tensor", s_qe, qevv.get((fb3 - 1, tb), 0))
                for q in range(8):
                    op("tensor", "matmul", pb[tb][:, 0:TB], w[:, q * 128:(q + 1) * 128],
                       hfm[:, q * TCORE + tb * TB: q * TCORE + tb * TB + TB],
                       start=(q == 0), stop=(q == 7), inc=(s_qm if q == 7 else None))
            qmv[fb3] = s_qm.v
            for tb in range(NTB):
                sl = (fb3 * NTB + tb) % 2
                wait("vector", s_qm, qmv[fb3])
                wait("vector", s_qs[sl], s_qs[sl].v)
                op("vector", "tensor_scalar", qstg[sl], pb[tb][:, 0:TB], 1.0, None,
                   op0=ALU.mult, inc=s_qe)
                qevv[(fb3, tb)] = s_qe.v
                wait("gpsimd", s_qe, s_qe.v)
                wait("gpsimd", s_qs[sl], s_qs[sl].v)
                dma("gpsimd", dscr1.ap()[fb3 * 128:(fb3 + 1) * 128, tb * TB:tb * TB + TB],
                    qstg[sl], s_qs[sl])
        QKVMM = s_qm.v

        klv = {}
        pvrec = {}
        if "noheads" in debug:
            HED = 0
        for h in range(HED):
            wait("sync", s_hl, s_hl.v)
            if h == 0:
                wait("sync", s_qs[0], s_qs[0].v)
                wait("sync", s_qs[1], s_qs[1].v)
                wait("sync", B.sems["fts"], B.sems["fts"].v)
                wait("sync", s_av, LAMDONE)
            else:
                wait("sync", s_av, s_av.v)
                wait("sync", s_at, s_at.v)
            dma("sync", kbuf[0:64, :], dscr1.ap()[1024 + 64 * h:1024 + 64 * (h + 1), :], s_hl)
            dma("sync", qaug[0:64, :], dscr1.ap()[64 * h:64 * (h + 1), :], s_hl)
            dma("sync", vbuf[0:64, :], dscr1.ap()[2048 + 64 * h:2048 + 64 * (h + 1), :], s_hl)
            wait("gpsimd", s_ag2, s_ag2.v)
            wait("gpsimd", s_av, LAMDONE if h == 0 else s_av.v)
            if h > 0:
                wait("gpsimd", s_at, s_at.v)
            dma("gpsimd", lamr0[0:1, 0:TCORE], lamsc[h:h + 1, :], s_ag2)
            if h == 0:
                wait("scalar", s_ag2, s_ag2.v)
                op("scalar", "activation", kaug[64:65, :], lamr0[0:1, 0:TCORE],
                   AF.Identity, scale=np.float32(0.0), bias=np.float32(1.0), inc=s_as)
            for tb in range(NTB):
                wait("tensor", s_ag2, s_ag2.v)
                if h > 0 or tb > 0:
                    wait("tensor", s_av, klv.get("b6", 0))
                op("tensor", "matmul", pb[6][0:64, 0:TB], ones_rb[0:1, 0:64],
                   lamr0[0:1, tb * TB:tb * TB + TB], start=True, stop=True, inc=s_at)
                wait("vector", s_at, s_at.v)
                wait("vector", s_hl, s_hl.v)
                op("vector", "tensor_tensor", kaug[0:64, tb * TB:tb * TB + TB],
                   kbuf[0:64, tb * TB:tb * TB + TB], pb[6][0:64, 0:TB],
                   op=ALU.mult, inc=s_av)
                klv["b6"] = s_av.v
            if "klonly" in debug:
                continue
            if "nomk" not in debug:
                wait("vector", s_av, s_av.v)
                wait("vector", s_at, s_at.v)
                op("vector", "tensor_tensor", sqb64[:, 0:TCORE], kaug[0:64, 0:TCORE],
                   kaug[0:64, 0:TCORE], op=ALU.mult, inc=s_av)
                for qb in range(NTB):
                    wait("tensor", s_av, s_av.v)
                    wait("tensor", s_as, s_as.v)
                    if h == 0 and qb == 0:
                        wait("tensor", B.sems["pnxs"], B.sems["pnxs"].v)
                    op("tensor", "matmul", pb[7][0:1, 0:TB], ones_cb[0:64, 0:1],
                       sqb64[:, qb * TB:qb * TB + TB], start=True, stop=True, inc=s_at)
                    wait("scalar", s_at, s_at.v)
                    if h == 0 and qb == 0:
                        wait("scalar", s_as, s_as.v)
                    op("scalar", "copy", rowsflat[0:1, qb * TB:qb * TB + TB],
                       pb[7][0:1, 0:TB], inc=s_as)
                wait("vector", s_as, s_as.v)
                op("vector", "tensor_reduce", mxr[0:1, 18:19],
                   rowsflat[0:1, 0:TCORE].rearrange("o (g k) -> o g k", g=1),
                   axis=AX.X, op=ALU.max, inc=s_av)
                wait("scalar", s_av, s_av.v)
                op("scalar", "activation", mxr[0:1, 18:19], mxr[0:1, 18:19], AF.Sqrt, inc=s_as)
                wait("vector", s_as, s_as.v)
                op("vector", "tensor_scalar", mxr[0:1, 19:20], mxr[0:1, 18:19], -1.0, None,
                   op0=ALU.mult, inc=s_av)
                wait("vector", s_at, s_at.v)
                op("vector", "tensor_tensor", sqb64[:, 0:TCORE], qaug[0:64, 0:TCORE],
                   qaug[0:64, 0:TCORE], op=ALU.mult, inc=s_av)
                for qb in range(NTB):
                    wait("tensor", s_av, s_av.v)
                    wait("tensor", s_as, s_as.v)
                    op("tensor", "matmul", pb[7][0:1, 0:TB], ones_cb[0:64, 0:1],
                       sqb64[:, qb * TB:qb * TB + TB], start=True, stop=True, inc=s_at)
                    wait("scalar", s_at, s_at.v)
                    op("scalar", "copy", rows2[0:1, qb * TB:qb * TB + TB],
                       pb[7][0:1, 0:TB], inc=s_as)
                wait("scalar", s_as, s_as.v)
                op("scalar", "activation", rows2[0:1, 0:TCORE], rows2[0:1, 0:TCORE],
                   AF.Sqrt, inc=s_as)
                wait("vector", s_as, s_as.v)
                op("vector", "tensor_scalar", rows2[0:1, 0:TCORE], rows2[0:1, 0:TCORE],
                   mxr[0:1, 19:20], 75.0, op0=ALU.mult, op1=ALU.add, inc=s_av)
                wait("scalar", s_av, s_av.v)
                op("scalar", "copy", qaug[64:65, 0:TCORE], rows2[0:1, 0:TCORE], inc=s_as)
            QAUGR = s_as.v
            for kb2 in range(NKB // 2):
                wait("tensor", s_hl, s_hl.v)
                wait("tensor", s_av, klv["b6"])
                wait("tensor", s_as, s_as.v)
                op("tensor", "transpose", pbb[6][:, 0:64],
                   vbuf[:, 2 * kb2 * 128:(2 * kb2 + 1) * 128], ident_b[0:64, 0:64])
                op("tensor", "transpose", pbb[6][:, 64:128],
                   vbuf[:, (2 * kb2 + 1) * 128:(2 * kb2 + 2) * 128], ident_b[0:64, 0:64],
                   inc=s_at)
                wait("scalar", s_at, s_at.v)
                op("scalar", "copy", vtokaug[:, 2 * kb2 * 65:2 * kb2 * 65 + 64],
                   pbb[6][:, 0:64])
                op("scalar", "copy", vtokaug[:, (2 * kb2 + 1) * 65:(2 * kb2 + 1) * 65 + 64],
                   pbb[6][:, 64:128], inc=s_as)
            if "no3d" not in debug:
                op("scalar", "activation",
                   vtokaug.rearrange("p (b c) -> p b c", c=65)[:, :, 64:65],
                   ident_b[:, 0:NKB].rearrange("p (b c) -> p b c", c=1),
                   AF.Identity, scale=np.float32(0.0), bias=np.float32(1.0), inc=s_as)
            VTOKR = s_as.v
            for kb in range(NKB if "noqk" not in debug else 0):
                par = kb % 2
                for qb in range(NTB):
                    wait("tensor", s_as, max(QAUGR, VTOKR))
                    wait("tensor", s_as, s_as.v)
                    wait("tensor", s_av, klv["b6"])
                    if h == 0 and kb == 0 and qb == 0:
                        wait("tensor", s_qe, s_qe.v)
                    op("tensor", "matmul", pb[4 + qb][:, 0:TB],
                       kaug[:, kb * 128:(kb + 1) * 128],
                       qaug[:, qb * TB:qb * TB + TB], start=True, stop=True, inc=s_at)
                    wait("scalar", s_at, s_at.v)
                    if kb >= 2:
                        wait("scalar", s_at, pvrec.get(par, 0))
                    op("scalar", "activation",
                       pbuf[:, par * TCORE + qb * TB: par * TCORE + qb * TB + TB],
                       pb[4 + qb][:, 0:TB], AF.Exp, inc=s_as)
                    wait("tensor", s_as, s_as.v)
                    op("tensor", "matmul", pb[qb][0:65, 0:TB],
                       vtokaug[:, kb * 65:kb * 65 + 65],
                       pbuf[:, par * TCORE + qb * TB: par * TCORE + qb * TB + TB],
                       start=(kb == 0), stop=(kb == NKB - 1),
                       inc=(s_at if qb == NTB - 1 else None))
                pvrec[par] = s_at.v
            for qb in range(NTB if "noqk" not in debug else 0):
                wait("scalar", s_at, s_at.v)
                op("scalar", "copy", rowsflat[0:1, qb * TB:qb * TB + TB],
                   pb[qb][64:65, 0:TB], inc=s_as)
            wait("vector", s_as, s_as.v)
            fop("vector", "tensor_scalar", rowsflat[0:1, 0:TCORE],
                rowsflat[0:1, 0:TCORE], 1e-30, None, op0=ALU.max)
            op("vector", "reciprocal", rows2[0:1, 0:TCORE], rowsflat[0:1, 0:TCORE],
               inc=s_av)
            rrsb = wkvvt_sb.bitcast(F32)[0:64, 0:TB]
            for qb in range(NTB if "noqk" not in debug else 0):
                wait("tensor", s_av, s_av.v)
                op("tensor", "matmul", pb[4 + qb][0:64, 0:TB], ones_f[0:1, 0:64],
                   rows2[0:1, qb * TB:qb * TB + TB], start=True, stop=True, inc=s_at)
                wait("scalar", s_at, s_at.v)
                if h == 0 and qb == 0:
                    wait("scalar", B.sems["fvq"], B.sems["fvq"].v)
                else:
                    wait("scalar", s_av, s_av.v)
                op("scalar", "copy", rrsb, pb[4 + qb][0:64, 0:TB], inc=s_as)
                wait("vector", s_as, s_as.v)
                if h == 0 and qb == 0:
                    wait("vector", s_qm, QKVMM)
                op("vector", "tensor_tensor",
                   hfm[64 * (h % 2):64 * (h % 2) + 64,
                       (h // 2) * TCORE + qb * TB: (h // 2) * TCORE + qb * TB + TB],
                   pb[qb][0:64, 0:TB], rrsb, op=ALU.mult, inc=s_av)
        ATTNV = s_av.v

        s_om = S_("fmm"); s_oev2 = S_("pnv")
        omv = {}
        oevv = {}
        for ob in range(8 if "noheads" not in debug else 0):
            par = ob % 2
            w = qw[par]; sw = s_qw[par]
            wait("gpsimd", sw, sw.v)
            if ob == 0:
                wait("gpsimd", s_qm, QKVMM)
            if ob >= 2:
                wait("gpsimd", s_om, omv[ob - 2])
            for q in range(8):
                dma("gpsimd", w[:, q * 128:(q + 1) * 128],
                    woutt[q * 128:(q + 1) * 128, ob * 128:(ob + 1) * 128], sw)
            wait("tensor", sw, sw.v)
            if ob == 0:
                wait("tensor", s_av, ATTNV)
            for tb in range(NTB):
                if ob > 0:
                    wait("tensor", s_oev2, oevv.get((ob - 1, tb), 0))
                for q in range(8):
                    op("tensor", "matmul", pb[tb][:, 0:TB], w[:, q * 128:(q + 1) * 128],
                       hfm[:, q * TCORE + tb * TB: q * TCORE + tb * TB + TB],
                       start=(q == 0), stop=(q == 7), inc=(s_om if q == 7 else None))
            omv[ob] = s_om.v
            for tb in range(NTB):
                sl = (ob * NTB + tb) % 2
                ab = xhat[:, sl * TB:sl * TB + TB] if TB == 512 else \
                     xhat[:, sl * TB:sl * TB + TB]
                wait("scalar", s_om, omv[ob])
                if ob >= 1:
                    wait("scalar", s_oev2, oevv.get((ob - 1, tb), 0))
                op("scalar", "activation", ab, pb[tb][:, 0:TB], AF.Identity,
                   scale=1.0, bias=boutc_sb[:, ob:ob + 1], inc=s_as)
                wait("vector", s_as, s_as.v)
                op("vector", "scalar_tensor_tensor",
                   A[:, ob * TCORE + tb * TB: ob * TCORE + tb * TB + TB],
                   ab, modT[:, 64 + ob:65 + ob],
                   A[:, ob * TCORE + tb * TB: ob * TCORE + tb * TB + TB],
                   op0=ALU.mult, op1=ALU.add, inc=s_oev2)
                oevv[(ob, tb)] = s_oev2.v
        s_x3, x3cnt = s_oev2, s_oev2.v
        if dbg_x3 is not None:
            s_dbg = S_("s_dbg")
            wait("gpsimd", s_x3, x3cnt)
            for _dc in range(0, 8 * TCORE, 2048):
                dma("gpsimd", dbg_x3[:, _dc:_dc + 2048], A[:, _dc:_dc + 2048], s_dbg)

    if phases >= 5:
        pn_ready2 = [("scalar", s_x3, x3cnt), ("vector", s_x3, x3cnt),
                     ("tensor", s_x3, x3cnt),
                     ("scalar", B.sems["fmm"], B.sems["fmm"].v),
                     ("vector", B.sems["fmm"], B.sems["fmm"].v),
                     ("tensor", B.sems["has"], B.sems["has"].v),
                     ("vector", B.sems["has"], B.sems["has"].v),
                     ("scalar", B.sems["has"], B.sems["has"].v),
                     ("scalar", B.sems["hav"], B.sems["hav"].v),
                     ("scalar", B.sems["hat"], B.sems["hat"].v),
                     ("vector", B.sems["hat"], B.sems["hat"].v),
                     ("vector", B.sems["pnv"], B.sems["pnv"].v)]
        s_hq1, hq1 = emit_prenorm(80, 72, 4, pn_ready2)
        if "s_dbg" in B.sems:
            wait("vector", B.sems["s_dbg"], B.sems["s_dbg"].v)
        s_f1, f1cnt = emit_ffn(1, 88, 5, 6, 7, s_hq1, hq1)

    if phases >= 6:
        s_otr = S_("hat"); s_oe = S_("has"); s_out = S_("s_out")
        outv = {}
        for i in range(NTILE):
            par = i % 2
            ostg = xtile[par]
            for kk in range(4):
                if i == 0 and kk == 0:
                    wait("tensor", s_f1, f1cnt)
                    wait("tensor", B.sems["pnv"], B.sems["pnv"].v)
                wait("tensor", s_oe, s_oe.v)
                op("tensor", "transpose", pb[5][:, 0:128],
                   A[:, 2 * kk * TCORE + i * 128: 2 * kk * TCORE + (i + 1) * 128],
                   ident_f[:])
                op("tensor", "transpose", pb[5][:, 128:256],
                   A[:, (2 * kk + 1) * TCORE + i * 128: (2 * kk + 1) * TCORE + (i + 1) * 128],
                   ident_f[:], inc=s_otr)
                wait("scalar", s_otr, s_otr.v)
                if i >= 2 and kk == 0:
                    wait("scalar", s_out, outv[i - 2])
                op("scalar", "copy", ostg[:, 2 * kk * 128:(2 * kk + 1) * 128],
                   pb[5][:, 0:128])
                op("scalar", "copy", ostg[:, (2 * kk + 1) * 128:(2 * kk + 2) * 128],
                   pb[5][:, 128:256], inc=s_oe)
            wait("gpsimd", s_oe, s_oe.v)
            wait("gpsimd", s_out, s_out.v)
            dma("gpsimd", out_h[i * 128:(i + 1) * 128, :], ostg[:], s_out)
            outv[i] = s_out.v


    for semname in ["s_agy", "s_yp20", "s_yp21", "s_hp0", "s_agw", "s_wqp0", "s_wqp1", "s_res", "s_out", "fsp0", "fsp1", "gsp0", "gsp1"]:
        if semname in B.sems:
            sem = B.sems[semname]
            wait("gpsimd", sem, sem.v)
    for dbgsem in ("s_dmod", "s_dh", "s_dv2", "s_dyag", "s_dres", "s_dwq", "s_dbg"):
        if dbgsem in B.sems:
            wait("gpsimd", B.sems[dbgsem], B.sems[dbgsem].v)

    with nc.Block() as block:
        block.sync(lambda e: B.run("sync", e))
        block.scalar(lambda e: B.run("scalar", e))
        block.vector(lambda e: B.run("vector", e))
        block.tensor(lambda e: B.run("tensor", e))
        block.gpsimd(lambda e: B.run("gpsimd", e))

    return nc, douts


Builder.run = lambda self, name, e: [fn(e) for fn in self.items[name]] and None


def prep_inputs(ins, S=16384):
    TCORE = S // NCORES
    MODS = 6 * D // NCORES
    x = np.asarray(ins["L0_attn_x"], np.float32).reshape(-1, D)[:S]
    W0 = np.asarray(ins["L0_attn_W0"], np.float32)
    Wkv = np.asarray(ins["L0_attn_Wkv"], np.float32)
    bkv = np.asarray(ins["L0_attn_bkv"], np.float32)
    eta = np.asarray(ins["L0_attn_eta"], np.float32).reshape(1, 1)
    c0 = np.asarray(ins["L0_c"], np.float32).reshape(-1)
    wmod0 = np.asarray(ins["L0_wmod"], np.float32)
    c1 = np.asarray(ins["L1_c"], np.float32).reshape(-1)
    wmod1 = np.asarray(ins["L1_wmod"], np.float32)
    g0 = np.asarray(ins["L0_ffn_w_gate"], np.float32)
    u0 = np.asarray(ins["L0_ffn_w_up"], np.float32)
    d0 = np.asarray(ins["L0_ffn_w_down"], np.float32)
    g1 = np.asarray(ins["L1_ffn_w_gate"], np.float32)
    u1 = np.asarray(ins["L1_ffn_w_up"], np.float32)
    d1 = np.asarray(ins["L1_ffn_w_down"], np.float32)
    wqkvt = np.ascontiguousarray(np.asarray(ins["L1_attn_Wqkv"], np.float32).T)
    woutt = np.ascontiguousarray(np.asarray(ins["L1_attn_Wout"], np.float32).T)
    wl1t = np.ascontiguousarray(np.asarray(ins["L1_attn_Wl1"], np.float32).T)
    bl1c = np.ascontiguousarray(np.asarray(ins["L1_attn_bl1"], np.float32).reshape(64, 1))
    wl2t = np.ascontiguousarray(np.asarray(ins["L1_attn_Wl2"], np.float32).T)
    bl2c = np.ascontiguousarray(np.asarray(ins["L1_attn_bl2"], np.float32).reshape(16, 1))
    boutc = np.ascontiguousarray(np.asarray(ins["L1_attn_bout"], np.float32).reshape(8, 128).T)

    import ml_dtypes
    identbf = np.eye(128, dtype=ml_dtypes.bfloat16)
    identf = np.eye(128, dtype=np.float32)
    onesf = np.ones((1, 128), np.float32)
    trium = np.triu(np.ones((128, 128), np.float32), 1)
    onescolbf = np.ones((128, 1), ml_dtypes.bfloat16)

    maps = []
    for i in range(NCORES):
        fo = slice(128 * i, 128 * (i + 1))
        f3 = slice(384 * i, 384 * (i + 1))
        maps.append({
            "xin": np.ascontiguousarray(x[TCORE * i:TCORE * (i + 1)]),
            "w0t": np.ascontiguousarray(W0[fo, :].T),
            "wkvvt": np.ascontiguousarray(Wkv[D:, :][fo, :].T),
            "bkvv": np.ascontiguousarray(bkv[D:][fo].reshape(128, 1)),
            "eta": eta,
            "c0t": np.ascontiguousarray(c0.reshape(8, 128).T),
            "wmod0s": np.ascontiguousarray(wmod0[:, MODS * i:MODS * (i + 1)]),
            "c1t": np.ascontiguousarray(c1.reshape(8, 128).T),
            "wmod1s": np.ascontiguousarray(wmod1[:, MODS * i:MODS * (i + 1)]),
            "g0s": np.ascontiguousarray(g0[f3, :]),
            "u0s": np.ascontiguousarray(u0[f3, :]),
            "d0s": np.ascontiguousarray(d0[fo, :]),
            "g1s": np.ascontiguousarray(g1[f3, :]),
            "u1s": np.ascontiguousarray(u1[f3, :]),
            "d1s": np.ascontiguousarray(d1[fo, :]),
            "wqkvt": wqkvt, "woutt": woutt, "wl1t": wl1t, "bl1c": bl1c,
            "wl2t": wl2t, "bl2c": bl2c, "boutc": boutc,
            "pos": (np.arange(TCORE * i + 1, TCORE * (i + 1) + 1, dtype=np.float32)
                    .reshape(1, TCORE)),
            "identbf": identbf, "identf": identf, "onesf": onesf, "trium": trium,
            "onescolbf": onescolbf,
        })
    return maps


build_p01 = None


if __name__ == "__main__":
    nc, _ = build_full(S=2048, debug=("h", "mod", "y", "v"))
    print("built ok")


def kernel(**inputs):
    S = 16384
    maps = prep_inputs(inputs, S=S)
    nc, _ = build_full(S=S, debug=(), phases=6)
    res = run_bass_kernel_spmd(nc, maps, core_ids=list(range(NCORES)))
    out = np.concatenate([np.asarray(res.results[i]["out"], np.float32)
                          for i in range(NCORES)], axis=0)
    return out.reshape(1, S, 1024)


# version 06 (final)
# speedup vs baseline: 1.0151x; 1.0151x over previous
import numpy as np
import concourse.bass as bass
import concourse.mybir as mybir
from concourse.bass_utils import run_bass_kernel_spmd

F32 = mybir.dt.float32
BF16 = mybir.dt.bfloat16
AF = mybir.ActivationFunctionType
ALU = mybir.AluOpType
AX = mybir.AxisListType

NCORES = 8
D = 1024
CH = 128
MAGIC = float(np.float32(12582912.0))
ENGINES = ("sync", "scalar", "vector", "tensor", "gpsimd")


class Sem:
    def __init__(self, nc, name):
        self.h = nc.alloc_semaphore(name)
        self.v = 0
    def inc(self, k=1):
        self.v += k
        return self.v


class Builder:
    def __init__(self, nc):
        self.nc = nc
        self.items = {e: [] for e in ENGINES}
        self.sems = {}
    ALIASES = {
        "s_ld2": "s_ld", "s_prep2": "s_prep", "s_mod2": "s_mod", "s_mev2": "s_mev",
        "s_mpub2": "s_mpub", "s_ccm2": "s_ccm", "s_mld2": "s_mld", "s_gab2": "s_gab",
        "s_adv2": "s_adv",
        "s_cmm": "s_umm", "s_ytr": "s_vtr",
        "fev": "pnv", "fdmm": "fmm", "ffres": "s_res", "fpw0": "fsp0", "fpw1": "fsp1",
        "frl0": "fwl0", "frl1": "fwl1", "arpub": "arA", "arrb": "arA",
        "arpv": "pnxv", "arrpm": "pnxt", "pnq1": "pnc", "pnq2": "pnq",
        "pmx": "pnxt", "fmx": "pnxv",
        "gsp0": "fsp0", "gsp1": "fsp1", "hag": "fsp0", "hlo": "fwl0",
    }
    def S(self, name):
        name = self.ALIASES.get(name, name)
        if name not in self.sems:
            self.sems[name] = Sem(self.nc, name)
        return self.sems[name]
    def emit(self, eng, fn):
        self.items[eng].append(fn)
    def dma(self, eng, out, in_, sem, **kw):
        sem.inc(16)
        self.emit(eng, lambda e, out=out, in_=in_, h=sem.h, kw=kw:
                  e.dma_start(out=out, in_=in_, **kw).then_inc(h, 16))
        return sem.v
    def wait(self, eng, sem, val=None):
        v = sem.v if val is None else val
        if v > 0:
            self.emit(eng, lambda e, h=sem.h, v=v: e.wait_ge(h, v))
    def op(self, eng, method, *args, inc=None, **kw):
        if inc is not None:
            inc.inc(1)
        def fn(e, method=method, args=args, kw=kw, inc=inc):
            r = getattr(e, method)(*args, **kw)
            if inc is not None:
                r.then_inc(inc.h, 1)
        self.emit(eng, fn)
        return None if inc is None else inc.v
    def fop(self, eng, method, *args, **kw):
        f = self.S("fence_" + eng)
        self.op(eng, method, *args, inc=f, **kw)
        self.wait(eng, f, f.v)
    def cc(self, kind, op_, ins_, outs_, sem):
        sem.inc(1)
        self.emit("gpsimd", lambda e, kind=kind, op_=op_, i=ins_, o=outs_, h=sem.h:
                  e.collective_compute(kind, op_, replica_groups=[list(range(NCORES))],
                                       ins=i, outs=o).then_inc(h, 1))
        return sem.v


def build_full(S=16384, debug=(), phases=6):
    FF = 3072
    TCORE = S // NCORES
    NTILE = TCORE // 128
    NCH = S // CH
    OWNCH = NTILE
    NPIECE = min(4, TCORE // 128)
    PTOK = S // NPIECE
    PCORE = TCORE // NPIECE
    PTILE = NTILE // NPIECE
    VSPAN = min(512, PCORE)
    MODS = 6 * D // NCORES
    MFS = MODS // 128

    nc = bass.Bass("TRN2", target_bir_lowering=False)
    B = Builder(nc)
    S_, dma, wait, op, cc = B.S, B.dma, B.wait, B.op, B.cc
    fop = B.fop

    def din(name, shape, dt=F32):
        return nc.dram_tensor(name, shape, dt, kind="ExternalInput").ap()
    xin = din("xin", [TCORE, D])
    w0t = din("w0t", [D, 128])
    wkvvt = din("wkvvt", [D, 128])
    bkvv = din("bkvv", [128, 1])
    eta_h = din("eta", [1, 1])
    c0t = din("c0t", [128, 8])
    wmod0s = din("wmod0s", [D, MODS])
    c1t = din("c1t", [128, 8])
    wmod1s = din("wmod1s", [D, MODS])
    g0s = din("g0s", [3 * 128, D])
    u0s = din("u0s", [3 * 128, D])
    d0s = din("d0s", [128, 3 * D])
    g1s = din("g1s", [3 * 128, D])
    u1s = din("u1s", [3 * 128, D])
    d1s = din("d1s", [128, 3 * D])
    wqkvt = din("wqkvt", [D, 3 * D])
    woutt = din("woutt", [D, D])
    wl1t = din("wl1t", [1, 64])
    bl1c = din("bl1c", [64, 1])
    wl2t = din("wl2t", [64, 16])
    bl2c = din("bl2c", [16, 1])
    pos = din("pos", [1, TCORE])
    boutc = din("boutc", [128, 8])
    onescolbf = din("onescolbf", [128, 1], BF16)
    identbf = din("identbf", [128, 128], BF16)
    identf = din("identf", [128, 128], F32)
    onesf = din("onesf", [1, 128], F32)
    trium = din("trium", [128, 128], F32)

    douts = {}
    def dout(name, shape, dt=F32):
        douts[name] = nc.dram_tensor(name, shape, dt, kind="ExternalOutput").ap()
        return douts[name]
    dbg_y = dout("dbg_y", [128, S], BF16) if "y" in debug else None
    dbg_hfm = dout("dbg_hfm", [128, 8 * TCORE], BF16) if "h" in debug else None
    dbg_mod = dout("dbg_mod", [128, 48], F32) if "mod" in debug else None
    dbg_v = dout("dbg_v", [128, S], F32) if "v" in debug else None
    dbg_c1 = dout("dbg_c1", [128, 8 * 128 + 1024 + 256 + 128 + 128 + 1024 + 512], F32) if "c1" in debug else None
    dbg_res = dout("dbg_res", [128, 8 * TCORE], F32) if "res" in debug else None
    dbg_wq = dout("dbg_wq", [128, 256], BF16) if "wq" in debug else None
    dbg_h2 = dout("dbg_h2", [128, 8 * TCORE], BF16) if "h2" in debug else None
    dbg_x2 = dout("dbg_x2", [128, 8 * TCORE], F32) if "x2" in debug else None
    dbg_x3 = dout("dbg_x3", [128, 8 * TCORE], F32) if "x3" in debug else None

    agin_mod = nc.dram_tensor("agin_mod", [1, MODS], F32)
    agout_mod = nc.dram_tensor("agout_mod", [NCORES, MODS], F32, addr_space="Shared")
    agin_h = nc.dram_tensor("agin_h", [1, D * TCORE * 2], BF16)
    agout_h = nc.dram_tensor("agout_h", [NCORES, D * TCORE * 2], BF16, addr_space="Shared")
    agin_g = nc.dram_tensor("agin_g", [1, OWNCH * 2 * 128 * 128], BF16)
    agout_g = nc.dram_tensor("agout_g", [NCORES, OWNCH * 2 * 128 * 128], BF16, addr_space="Shared")
    agin_y = [nc.dram_tensor(f"agin_y{p}", [128, PTOK], BF16) for p in range(NPIECE)]
    agout_yall = nc.dram_tensor("agout_yall", [NPIECE * NCORES * 128, PTOK], BF16, addr_space="Shared")
    agout_y = [agout_yall.ap()[p * NCORES * 128:(p + 1) * NCORES * 128, :] for p in range(NPIECE)]
    WDQN = 3 * FF * 128
    agin_wdq = [nc.dram_tensor(f"agin_wdq{l}", [1, WDQN], BF16) for l in range(2)]
    agout_wdq = [nc.dram_tensor(f"agout_wdq{l}", [NCORES, WDQN], BF16, addr_space="Shared") for l in range(2)]
    MXW = [1, 2, 1, 1, 1, 2, 1, 1]
    agin_mx = [nc.dram_tensor(f"agin_mx{j}", [1, MXW[j]], F32) for j in range(8)]
    agout_mx = [nc.dram_tensor(f"agout_mx{j}", [NCORES, MXW[j]], F32, addr_space="Shared") for j in range(8)]
    dscr1 = nc.dram_tensor("dscr1", [FF, TCORE], BF16)
    dscr2 = nc.dram_tensor("dscr2", [FF, TCORE], BF16)
    out_h = nc.dram_tensor("out", [TCORE, D], F32, kind="ExternalOutput").ap() if phases >= 6 else None

    sb = lambda name, shape, dt=F32: nc.alloc_sbuf_tensor(name, shape, dt).ap()
    A = sb("arenaA", [128, S], F32)
    hfm = sb("hfm", [128, 8 * TCORE], BF16)
    if S >= 16384:
        htok = A.bitcast(BF16)[:, 2 * MODS * 8:2 * MODS * 8 + NTILE * D]
    else:
        htok = sb("htok", [128, NTILE * D], BF16)
    xtile = [sb(f"xtile{j}", [128, D]) for j in range(2)]
    xhat = sb("xhat", [128, D])
    tt1 = sb("tt1", [128, D])
    carep = sb("carep", [128, D])
    sarep = sb("sarep", [128, D])
    rows8 = sb("rows8", [64, 128])
    rowsflat = sb("rowsflat", [1, 16 * 128])
    stats = sb("stats", [128, 8 * NTILE])
    modT = sb("modT", [128, 96])
    csc = sb("csc", [128, 16])
    wmods = A[:, 0:8 * MODS] if S >= 16384 else sb("wmods", [128, 8 * MODS])
    cval = sb("cval", [128, 1])
    ngc = sb("ngc", [128, 1])
    ident_b = sb("ident_b", [128, 128], BF16)
    ones_cb = sb("ones_cb", [128, 1], BF16)
    ones_rb = sb("ones_rb", [1, 128], BF16)
    boutc_sb = sb("boutc_sb", [128, 8])
    bl1c_sb = sb("bl1c_sb", [64, 1])
    bl2c_sb = sb("bl2c_sb", [16, 1])
    wl2b = sb("wl2b", [64, 16], BF16)
    lamsc = sb("lamsc", [16, TCORE], BF16)
    rows2 = sb("rows2", [1, max(TCORE, 1024)])
    mxp = sb("mxp", [128, 8])
    mxr = sb("mxr", [1, 24])
    qcols = sb("qcols", [128, 48])
    ident_f = sb("ident_f", [128, 128], F32)
    ones_f = sb("ones_f", [1, 128], F32)
    triu_f = sb("triu_f", [128, 128], F32)
    w0t_sb = sb("w0t_sb", [128, 8 * 128])
    wkvvt_sb = sb("wkvvt_sb", [128, 8 * 128], BF16)
    bkvv_sb = sb("bkvv_sb", [128, 1])
    gup = sb("gup", [128, 128])
    nlo = sb("nlo", [128, 128], BF16)
    nup = sb("nup", [128, 128], BF16)
    ilq = sb("ilq", [128, 128], BF16)
    rbuf = sb("rbuf", [128, 128], BF16)
    gpub = sb("gpub", [128, OWNCH * 256], BF16)
    wbhi = sb("wbhi", [128, 8 * 128], BF16)
    wblo = sb("wblo", [128, 8 * 128], BF16)
    xf_c = [sb(f"xf_c{j}", [128, 8 * 128], BF16) for j in range(2)]
    xt_c = [sb(f"xt_c{j}", [128, D], BF16) for j in range(2)]
    gt_c = [sb(f"gt_c{j}", [128, 2 * 128], BF16) for j in range(2)]
    vxf = [sb(f"vxf{j}", [128, 8 * VSPAN], BF16) for j in range(2)]
    d_c = sb("d_c", [128, 128], BF16)
    u_c = sb("u_c", [128, 128], BF16)
    y_c = sb("y_c", [128, 128], BF16)
    yt_c2 = [sb(f"yt_c2{j}", [128, 128], BF16) for j in range(2)]
    vstage = sb("vstage", [128, VSPAN])

    pb = [nc.alloc_psum_tensor(f"pb{k}", [128, 512], F32).ap() for k in range(8)]
    pbb = [p.bitcast(BF16) for p in pb]
    def WP(k):
        return pb[k // 4][:, (k % 4) * 128:(k % 4) * 128 + 128]

    s_ld = S_("s_ld")
    dma("sync", ident_b[:], identbf[:, :], s_ld)
    dma("sync", ident_f[:], identf[:, :], s_ld)
    dma("sync", ones_f[:], onesf[:, :], s_ld)
    dma("sync", triu_f[:], trium[:, :], s_ld)
    dma("sync", w0t_sb[:].rearrange("p (k f) -> p k f", f=128), w0t.rearrange("(k p) f -> p k f", p=128), s_ld)
    s_gld = S_("s_gld")
    dma("gpsimd", wkvvt_sb[:].rearrange("p (k f) -> p k f", f=128), wkvvt.rearrange("(k p) f -> p k f", p=128), s_gld)
    dma("sync", bkvv_sb[:], bkvv[:, :], s_ld)
    dma("sync", ones_cb[:], onescolbf[:, :], s_ld)
    dma("sync", boutc_sb[:], boutc[:, :], s_ld)
    dma("sync", bl1c_sb[:], bl1c[:, :], s_ld)
    dma("sync", bl2c_sb[:], bl2c[:, :], s_ld)
    s_gld2 = S_("hag")
    dma("gpsimd", wl2b[:], wl2t[:, :], s_gld2)
    dma("sync", csc[:, 0:8], c0t[:, :], s_ld)
    dma("sync", wmods[:].rearrange("p (k f) -> p k f", f=768), wmod0s.rearrange("(k p) f -> p k f", p=128), s_ld)
    dma("sync", cval[0:1, 0:1], eta_h[:, :], s_ld)
    LD = s_ld.v

    s_prep = S_("s_prep")
    wait("scalar", s_ld, LD)
    op("scalar", "mul", cval[0:1, 0:1], cval[0:1, 0:1], float(2.0 / D))
    op("scalar", "copy", ones_rb[:], ones_f[:])
    op("scalar", "activation", csc[:, 0:8], csc[:, 0:8], AF.Silu, inc=s_prep)

    s_cv = S_("s_cv")
    wait("tensor", s_prep)
    op("tensor", "matmul", pb[7][:, 0:1], ones_f[0:1, :], cval[0:1, 0:1], start=True, stop=True, inc=s_cv)
    s_cv2 = S_("s_cv2")
    wait("scalar", s_cv)
    op("scalar", "copy", cval[:, 0:1], pb[7][:, 0:1], inc=s_cv2)
    wait("scalar", s_cv2, s_cv2.v)
    op("scalar", "mul", ngc[:, 0:1], cval[:, 0:1], -1.0, inc=s_cv2)
    CV2 = s_cv2.v

    s_mod = S_("s_mod")
    wait("tensor", s_cv2, CV2)
    for mf in range(MFS):
        for hc in range(8):
            op("tensor", "matmul", pb[7][:, 128 + mf:129 + mf],
               wmods[:, hc * MODS + mf * 128: hc * MODS + mf * 128 + 128],
               csc[:, hc:hc + 1],
               start=(hc == 0), stop=(hc == 7), inc=(s_mod if hc == 7 else None))
    s_mev = S_("s_mev")
    wait("scalar", s_mod, MFS)
    op("scalar", "copy", modT[:, 0:MFS], pb[7][:, 128:128 + MFS], inc=s_mev)
    s_mpub = S_("s_mpub")
    wait("gpsimd", s_mev)
    dma("gpsimd", agin_mod.ap().rearrange("o (p k) -> p (o k)", p=128), modT[:, 0:MFS], s_mpub)
    wait("gpsimd", s_mpub)
    s_ccm = S_("s_ccm")
    cc("AllGather", ALU.bypass, [agin_mod.ap()], [agout_mod.ap()], s_ccm)
    wait("gpsimd", s_ccm)
    s_mld = S_("s_mld")
    dma("gpsimd", modT[:, 0:48].rearrange("p (r k) -> p r k", k=MFS),
        agout_mod.ap().rearrange("r (p k) -> p r k", p=128), s_mld)
    MLD = s_mld.v

    s_adv = S_("s_adv")
    wait("vector", s_mld, MLD)
    op("vector", "tensor_scalar", modT[:, 8:16], modT[:, 8:16], 1.0, None, op0=ALU.add)
    op("vector", "tensor_scalar", modT[:, 32:40], modT[:, 32:40], 1.0, None, op0=ALU.add, inc=s_adv)
    s_gab = S_("s_gab")
    wait("scalar", s_mld, MLD)
    op("scalar", "activation", tt1[:, 0:8], modT[:, 16:24], AF.Abs)
    op("scalar", "activation", tt1[:, 8:16], modT[:, 16:24], AF.Sign)
    op("scalar", "activation", tt1[:, 16:24], modT[:, 40:48], AF.Abs)
    op("scalar", "activation", tt1[:, 24:32], modT[:, 40:48], AF.Sign, inc=s_gab)
    wait("vector", s_gab)
    fop("vector", "tensor_scalar", tt1[:, 0:8], tt1[:, 0:8], 0.25, None, op0=ALU.max)
    fop("vector", "tensor_scalar", tt1[:, 16:24], tt1[:, 16:24], 0.25, None, op0=ALU.max)
    op("vector", "tensor_tensor", modT[:, 16:24], tt1[:, 0:8], tt1[:, 8:16], op=ALU.mult)
    op("vector", "tensor_tensor", modT[:, 40:48], tt1[:, 16:24], tt1[:, 24:32], op=ALU.mult, inc=s_adv)
    ADV = s_adv.v
    if dbg_mod is not None:
        s_dmod = S_("s_dmod")
        wait("gpsimd", s_adv, ADV)
        dma("gpsimd", dbg_mod[:, :], modT[:, 0:48], s_dmod)

    agin_mod2 = nc.dram_tensor("agin_mod2", [1, MODS], F32)
    agout_mod2 = nc.dram_tensor("agout_mod2", [NCORES, MODS], F32, addr_space="Shared")
    s_ld2 = S_("s_ld2")
    wait("sync", s_mod, MFS)
    dma("sync", wmods[:].rearrange("p (k f) -> p k f", f=MODS), wmod1s.rearrange("(k p) f -> p k f", p=128), s_ld2)
    dma("sync", csc[:, 8:16], c1t[:, :], s_ld2)
    s_prep2 = S_("s_prep2")
    wait("scalar", s_ld2, s_ld2.v)
    op("scalar", "activation", csc[:, 8:16], csc[:, 8:16], AF.Silu, inc=s_prep2)
    s_mod2 = S_("s_mod2")
    wait("tensor", s_prep2, s_prep2.v)
    wait("tensor", s_mev, 1)
    for mf in range(MFS):
        for hc in range(8):
            op("tensor", "matmul", pb[7][:, 128 + mf:129 + mf],
               wmods[:, hc * MODS + mf * 128: hc * MODS + mf * 128 + 128],
               csc[:, 8 + hc:9 + hc],
               start=(hc == 0), stop=(hc == 7), inc=(s_mod2 if hc == 7 else None))
    s_mev2 = S_("s_mev2")
    wait("scalar", s_mod2, s_mod2.v)
    op("scalar", "copy", modT[:, 48:48 + MFS], pb[7][:, 128:128 + MFS], inc=s_mev2)
    s_mpub2 = S_("s_mpub2")
    wait("gpsimd", s_mev2)
    dma("gpsimd", agin_mod2.ap().rearrange("o (p k) -> p (o k)", p=128), modT[:, 48:48 + MFS], s_mpub2)
    wait("gpsimd", s_mpub2)
    s_ccm2 = S_("s_ccm2")
    cc("AllGather", ALU.bypass, [agin_mod2.ap()], [agout_mod2.ap()], s_ccm2)
    wait("gpsimd", s_ccm2)
    s_mld2 = S_("s_mld2")
    dma("gpsimd", modT[:, 48:96].rearrange("p (r k) -> p r k", k=MFS),
        agout_mod2.ap().rearrange("r (p k) -> p r k", p=128), s_mld2)
    s_adv2 = S_("s_adv2")
    wait("vector", s_mld2, s_mld2.v)
    op("vector", "tensor_scalar", modT[:, 56:64], modT[:, 56:64], 1.0, None, op0=ALU.add)
    op("vector", "tensor_scalar", modT[:, 80:88], modT[:, 80:88], 1.0, None, op0=ALU.add, inc=s_adv2)
    s_gab2 = S_("s_gab2")
    wait("scalar", s_mld2, s_mld2.v)
    wait("scalar", s_adv, ADV)
    op("scalar", "activation", tt1[:, 32:40], modT[:, 64:72], AF.Abs)
    op("scalar", "activation", tt1[:, 40:48], modT[:, 64:72], AF.Sign)
    op("scalar", "activation", tt1[:, 48:56], modT[:, 88:96], AF.Abs)
    op("scalar", "activation", tt1[:, 56:64], modT[:, 88:96], AF.Sign, inc=s_gab2)
    wait("vector", s_gab2, s_gab2.v)
    fop("vector", "tensor_scalar", tt1[:, 32:40], tt1[:, 32:40], 0.25, None, op0=ALU.max)
    fop("vector", "tensor_scalar", tt1[:, 48:56], tt1[:, 48:56], 0.25, None, op0=ALU.max)
    op("vector", "tensor_tensor", modT[:, 64:72], tt1[:, 32:40], tt1[:, 40:48], op=ALU.mult)
    op("vector", "tensor_tensor", modT[:, 88:96], tt1[:, 48:56], tt1[:, 56:64], op=ALU.mult, inc=s_adv2)
    ADV2 = s_adv2.v

    s_tr = S_("s_tr")
    wait("tensor", s_adv, ADV)
    op("tensor", "transpose", pb[6][0:8, 0:128], modT[:, 8:16], ident_f[:])
    op("tensor", "transpose", pb[6][0:8, 128:256], modT[:, 0:8], ident_f[:], inc=s_tr)
    s_tr2 = S_("s_tr2")
    wait("scalar", s_tr)
    op("scalar", "copy", rows8[0:8, 0:128], pb[6][0:8, 0:128])
    op("scalar", "copy", rows8[32:40, 0:128], pb[6][0:8, 128:256], inc=s_tr2)
    s_rflat = S_("s_rflat")
    wait("gpsimd", s_tr2)
    dma("gpsimd", rowsflat[0:1, 0:8 * 128], rows8[0:8, :], s_rflat)
    dma("gpsimd", rowsflat[0:1, 8 * 128:16 * 128], rows8[32:40, :], s_rflat)
    s_rep = S_("s_rep")
    s_repev = S_("s_repev")
    wait("tensor", s_rflat)
    wait("tensor", s_mev, 1)
    for k in range(8):
        if k > 0:
            wait("tensor", s_repev, s_repev.v)
        op("tensor", "matmul", pb[7][:, 0:128], ones_f[0:1, :], rowsflat[0:1, k * 128:(k + 1) * 128], start=True, stop=True)
        op("tensor", "matmul", pb[7][:, 128:256], ones_f[0:1, :], rowsflat[0:1, (8 + k) * 128:(9 + k) * 128], start=True, stop=True, inc=s_rep)
        wait("scalar", s_rep, s_rep.v)
        op("scalar", "copy", carep[:, k * 128:k * 128 + 128], pb[7][:, 0:128])
        op("scalar", "copy", sarep[:, k * 128:k * 128 + 128], pb[7][:, 128:256], inc=s_repev)
    REPDONE = s_repev.v

    s_hst = S_("s_hst")
    s_hsc = S_("s_hsc")
    s_hv = S_("s_hv")
    s_htr = S_("s_htr")
    s_hev = S_("s_hev")
    s_gmm = S_("s_gmm")
    s_ovec = S_("s_ovec")
    s_osq = S_("s_osq")
    s_oev = S_("s_oev")
    s_hp = [S_("s_hp0")]
    s_gpubd = S_("s_gpubd")
    s_agdone = S_("s_agdone")

    s_xp = [S_("s_xp0"), S_("s_xp1")]
    dma("sync", xtile[0][:], xin[0:128, :], s_xp[0])
    for i in range(NTILE):
        buf = xtile[i % 2]
        if i + 1 < NTILE:
            if i >= 1:
                wait("sync", s_hsc, s_hsc.v)
                wait("sync", s_xp[(i + 1) % 2], s_xp[(i + 1) % 2].v)
            dma("sync", xtile[(i + 1) % 2][:], xin[(i + 1) * 128:(i + 2) * 128, :], s_xp[(i + 1) % 2])
        sc = stats[:, i * 8:(i + 1) * 8]
        wait("vector", s_xp[i % 2], 16 * (i // 2 + 1))
        if i == 0:
            wait("vector", s_repev, REPDONE)
            wait("vector", s_adv2, ADV2)
        op("vector", "tensor_reduce", sc[:, 0:1], buf[:], axis=AX.X, op=ALU.add)
        fop("vector", "tensor_tensor", tt1[:], buf[:], buf[:], op=ALU.mult)
        fop("vector", "tensor_reduce", sc[:, 1:2], tt1[:], axis=AX.X, op=ALU.add)
        fop("vector", "tensor_scalar", sc[:, 2:3], sc[:, 0:1], float(1.0 / D), None, op0=ALU.mult)
        fop("vector", "tensor_scalar", sc[:, 3:4], sc[:, 1:2], float(1.0 / D), None, op0=ALU.mult)
        fop("vector", "tensor_tensor", sc[:, 4:5], sc[:, 2:3], sc[:, 2:3], op=ALU.mult)
        fop("vector", "tensor_tensor", sc[:, 4:5], sc[:, 3:4], sc[:, 4:5], op=ALU.subtract)
        fop("vector", "tensor_scalar", sc[:, 4:5], sc[:, 4:5], 1e-5, None, op0=ALU.add)
        op("vector", "reciprocal", sc[:, 5:6], sc[:, 4:5], inc=s_hst)
        wait("scalar", s_hst, s_hst.v)
        op("scalar", "activation", sc[:, 6:7], sc[:, 5:6], AF.Sqrt, inc=s_hsc)
        wait("vector", s_hsc, s_hsc.v)
        fop("vector", "tensor_tensor", sc[:, 7:8], sc[:, 2:3], sc[:, 6:7], op=ALU.mult)
        op("vector", "tensor_scalar", sc[:, 7:8], sc[:, 7:8], -1.0, None, op0=ALU.mult, inc=s_hst)
        wait("scalar", s_hst, s_hst.v)
        op("scalar", "activation", xhat[:], buf[:], AF.Identity, scale=sc[:, 6:7], bias=sc[:, 7:8], inc=s_hsc)
        wait("vector", s_hsc, s_hsc.v)
        fop("vector", "tensor_tensor", tt1[:], xhat[:], carep[:], op=ALU.mult)
        op("vector", "tensor_tensor", htok[:, i * D:(i + 1) * D], tt1[:], sarep[:], op=ALU.add, inc=s_hv)
        wait("tensor", s_hv, s_hv.v)
        for kk in range(4):
            if not (i == 0 and kk == 0):
                wait("tensor", s_hev, s_hev.v)
            op("tensor", "transpose", pbb[6][:, 0:128], htok[:, i * D + 2 * kk * 128: i * D + (2 * kk + 1) * 128], ident_b[:])
            op("tensor", "transpose", pbb[6][:, 128:256], htok[:, i * D + (2 * kk + 1) * 128: i * D + (2 * kk + 2) * 128], ident_b[:], inc=s_htr)
            wait("scalar", s_htr, s_htr.v)
            op("scalar", "copy", hfm[:, 2 * kk * TCORE + i * 128: 2 * kk * TCORE + (i + 1) * 128], pbb[6][:, 0:128])
            op("scalar", "copy", hfm[:, (2 * kk + 1) * TCORE + i * 128: (2 * kk + 1) * TCORE + (i + 1) * 128], pbb[6][:, 128:256], inc=s_hev)
        ofs_tok = D * TCORE
        wait("gpsimd", s_hv, s_hv.v)
        dma("gpsimd",
            agin_h.ap()[0:1, ofs_tok + i * 128 * D: ofs_tok + (i + 1) * 128 * D]
            .rearrange("o (t f) -> (o t) f", f=D),
            htok[:, i * D:(i + 1) * D], s_hp[0])
        if i + 1 == NTILE:
            wait("gpsimd", s_hev, s_hev.v)
            dma("gpsimd",
                agin_h.ap()[0:1, 0:D * TCORE].rearrange("o (k pp t) -> (o pp) k t", pp=128, t=TCORE),
                hfm.rearrange("pp (k tc) -> pp k tc", tc=TCORE)[:, :, :],
                s_hp[0])
            wait("gpsimd", s_hp[0], 16 * (NTILE + 1))
            cc("AllGather", ALU.bypass, [agin_h.ap()], [agout_h.ap()], s_agdone)

        wait("tensor", s_hev, s_hev.v)
        wait("tensor", s_cv2, CV2)
        wait("tensor", s_ovec, s_ovec.v)
        for k in range(8):
            op("tensor", "matmul", pb[2][:, 0:128],
               hfm[:, k * TCORE + i * 128: k * TCORE + (i + 1) * 128],
               hfm[:, k * TCORE + i * 128: k * TCORE + (i + 1) * 128],
               start=(k == 0), stop=(k == 7), inc=(s_gmm if k == 7 else None))
        wait("vector", s_gmm, s_gmm.v)
        wait("vector", s_adv, ADV)
        fop("vector", "tensor_tensor", gup[:], pb[2][:, 0:128], triu_f[:], op=ALU.mult)
        B.sems["s_ovec"].inc(1) if False else None
        op("vector", "tensor_copy", gpub[:, i * 256 + 128: i * 256 + 256], gup[:], inc=s_ovec)
        fop("vector", "tensor_scalar", nup[:], gup[:], ngc[:, 0:1], None, op0=ALU.mult)
        op("vector", "tensor_tensor", rbuf[:], nup[:], ident_b[:], op=ALU.add, inc=s_ovec)
        wait("tensor", s_ovec, s_ovec.v)
        op("tensor", "transpose", pbb[3][:, 0:128], nup[:], ident_b[:], inc=s_osq)
        wait("scalar", s_osq, s_osq.v)
        op("scalar", "copy", nlo[:], pbb[3][:, 0:128], inc=s_oev)
        for j in range(6):
            wait("tensor", s_oev, s_oev.v)
            op("tensor", "matmul", pb[3][:, 0:128], nlo[:], nup[:], start=True, stop=True)
            op("tensor", "matmul", pb[3][:, 128:256], nup[:], nlo[:], start=True, stop=True, inc=s_osq)
            wait("scalar", s_osq, s_osq.v)
            op("scalar", "copy", nup[:], pb[3][:, 0:128])
            op("scalar", "copy", nlo[:], pb[3][:, 128:256], inc=s_oev)
            wait("vector", s_oev, s_oev.v)
            op("vector", "tensor_tensor", ilq[:], nlo[:], ident_b[:], op=ALU.add, inc=s_ovec)
            wait("tensor", s_ovec, s_ovec.v)
            op("tensor", "matmul", pb[4][:, 0:128], ilq[:], rbuf[:], start=True, stop=True, inc=s_osq)
            wait("scalar", s_osq, s_osq.v)
            op("scalar", "copy", rbuf[:], pb[4][:, 0:128], inc=s_oev)
        wait("vector", s_oev, s_oev.v)
        op("vector", "tensor_scalar", gpub[:, i * 256: i * 256 + 128], rbuf[:], cval[:, 0:1], None,
           op0=ALU.mult, inc=s_ovec)

    wait("gpsimd", s_ovec, s_ovec.v)
    dma("gpsimd", agin_g.ap().rearrange("o (pp f) -> (o pp) f", pp=128), gpub[:, :], s_gpubd)
    wait("gpsimd", s_gpubd, s_gpubd.v)
    cc("AllGather", ALU.bypass, [agin_g.ap()], [agout_g.ap()], s_agdone)
    AG_G = s_agdone.v

    if dbg_hfm is not None:
        s_dh = S_("s_dh")
        wait("gpsimd", s_hev, s_hev.v)
        for _dc in range(0, 8 * TCORE, 2048):
            dma("gpsimd", dbg_hfm[:, _dc:_dc + 2048], hfm[:, _dc:_dc + 2048], s_dh)

    s_wi = S_("s_wi")
    wait("tensor", s_ld, LD)
    for k in range(8):
        op("tensor", "matmul", WP(k), ident_f[:], w0t_sb[:, k * 128:(k + 1) * 128],
           start=(k % 4 == 0), stop=(k % 4 == 3), skip_group_check=True, inc=(s_wi if k == 7 else None))

    s_vx2 = [S_("s_vx20"), S_("s_vx21")]
    s_vmm = S_("s_vmm")
    s_vev = S_("s_vev")
    s_vtr = S_("s_vtr")
    s_vta = S_("s_vta")
    s_xf2 = [S_("s_xf20"), S_("s_xf21")]
    s_wcast = S_("s_wcast")
    s_wlo = S_("s_wlo")
    s_zmm = S_("s_zmm")
    s_dv = S_("s_dv")
    s_umm = S_("s_umm")
    s_uev = S_("s_uev")
    s_cmm = S_("s_cmm")
    s_yevac = S_("s_yevac")
    s_ytr = S_("s_ytr")
    s_ytev = S_("s_ytev")
    s_upd = S_("s_upd")
    s_yp2 = [S_("s_yp20"), S_("s_yp21")]
    YINC = 32 if dbg_y is not None else 16

    def emit_vspan(v):
        r = (v * VSPAN) // TCORE
        tok0 = (v * VSPAN) % TCORE
        buf = vxf[v % 2]
        svx = s_vx2[v % 2]
        wait("sync", s_agdone, 1)
        if v >= 2:
            wait("sync", s_vmm, v - 1)
        wait("sync", svx, svx.v)
        dma("sync", buf[:],
            agout_h.ap()[r:r + 1, 0:D * TCORE]
            .rearrange("o (k pp t) -> (o pp) k t", pp=128, t=TCORE)[:, :, tok0:tok0 + VSPAN], svx)
        wait("tensor", svx, svx.v)
        wait("tensor", B.sems["s_gld"], 16)
        wait("tensor", B.sems["s_repev"], B.sems["s_repev"].v)
        wait("tensor", s_ytev, s_ytev.v)
        wait("tensor", s_vta, s_vta.v)
        wait("tensor", s_vev, s_vev.v)
        for k in range(8):
            op("tensor", "matmul", pb[7][:, 0:VSPAN], wkvvt_sb[:, k * 128:(k + 1) * 128],
               buf[:, k * VSPAN:(k + 1) * VSPAN],
               start=(k == 0), stop=(k == 7), inc=(s_vmm if k == 7 else None))
        wait("scalar", s_vmm, s_vmm.v)
        op("scalar", "activation", vstage[:], pb[7][:, 0:VSPAN], AF.Identity,
           scale=1.0, bias=bkvv_sb[:, 0:1], inc=s_vev)
        wait("tensor", s_vev, s_vev.v)
        for tq in range(VSPAN // 128):
            if tq > 0:
                wait("tensor", s_vta, s_vta.v)
            op("tensor", "transpose", pb[6][:, 0:128], vstage[:, tq * 128:(tq + 1) * 128], ident_f[:], inc=s_vtr)
            wait("scalar", s_vtr, s_vtr.v)
            op("scalar", "copy", A[:, (v * VSPAN // 128 + tq) * 128:(v * VSPAN // 128 + tq + 1) * 128],
               pb[6][:, 0:128], inc=s_vta)

    def emit_chunk(c):
        par = c % 2
        zb = pb[2 + par]
        ub = pb[4 + par]
        xf, xt, gt = xf_c[par], xt_c[par], gt_c[par]
        r = (c * CH) // TCORE
        tokp = (c * CH) % TCORE

        sxf = s_xf2[par]
        if "s_dc1" in B.sems and c == 1:
            for _e in ("scalar", "vector", "tensor"):
                wait(_e, B.sems["s_dc1"], B.sems["s_dc1"].v)
        if "s_dc1b" in B.sems and c == 2:
            for _e in ("scalar", "vector", "tensor"):
                wait(_e, B.sems["s_dc1b"], B.sems["s_dc1b"].v)
        wait("sync", s_agdone, 1)
        if c == 0:
            wait("sync", s_agdone, AG_G)
        if c >= 2:
            wait("sync", s_upd, c - 1)
        wait("sync", sxf, sxf.v)
        if "s_dc1" in B.sems:
            wait("sync", B.sems["s_dc1"], B.sems["s_dc1"].v)
        dma("sync", xf[:],
            agout_h.ap()[r:r + 1, 0:D * TCORE]
            .rearrange("o (k pp t) -> (o pp) k t", pp=128, t=TCORE)[:, :, tokp:tokp + 128], sxf)
        dma("sync", xt[:],
            agout_h.ap()[r:r + 1, D * TCORE:2 * D * TCORE]
            .rearrange("o (t f) -> (o t) f", f=D)[tokp:tokp + 128, :], sxf)
        dma("sync", gt[:],
            agout_g.ap()[c // OWNCH:c // OWNCH + 1, :]
            .rearrange("o (pp f) -> (o pp) f", pp=128)[:, (c % OWNCH) * 256:((c % OWNCH) + 1) * 256], sxf)
        XL = sxf.v

        if c > 0:
            wait("scalar", s_upd, c)
        else:
            wait("scalar", s_wi, 1)
        op("scalar", "activation", wbhi[:, 0:512], pb[0][:, 0:512], AF.Copy)
        op("scalar", "activation", wbhi[:, 512:1024], pb[1][:, 0:512], AF.Copy, inc=s_wcast)
        wait("vector", s_wcast, s_wcast.v)
        op("vector", "scalar_tensor_tensor", wblo[:, 0:512], pb[0][:, 0:512], 1.0, wbhi[:, 0:512],
           op0=ALU.mult, op1=ALU.subtract)
        op("vector", "scalar_tensor_tensor", wblo[:, 512:1024], pb[1][:, 0:512], 1.0, wbhi[:, 512:1024],
           op0=ALU.mult, op1=ALU.subtract, inc=s_wlo)

        wait("tensor", sxf, XL)
        wait("tensor", s_wlo, s_wlo.v)
        if c >= 2:
            wait("tensor", s_yevac, c - 1)
        for k in range(8):
            op("tensor", "matmul", zb[:, 0:128], xf[:, k * 128:(k + 1) * 128], wbhi[:, k * 128:(k + 1) * 128],
               start=(k == 0), stop=False)
        for k in range(8):
            op("tensor", "matmul", zb[:, 0:128], xf[:, k * 128:(k + 1) * 128], wblo[:, k * 128:(k + 1) * 128],
               start=False, stop=(k == 7), inc=(s_zmm if k == 7 else None))
        wait("vector", s_zmm, s_zmm.v)
        wait("vector", s_vta, s_vta.v)
        op("vector", "tensor_tensor", d_c[:], A[:, c * 128:(c + 1) * 128], zb[:, 0:128], op=ALU.subtract, inc=s_dv)
        wait("tensor", s_dv, s_dv.v)
        op("tensor", "matmul", ub[:, 0:128], gt[:, 0:128], d_c[:], start=True, stop=True, inc=s_umm)
        wait("scalar", s_umm, s_umm.v)
        op("scalar", "copy", u_c[:], ub[:, 0:128], inc=s_uev)
        wait("tensor", s_uev, s_uev.v)
        op("tensor", "matmul", zb[:, 0:128], gt[:, 128:256], u_c[:], start=False, stop=False,
           skip_group_check=True, inc=s_cmm)
        for k in range(8):
            op("tensor", "matmul", WP(k), xt[:, k * 128:(k + 1) * 128], u_c[:],
               start=False, stop=(c == NCH - 1), skip_group_check=True,
               inc=(s_upd if k == 7 else None))
        wait("scalar", s_cmm, s_cmm.v)
        if c >= 2:
            wait("scalar", s_yp2[par], YINC * (c // 2))
        op("scalar", "copy", y_c[:], zb[:, 0:128], inc=s_yevac)
        wait("tensor", s_yevac, c + 1)
        wait("tensor", s_vta, s_vta.v)
        op("tensor", "transpose", pbb[6][:, 0:128], y_c[:], ident_b[:], inc=s_ytr)
        wait("scalar", s_ytr, s_ytr.v)
        op("scalar", "copy", yt_c2[par][:], pbb[6][:, 0:128], inc=s_ytev)
        if dbg_c1 is not None and c == 0:
            s_dc1 = S_("s_dc1")
            wait("gpsimd", s_uev, s_uev.v)
            wait("gpsimd", s_wlo, s_wlo.v)
            o = 0
            dma("gpsimd", dbg_c1[:, 0:1024].bitcast(BF16)[:, 0:1024], xf[:], s_dc1)
            dma("gpsimd", dbg_c1[:, 1024:1536].bitcast(BF16)[:, 0:1024], xt[:], s_dc1)
            dma("gpsimd", dbg_c1[:, 1536:1664].bitcast(BF16)[:, 0:256], gt[:], s_dc1)
            dma("gpsimd", dbg_c1[:, 1664:1728].bitcast(BF16)[:, 0:128], d_c[:], s_dc1)
            dma("gpsimd", dbg_c1[:, 1728:1792].bitcast(BF16)[:, 0:128], u_c[:], s_dc1)
            dma("gpsimd", dbg_c1[:, 1792:2304].bitcast(BF16)[:, 0:1024], wbhi[:], s_dc1)
        if dbg_c1 is not None and c == 1:
            s_dc1b = S_("s_dc1b")
            wait("gpsimd", s_wlo, s_wlo.v)
            dma("gpsimd", dbg_c1[:, 2304:2816].bitcast(BF16)[:, 0:1024], wbhi[:], s_dc1b)
        wait("gpsimd", s_ytev, s_ytev.v)
        wait("gpsimd", s_yp2[par], s_yp2[par].v)
        pY = (c * CH) // PTOK
        dma("gpsimd", agin_y[pY].ap()[:, (c * CH) % PTOK: (c * CH) % PTOK + 128], yt_c2[par][:], s_yp2[par])
        if dbg_y is not None:
            dma("gpsimd", dbg_y[:, c * 128:(c + 1) * 128], yt_c2[par][:], s_yp2[par])

    SP_PER_PIECE = PTOK // VSPAN
    CH_PER_PIECE = PTOK // CH
    for p in range(NPIECE):
        for v in range(SP_PER_PIECE):
            emit_vspan(p * SP_PER_PIECE + v)
        for j in range(CH_PER_PIECE):
            emit_chunk(p * CH_PER_PIECE + j)

    if dbg_v is not None:
        s_dv2 = S_("s_dv2")
        wait("gpsimd", s_vta, s_vta.v)
        for _dc in range(0, S, 2048):
            dma("gpsimd", dbg_v[:, _dc:_dc + 2048], A[:, _dc:_dc + 2048], s_dv2)

    s_agy = S_("s_agy")
    for p in range(NPIECE):
        nch_done = (p + 1) * CH_PER_PIECE
        wait("gpsimd", s_yp2[0], YINC * ((nch_done + 1) // 2))
        wait("gpsimd", s_yp2[1], YINC * (nch_done // 2))
        cc("AllGather", ALU.bypass, [agin_y[p].ap()], [agout_y[p]], s_agy)

    if "yag" in debug:
        dbg_yag = dout("dbg_yag", [128, 8 * TCORE], BF16)
        s_dyag = S_("s_dyag")
        myp = (0 * 1)
        i_core_tok0 = 0
        wait("gpsimd", s_agy, s_agy.v)
        for rr in range(8):
            dma("gpsimd", dbg_yag[:, rr * TCORE:(rr + 1) * TCORE],
                agout_y[0][rr * 128:(rr + 1) * 128, 0:TCORE], s_dyag)

    NSPAN = S // VSPAN
    if phases >= 2:
        PCC = PTOK // TCORE
        pid = nc.partition_id()
        s_xr = [S_("s_xr0"), S_("s_xr1")]
        s_rtr = S_("s_rtr")
        s_rev = [S_("s_rev0"), S_("s_rev1")]
        xrv = {}
        wait("tensor", s_uev, NCH)
        for i in range(NTILE):
            buf = xtile[i % 2]
            sx = s_xr[i % 2]
            wait("sync", sx, sx.v)
            if i >= 2:
                wait("sync", s_rtr, xrv[i - 2])
            dma("sync", buf[:], xin[i * 128:(i + 1) * 128, :], sx)
            for kk in range(4):
                wait("tensor", sx, sx.v)
                wait("tensor", s_rev[0], s_rev[0].v)
                op("tensor", "transpose", pb[5][:, 0:128],
                   buf[:, 2 * kk * 128:(2 * kk + 1) * 128], ident_f[:])
                op("tensor", "transpose", pb[5][:, 128:256],
                   buf[:, (2 * kk + 1) * 128:(2 * kk + 2) * 128], ident_f[:], inc=s_rtr)
                wait("scalar", s_rtr, s_rtr.v)
                if i == 0 and kk == 0:
                    wait("scalar", s_dv, NCH)
                    if dbg_v is not None:
                        wait("scalar", B.sems["s_dv2"], B.sems["s_dv2"].v)
                op("scalar", "copy",
                   A[:, 2 * kk * TCORE + i * 128: 2 * kk * TCORE + (i + 1) * 128],
                   pb[5][:, 0:128])
                op("scalar", "copy",
                   A[:, (2 * kk + 1) * TCORE + i * 128: (2 * kk + 1) * TCORE + (i + 1) * 128],
                   pb[5][:, 128:256], inc=s_rev[0])
            xrv[i] = s_rtr.v

        ybuf = [gpub[:, 0:TCORE], gpub[:, TCORE:2 * TCORE]]
        s_yb = [S_("s_yb0"), S_("s_yb1")]
        s_res = S_("s_res")
        ybv = {}
        for k in range(8 if "nop2b" not in debug else 0):
            j = k % 2
            sy = s_yb[j]
            if k == 0:
                wait("gpsimd", s_agy, NPIECE)
                wait("gpsimd", s_gpubd, s_gpubd.v)
            wait("gpsimd", sy, sy.v)
            if k >= 2:
                wait("gpsimd", s_res, ybv[k - 2])
            dma("gpsimd", ybuf[j],
                agout_yall.ap()[bass.ds((pid // PCC) * NCORES * 128 + k * 128, 128),
                                bass.ds((pid % PCC) * TCORE, TCORE)], sy)
            wait("vector", sy, sy.v)
            if k == 0:
                wait("vector", s_rev[0], s_rev[0].v)
                wait("vector", s_rev[1], s_rev[1].v)
            op("vector", "scalar_tensor_tensor", A[:, k * TCORE:(k + 1) * TCORE],
               ybuf[j], modT[:, 16 + k:17 + k], A[:, k * TCORE:(k + 1) * TCORE],
               op0=ALU.mult, op1=ALU.add, inc=s_res)
            ybv[k] = s_res.v
        if dbg_res is not None:
            s_dres = S_("s_dres")
            wait("gpsimd", s_res, 8)
            for _dc in range(0, 8 * TCORE, 2048):
                dma("gpsimd", dbg_res[:, _dc:_dc + 2048], A[:, _dc:_dc + 2048], s_dres)

    if phases >= 2 and "nop0d" not in debug:
        if S >= 16384:
            wq_in = vxf[0].bitcast(F32)[:, 0:D]
            wq_sg = vxf[0].bitcast(F32)[:, D:D + 64]
            wq_bf = vxf[1][:, 0:D]
            wq_st = vxf[1][:, D:D + 256]
            wqdbg = vxf[1][:, 512:768]
        else:
            wq_in = sb("wq_in", [128, D])
            wq_sg = sb("wq_sg", [128, 64])
            wq_bf = sb("wq_bf", [128, D], BF16)
            wq_st = sb("wq_st", [128, 256], BF16)
            wqdbg = sb("wqdbg", [128, 256], BF16)
        s_wql = [S_("s_wql0"), S_("s_wql1")]
        s_wqv = S_("s_wqv")
        s_wqt = S_("s_wqt")
        s_wqe = [S_("s_wqe0"), S_("s_wqe1")]
        s_wqp = [S_("s_wqp0"), S_("s_wqp1")]
        s_agw = S_("s_agw")
        NG = 32
        wq_tiles = []
        for (gsl, usl, dsl) in ((g0s, u0s, d0s), (g1s, u1s, d1s)):
            tl = []
            for t3 in range(3):
                tl.append((gsl[t3 * 128:(t3 + 1) * 128, :], t3 * D * 128))
            for t3 in range(3):
                tl.append((usl[t3 * 128:(t3 + 1) * 128, :], (3 + t3) * D * 128))
            for s3 in range(3):
                tl.append((dsl[:, s3 * D:(s3 + 1) * D], (6 + s3) * D * 128))
            wq_tiles.append(tl)

        wait("tensor", s_ytev, s_ytev.v)
        wait("tensor", s_vta, s_vta.v)
        g32 = lambda ap: ap.rearrange("p (g k) -> p g k", k=32)
        sgb = lambda c0: wq_sg[:, c0:c0 + NG].rearrange("p (g o) -> p g o", o=1).to_broadcast((128, NG, 32))
        tc = 0
        for lsel in range(2):
            for (srcap, pubofs) in wq_tiles[lsel]:
                par = tc % 2
                svl = s_wql[par]
                wait("sync", svl, svl.v)
                wait("sync", s_wqv, s_wqv.v)
                if tc == 0:
                    wait("sync", s_vmm, NSPAN)
                dma("sync", wq_in[:, :], srcap, svl)
                wait("vector", svl, svl.v)
                fop("vector", "tensor_reduce", wq_sg[:, 0:NG].rearrange("p (g o) -> p g o", o=1),
                    g32(wq_in), axis=AX.X, op=ALU.max, apply_absolute_value=True)
                fop("vector", "tensor_scalar", wq_sg[:, 0:NG], wq_sg[:, 0:NG],
                    float(1.0 / 7.0), 1e-12, op0=ALU.mult, op1=ALU.max)
                fop("vector", "reciprocal", wq_sg[:, NG:2 * NG], wq_sg[:, 0:NG])
                fop("vector", "tensor_tensor", g32(wq_in), g32(wq_in), sgb(NG), op=ALU.mult)
                fop("vector", "tensor_scalar", wq_in[:, :], wq_in[:, :], MAGIC, None, op0=ALU.add)
                fop("vector", "tensor_scalar", wq_in[:, :], wq_in[:, :], MAGIC, None, op0=ALU.subtract)
                wait("vector", s_wqt, s_wqt.v)
                op("vector", "tensor_tensor", g32(wq_bf), g32(wq_in), sgb(0), op=ALU.mult, inc=s_wqv)
                for bb in range(4):
                    wait("tensor", s_wqv, s_wqv.v)
                    wait("tensor", s_wqe[0], s_wqe[0].v)
                    wait("tensor", s_wqe[1], s_wqe[1].v)
                    op("tensor", "transpose", pbb[6][:, 0:128],
                       wq_bf[:, 2 * bb * 128:(2 * bb + 1) * 128], ident_b[:])
                    op("tensor", "transpose", pbb[6][:, 128:256],
                       wq_bf[:, (2 * bb + 1) * 128:(2 * bb + 2) * 128], ident_b[:], inc=s_wqt)
                    wait("scalar", s_wqt, s_wqt.v)
                    wait("scalar", s_wqp[0], s_wqp[0].v)
                    wait("scalar", s_wqp[1], s_wqp[1].v)
                    op("scalar", "copy", wq_st[:, 0:128], pbb[6][:, 0:128], inc=s_wqe[0])
                    op("scalar", "copy", wq_st[:, 128:256], pbb[6][:, 128:256], inc=s_wqe[1])
                    for half in range(2):
                        b = 2 * bb + half
                        wait("gpsimd", s_wqe[half], s_wqe[half].v)
                        wait("gpsimd", s_wqp[half], s_wqp[half].v)
                        dma("gpsimd",
                            agin_wdq[lsel].ap()[0:1, pubofs + b * 16384: pubofs + (b + 1) * 16384]
                            .rearrange("o (r q) -> (o r) q", q=128),
                            wq_st[:, half * 128:half * 128 + 128], s_wqp[half])
                tc += 1
            wait("gpsimd", s_wqp[0], s_wqp[0].v)
            wait("gpsimd", s_wqp[1], s_wqp[1].v)
            cc("AllGather", ALU.bypass, [agin_wdq[lsel].ap()], [agout_wdq[lsel].ap()], s_agw)

        if dbg_wq is not None:
            s_dwq = S_("s_dwq")
            s_wqd = S_("s_wqd")
            wait("sync", s_agw, 2)
            dma("sync", wqdbg[:, 0:128],
                agout_wdq[0].ap()[0:1, 0:16384].rearrange("o (p f) -> (o p) f", f=128), s_wqd)
            dma("sync", wqdbg[:, 128:256],
                agout_wdq[0].ap()[0:1, 6 * D * 128: 6 * D * 128 + 16384]
                .rearrange("o (p f) -> (o p) f", f=128), s_wqd)
            wait("gpsimd", s_wqd, s_wqd.v)
            dma("gpsimd", dbg_wq[:, :], wqdbg[:, :], s_dwq)

    if phases >= 3:
        TB = min(512, TCORE)
        NTB = TCORE // TB
        FBLK = FF // 128
        s_b3 = S_("s_b3")

        def emit_armax(j, srcs, psem):
            s_pub = S_("arpub"); s_cc = S_("arcc"); s_rb = S_("arrb")
            s_pv = S_("arpv"); s_rpm = S_("arrpm")
            wait("gpsimd", psem, psem.v)
            for m, sap in enumerate(srcs):
                wait("gpsimd", s_pub, s_pub.v)
                dma("gpsimd", agin_mx[j].ap()[0:1, m:m + 1], sap, s_pub)
            wait("gpsimd", s_pub, s_pub.v)
            cc("AllGather", ALU.bypass, [agin_mx[j].ap()], [agout_mx[j].ap()], s_cc)
            wait("gpsimd", s_cc, s_cc.v)
            for m in range(len(srcs)):
                wait("gpsimd", s_rb, s_rb.v)
                if m > 0:
                    wait("gpsimd", s_pv, s_pv.v)
                dma("gpsimd", mxr[0:1, 0:NCORES],
                    agout_mx[j].ap()[:, m:m + 1].rearrange("r o -> o r"), s_rb,
                    allow_slow_non_contiguous=True)
                wait("vector", s_rb, s_rb.v)
                wait("vector", s_rpm, s_rpm.v)
                fop("vector", "tensor_reduce", mxr[0:1, 8:9],
                    mxr[0:1, 0:NCORES].rearrange("o (g k) -> o g k", g=1),
                    axis=AX.X, op=ALU.max)
                fop("vector", "tensor_scalar", mxr[0:1, 9:10], mxr[0:1, 8:9],
                    float(1.0 / 32767.0), 1e-12, op0=ALU.mult, op1=ALU.max)
                fop("vector", "reciprocal", mxr[0:1, 10:11], mxr[0:1, 9:10])
                op("vector", "tensor_scalar", mxr[0:1, 11:12], mxr[0:1, 9:10],
                   -MAGIC, None, op0=ALU.mult, inc=s_pv)
                cb0 = 6 * j + 3 * m
                wait("tensor", s_pv, s_pv.v)
                wait("tensor", s_b3, s_b3.v)
                op("tensor", "matmul", pb[3][:, 0:1], ones_f[0:1, :], mxr[0:1, 9:10],
                   start=True, stop=True)
                op("tensor", "matmul", pb[3][:, 1:2], ones_f[0:1, :], mxr[0:1, 10:11],
                   start=True, stop=True)
                op("tensor", "matmul", pb[3][:, 2:3], ones_f[0:1, :], mxr[0:1, 11:12],
                   start=True, stop=True, inc=s_rpm)
                wait("scalar", s_rpm, s_rpm.v)
                op("scalar", "copy", qcols[:, cb0:cb0 + 3], pb[3][:, 0:3], inc=s_b3)
            return s_b3, s_b3.v

        def emit_ptmax(srccol, psem, bank):
            s_t = S_("pnxt"); s_vv = S_("pnxv")
            wait("tensor", psem, psem.v)
            wait("tensor", s_b3, s_b3.v)
            op("tensor", "transpose", bank[0:1, 0:128], mxp[:, srccol:srccol + 1],
               ident_f[:], inc=s_t)
            wait("vector", s_t, s_t.v)
            op("vector", "tensor_reduce", mxr[0:1, 12 + srccol:13 + srccol],
               bank[0:1, 0:128].rearrange("o (g k) -> o g k", g=1),
               axis=AX.X, op=ALU.max, inc=s_vv)
            return s_vv, s_vv.v

        def emit_prenorm(ccol, scol, qdq_j, ready):
            s_c = S_("pnc"); s_q = S_("pnq")
            s_m = S_("pnm")
            s_xv = S_("pnxv"); s_xs = S_("pnxs"); s_xt = S_("pnxt")
            s_v = S_("pnv"); s_a = S_("pna")
            for eng, sem, cnt in ready:
                wait(eng, sem, cnt)
            stfree = {}
            for tb in range(NTB):
                for k in range(8):
                    slot = k % 2
                    n = tb * 8 + k
                    rb = A[:, k * TCORE + tb * TB: k * TCORE + tb * TB + TB]
                    sta = xf_c[0][:, slot * TB: slot * TB + TB]
                    stb = xf_c[1][:, slot * TB: slot * TB + TB]
                    if n >= 2:
                        wait("scalar", s_m, stfree[slot])
                        wait("vector", s_m, stfree[slot])
                    op("scalar", "copy", sta, rb, inc=s_c)
                    op("vector", "tensor_tensor", stb, rb, rb, op=ALU.mult, inc=s_q)
                    wait("tensor", s_c, s_c.v)
                    wait("tensor", s_q, s_q.v)
                    if k == 0 and tb > 0:
                        wait("tensor", s_xv, s_xv.v)
                        wait("tensor", s_xs, s_xs.v)
                    op("tensor", "matmul", pb[7][0:1, 0:TB], ones_cb[:, 0:1], sta,
                       start=(k == 0), stop=(k == 7), inc=s_m)
                    op("tensor", "matmul", pb[7][32:33, 0:TB], ones_cb[:, 0:1], stb,
                       start=(k == 0), stop=(k == 7), inc=s_m)
                    stfree[slot] = s_m.v
                r0 = pb[7][0:1, 0:TB]; r32 = pb[7][32:33, 0:TB]
                mu = rows2[0:1, 0:TB]; e2 = rows2[0:1, 512:512 + TB]
                wait("scalar", s_m, s_m.v)
                op("scalar", "copy", mu, r0)
                op("scalar", "copy", e2, r32, inc=s_xs)
                wait("vector", s_xs, s_xs.v)
                fop("vector", "tensor_scalar", mu, mu, float(1.0 / D), None, op0=ALU.mult)
                fop("vector", "tensor_scalar", e2, e2, float(1.0 / D), None, op0=ALU.mult)
                fop("vector", "tensor_tensor", rowsflat[0:1, 0:TB], mu, mu, op=ALU.mult)
                fop("vector", "tensor_tensor", e2, e2, rowsflat[0:1, 0:TB], op=ALU.subtract)
                op("vector", "tensor_scalar", e2, e2, 1e-5, None, op0=ALU.add, inc=s_xv)
                wait("scalar", s_xv, s_xv.v)
                op("scalar", "activation", rowsflat[0:1, 0:TB], e2, AF.Sqrt, inc=s_xs)
                wait("vector", s_xs, s_xs.v)
                fop("vector", "reciprocal", e2, rowsflat[0:1, 0:TB])
                fop("vector", "tensor_tensor", rowsflat[0:1, 512:512 + TB], mu, e2, op=ALU.mult)
                fop("vector", "tensor_scalar", rowsflat[0:1, 512:512 + TB],
                    rowsflat[0:1, 512:512 + TB], -1.0, None, op0=ALU.mult)
                op("vector", "tensor_copy", rowsflat[0:1, 0:TB], e2, inc=s_xv)
                wait("tensor", s_xv, s_xv.v)
                if tb > 0:
                    wait("tensor", s_v, s_v.v)
                op("tensor", "matmul", pb[4][:, 0:TB], ones_f[0:1, :],
                   rowsflat[0:1, 0:TB], start=True, stop=True)
                op("tensor", "matmul", pb[5][:, 0:TB], ones_f[0:1, :],
                   rowsflat[0:1, 512:512 + TB], start=True, stop=True, inc=s_xt)
                t2free = {}
                for k in range(8):
                    slot = k % 2
                    n = tb * 8 + k
                    rb = A[:, k * TCORE + tb * TB: k * TCORE + tb * TB + TB]
                    t2 = xhat[:, slot * TB: slot * TB + TB] if TB == 512 else \
                         xhat[:, slot * TB: slot * TB + TB]
                    wait("vector", s_xt, s_xt.v)
                    if k >= 2:
                        wait("vector", s_a, t2free[slot])
                    fop("vector", "tensor_tensor", t2, rb, pb[4][:, 0:TB], op=ALU.mult)
                    op("vector", "tensor_tensor", t2, t2, pb[5][:, 0:TB], op=ALU.add, inc=s_v)
                    if qdq_j is not None:
                        wait("vector", s_v, s_v.v)
                        if n == 0:
                            fop("vector", "tensor_reduce", mxp[:, 0:1], t2, axis=AX.X,
                                op=ALU.max, apply_absolute_value=True)
                        else:
                            fop("vector", "tensor_reduce", mxp[:, 1:2], t2, axis=AX.X,
                                op=ALU.max, apply_absolute_value=True)
                            fop("vector", "tensor_tensor", mxp[:, 0:1], mxp[:, 0:1],
                                mxp[:, 1:2], op=ALU.max)
                    wait("scalar", s_v, s_v.v)
                    op("scalar", "activation",
                       hfm[:, k * TCORE + tb * TB: k * TCORE + tb * TB + TB],
                       t2, AF.Identity, scale=modT[:, ccol + k:ccol + k + 1],
                       bias=modT[:, scol + k:scol + k + 1], inc=s_a)
                    t2free[slot] = s_a.v
            if qdq_j is None:
                return s_a, s_a.v
            s_mxf = S_("pnxv")
            op("vector", "tensor_copy", mxp[:, 1:2], mxp[:, 0:1], inc=s_mxf)
            psem, _ = emit_ptmax(0, s_mxf, pb[3])
            qsem, qcnt = emit_armax(qdq_j, [mxr[0:1, 12:13]], psem)
            cb0 = 6 * qdq_j
            s_q1 = S_("pnq1"); s_q2 = S_("pnq2")
            hafinal = s_a.v
            qfree = {}
            for tb in range(NTB):
                for k in range(8):
                    slot = (tb * 8 + k) % 2
                    n = tb * 8 + k
                    hb = hfm[:, k * TCORE + tb * TB: k * TCORE + tb * TB + TB]
                    t2 = xhat[:, slot * TB: slot * TB + TB]
                    if n == 0:
                        wait("scalar", qsem, qcnt)
                        wait("scalar", s_a, hafinal)
                    if n >= 2:
                        wait("scalar", s_q2, qfree[slot])
                    op("scalar", "activation", t2, hb, AF.Identity,
                       scale=qcols[:, cb0 + 1:cb0 + 2], bias=np.float32(MAGIC), inc=s_q1)
                    wait("vector", s_q1, s_q1.v)
                    op("vector", "tensor_scalar", hb, t2, MAGIC, qcols[:, cb0:cb0 + 1],
                       op0=ALU.subtract, op1=ALU.mult, inc=s_q2)
                    qfree[slot] = s_q2.v
            return s_q2, s_q2.v

        def emit_ffn(lsel, gbcol, jug, jp2, jp3, s_hq, hqcnt):
            NSB = NTB
            wstage = [carep.bitcast(BF16), sarep.bitcast(BF16)]
            ustg = [wbhi[:, 0:TB], wbhi[:, 512:512 + TB] if TB == 512 else wbhi[:, TB:2 * TB]]
            gstg = [wblo[:, 0:TB], wblo[:, 512:512 + TB] if TB == 512 else wblo[:, TB:2 * TB]]
            s_wl = [S_("fwl0"), S_("fwl1")]
            s_mm = S_("fmm"); s_ev = S_("fev"); s_mx = S_("fmx")
            s_sp = [S_("fsp0"), S_("fsp1")]
            s_sp2 = [S_("ssp0"), S_("ssp1")]
            mmv = {}
            evfree = {}
            wsfree = {}
            spfree = [(s_sp[0].v, s_sp2[0].v), (s_sp[1].v, s_sp2[1].v)]
            MM0 = s_mm.v
            for fb in range(FBLK):
                rank, t3 = fb // 3, fb % 3
                par = fb % 2
                ws = wstage[par]; sw = s_wl[par]
                wait("sync", sw, sw.v)
                if fb == 0:
                    wait("sync", s_agw, lsel + 1)
                    wait("sync", s_mm, MM0)
                if fb >= 2:
                    wait("sync", s_mm, wsfree[fb - 2])
                for q in range(8):
                    dma("sync", ws[:, q * 128:(q + 1) * 128],
                        agout_wdq[lsel].ap()[rank:rank + 1,
                            t3 * D * 128 + q * 16384: t3 * D * 128 + (q + 1) * 16384]
                        .rearrange("o (p f) -> (o p) f", f=128), sw)
                for q in range(8):
                    dma("sync", ws[:, 1024 + q * 128:1024 + (q + 1) * 128],
                        agout_wdq[lsel].ap()[rank:rank + 1,
                            (3 + t3) * D * 128 + q * 16384: (3 + t3) * D * 128 + (q + 1) * 16384]
                        .rearrange("o (p f) -> (o p) f", f=128), sw)
                wait("tensor", sw, sw.v)
                if fb == 0:
                    wait("tensor", s_hq, hqcnt)
                    wait("tensor", s_wqe[0], s_wqe[0].v)
                    wait("tensor", s_wqe[1], s_wqe[1].v)
                for tb in range(NSB):
                    if fb > 0:
                        uv, gv = evfree[(fb - 1, tb)]
                        wait("tensor", s_ev, uv)
                    for q in range(8):
                        op("tensor", "matmul", pb[tb][:, 0:TB],
                           ws[:, 1024 + q * 128:1024 + (q + 1) * 128],
                           hfm[:, q * TCORE + tb * TB: q * TCORE + tb * TB + TB],
                           start=(q == 0), stop=(q == 7))
                    if fb > 0:
                        wait("tensor", s_ev, evfree[(fb - 1, tb)][1])
                    for q in range(8):
                        op("tensor", "matmul", pb[4 + tb][:, 0:TB],
                           ws[:, q * 128:(q + 1) * 128],
                           hfm[:, q * TCORE + tb * TB: q * TCORE + tb * TB + TB],
                           start=(q == 0), stop=(q == 7), inc=(s_mm if q == 7 else None))
                    mmv[(fb, tb)] = s_mm.v
                wsfree[fb] = s_mm.v
                for tb in range(NSB):
                    sl = (fb * NSB + tb) % 2
                    first = (fb == 0 and tb == 0)
                    wait("vector", s_mm, mmv[(fb, tb)])
                    wait("vector", s_sp[sl], spfree[sl][0])
                    wait("vector", s_sp2[sl], spfree[sl][1])
                    op("vector", "tensor_scalar", ustg[sl], pb[tb][:, 0:TB], 1.0, None,
                       op0=ALU.mult, inc=s_ev)
                    u_ev = s_ev.v
                    op("vector", "tensor_reduce", mxp[:, 1:2], pb[tb][:, 0:TB],
                       axis=AX.X, op=ALU.max, apply_absolute_value=True, inc=s_ev)
                    uv = s_ev.v
                    wait("vector", s_ev, s_ev.v)
                    if first:
                        fop("vector", "tensor_copy", mxp[:, 2:3], mxp[:, 1:2])
                    else:
                        fop("vector", "tensor_tensor", mxp[:, 2:3], mxp[:, 2:3],
                            mxp[:, 1:2], op=ALU.max)
                    op("vector", "tensor_scalar", gstg[sl], pb[4 + tb][:, 0:TB], 1.0, None,
                       op0=ALU.mult, inc=s_ev)
                    g_ev = s_ev.v
                    op("vector", "tensor_reduce", mxp[:, 1:2], pb[4 + tb][:, 0:TB],
                       axis=AX.X, op=ALU.max, apply_absolute_value=True, inc=s_ev)
                    gv = s_ev.v
                    wait("vector", s_ev, s_ev.v)
                    if first:
                        fop("vector", "tensor_copy", mxp[:, 3:4], mxp[:, 1:2])
                    else:
                        fop("vector", "tensor_tensor", mxp[:, 3:4], mxp[:, 3:4],
                            mxp[:, 1:2], op=ALU.max)
                    evfree[(fb, tb)] = (uv, gv)
                    wait("gpsimd", s_ev, u_ev)
                    wait("gpsimd", s_sp[sl], s_sp[sl].v)
                    dma("gpsimd", dscr2.ap()[fb * 128:(fb + 1) * 128, tb * TB:tb * TB + TB],
                        ustg[sl], s_sp[sl])
                    wait("scalar", s_ev, g_ev)
                    wait("scalar", s_sp2[sl], s_sp2[sl].v)
                    dma("scalar", dscr1.ap()[fb * 128:(fb + 1) * 128, tb * TB:tb * TB + TB],
                        gstg[sl], s_sp2[sl])
                    spfree[sl] = (s_sp[sl].v, s_sp2[sl].v)
            ALLSP = [s_sp[0].v, s_sp[1].v]
            ALLMM = s_mm.v
            s_mxf = S_("fmx")
            op("vector", "tensor_copy", mxp[:, 1:2], mxp[:, 2:3], inc=s_mxf)
            pA, _ = emit_ptmax(2, s_mxf, pb[3])
            op("vector", "tensor_copy", mxp[:, 1:2], mxp[:, 3:4], inc=s_mxf)
            pB, _ = emit_ptmax(3, s_mxf, pb[2])
            qsem, qcnt = emit_armax(jug, [mxr[0:1, 14:15], mxr[0:1, 15:16]], pB)
            cbu, cbg = 6 * jug, 6 * jug + 3

            gbufs = [w0t_sb.bitcast(BF16)[:, 0:TCORE], gpub[:, 0:TCORE]]
            tbufs = [xt_c[0].bitcast(F32)[:, 0:TB], xt_c[1].bitcast(F32)[:, 0:TB]]
            sgbufs = [xhat[:, 0:TB], xhat[:, 512:512 + TB] if TB == 512 else xhat[:, TB:2 * TB]]
            w32 = wkvvt_sb.bitcast(F32)
            gqbufs = ([vstage[:, 0:TB], w32[:, 0:TB]] if VSPAN >= TB
                      else [w32[:, 0:TB], w32[:, TB:2 * TB]])
            p2stg = [xf_c[0][:, 0:TB], xf_c[0][:, 512:512 + TB] if TB == 512 else xf_c[0][:, TB:2 * TB]]
            s_rl = [S_("frl0"), S_("frl1")]
            s_ts = S_("fts"); s_vq = S_("fvq"); s_pv = S_("fpv")
            s_pw = [S_("fpw0"), S_("fpw1")]
            tbfree = {}; sgfree = {}; gqfree = {}; p2free = {}; gbfree = {}
            pwfree = [s_pw[0].v, s_pw[1].v]
            for fb in range(FBLK):
                par = fb % 2
                gb = gbufs[par]; sg = s_rl[par]
                wait("sync", sg, sg.v)
                if fb == 0:
                    wait("sync", s_sp2[0], s_sp2[0].v)
                    wait("sync", s_sp2[1], s_sp2[1].v)
                    wait("sync", s_res, 8)
                if fb >= 2:
                    wait("sync", s_ts, gbfree[par])
                dma("sync", gb, dscr1.ap()[fb * 128:(fb + 1) * 128, :], sg)
                for sbk in range(NSB):
                    sl = (fb * NSB + sbk) % 2
                    n = fb * NSB + sbk
                    tbf = tbufs[sl]; sgb = sgbufs[sl]; gq = gqbufs[sl]; p2s = p2stg[sl]
                    gblk = gb[:, sbk * TB: sbk * TB + TB]
                    wait("scalar", sg, sg.v)
                    if n == 0:
                        wait("scalar", qsem, qcnt)
                    if n >= 2:
                        wait("scalar", s_vq, tbfree[sl])
                    op("scalar", "activation", tbf, gblk, AF.Identity,
                       scale=qcols[:, cbg + 1:cbg + 2], bias=np.float32(MAGIC), inc=s_ts)
                    wait("scalar", s_ts, s_ts.v)
                    if n >= 2:
                        wait("scalar", s_pv, sgfree[sl])
                    op("scalar", "activation", sgb, tbf, AF.Sigmoid,
                       scale=qcols[:, cbg:cbg + 1], bias=qcols[:, cbg + 2:cbg + 3], inc=s_ts)
                    wait("vector", s_ts, s_ts.v)
                    if n >= 2:
                        wait("vector", s_pv, gqfree[sl])
                    op("vector", "tensor_scalar", gq, tbf, MAGIC, qcols[:, cbg:cbg + 1],
                       op0=ALU.subtract, op1=ALU.mult, inc=s_vq)
                    tbfree[sl] = s_vq.v
                    fop("vector", "tensor_scalar", sgb, sgb, 192.0,
                        float(192.0 + 65535.0 / 65536.0), op0=ALU.add, op1=ALU.min)
                    wait("vector", s_pw[sl], pwfree[sl])
                    wait("vector", s_vq, s_vq.v)
                    op("vector", "scalar_tensor_tensor", p2s, sgb, 192.0, gq,
                       op0=ALU.subtract, op1=ALU.mult, inc=s_pv)
                    sgfree[sl] = s_pv.v
                    gqfree[sl] = s_pv.v
                    wait("vector", s_pv, s_pv.v)
                    op("vector", "tensor_reduce", mxp[:, 1:2], p2s, axis=AX.X,
                       op=ALU.max, apply_absolute_value=True, inc=s_pv)
                    wait("vector", s_pv, s_pv.v)
                    if n == 0:
                        fop("vector", "tensor_copy", mxp[:, 4:5], mxp[:, 1:2])
                    else:
                        fop("vector", "tensor_tensor", mxp[:, 4:5], mxp[:, 4:5],
                            mxp[:, 1:2], op=ALU.max)
                    wait("gpsimd", s_pv, s_pv.v)
                    wait("gpsimd", s_pw[sl], s_pw[sl].v)
                    dma("gpsimd", dscr1.ap()[fb * 128:(fb + 1) * 128, sbk * TB:sbk * TB + TB],
                        p2s, s_pw[sl])
                    pwfree[sl] = s_pw[sl].v
                gbfree[par] = s_ts.v
            P2W = [s_pw[0].v, s_pw[1].v]
            s_mxf2 = S_("fmx")
            op("vector", "tensor_copy", mxp[:, 1:2], mxp[:, 4:5], inc=s_mxf2)
            pC, _ = emit_ptmax(4, s_mxf2, pb[3])
            qsem2, qcnt2 = emit_armax(jp2, [mxr[0:1, 16:17]], pC)
            cbp = 6 * jp2

            upbufs = [carep.bitcast(BF16)[:, 0:TCORE], sarep.bitcast(BF16)[:, 0:TCORE]]
            tbufs2 = [wbhi.bitcast(F32)[:, 0:TB], wblo.bitcast(F32)[:, 0:TB]]
            tb2free = {}; p3free = {}; uqfree = {}; g2free = {}; rlfree = {}
            pw3free = [s_pw[0].v, s_pw[1].v]
            s_ur = [S_("vrl0"), S_("vrl1")]
            for fb in range(FBLK):
                par = fb % 2
                p2b = gbufs[par]; upb = upbufs[par]
                sp = s_rl[par]
                su = s_ur[par]
                wait("sync", sp, sp.v)
                if fb == 0:
                    wait("sync", s_pw[0], P2W[0])
                    wait("sync", s_pw[1], P2W[1])
                if fb >= 2:
                    wait("sync", s_ts, rlfree[par])
                dma("sync", p2b, dscr1.ap()[fb * 128:(fb + 1) * 128, :], sp)
                wait("scalar", su, su.v)
                if fb == 0:
                    wait("scalar", s_sp[0], ALLSP[0])
                    wait("scalar", s_sp[1], ALLSP[1])
                    wait("scalar", s_mm, ALLMM)
                if fb >= 2:
                    wait("scalar", s_ts, rlfree[par])
                dma("scalar", upb, dscr2.ap()[fb * 128:(fb + 1) * 128, :], su)
                for sbk in range(NSB):
                    sl = (fb * NSB + sbk) % 2
                    n = fb * NSB + sbk
                    tbf = tbufs[sl]; tbf2 = tbufs2[sl]
                    g2 = gqbufs[sl]; uq = sgbufs[sl]; p3s = p2stg[sl]
                    wait("scalar", sp, sp.v)
                    wait("scalar", su, su.v)
                    if n == 0:
                        wait("scalar", qsem2, qcnt2)
                        wait("scalar", s_vq, s_vq.v)
                        wait("scalar", s_pv, s_pv.v)
                    if n >= 2:
                        wait("scalar", s_vq, tb2free[sl])
                    op("scalar", "activation", tbf, p2b[:, sbk * TB: sbk * TB + TB],
                       AF.Identity, scale=qcols[:, cbp + 1:cbp + 2],
                       bias=np.float32(MAGIC), inc=s_ts)
                    op("scalar", "activation", tbf2, upb[:, sbk * TB: sbk * TB + TB],
                       AF.Identity, scale=qcols[:, cbu + 1:cbu + 2],
                       bias=np.float32(MAGIC), inc=s_ts)
                    wait("vector", s_ts, s_ts.v)
                    if n >= 2:
                        wait("vector", s_pv, g2free[sl])
                    op("vector", "tensor_scalar", g2, tbf, MAGIC, qcols[:, cbp:cbp + 1],
                       op0=ALU.subtract, op1=ALU.mult, inc=s_vq)
                    op("vector", "tensor_scalar", uq, tbf2, MAGIC, qcols[:, cbu:cbu + 1],
                       op0=ALU.subtract, op1=ALU.mult, inc=s_vq)
                    tb2free[sl] = s_vq.v
                    wait("vector", s_pw[sl], pw3free[sl])
                    wait("vector", s_vq, s_vq.v)
                    op("vector", "tensor_tensor", p3s, g2, uq, op=ALU.mult, inc=s_pv)
                    g2free[sl] = s_pv.v
                    wait("vector", s_pv, s_pv.v)
                    op("vector", "tensor_reduce", mxp[:, 1:2], p3s, axis=AX.X,
                       op=ALU.max, apply_absolute_value=True, inc=s_pv)
                    wait("vector", s_pv, s_pv.v)
                    if n == 0:
                        fop("vector", "tensor_copy", mxp[:, 5:6], mxp[:, 1:2])
                    else:
                        fop("vector", "tensor_tensor", mxp[:, 5:6], mxp[:, 5:6],
                            mxp[:, 1:2], op=ALU.max)
                    wait("gpsimd", s_pv, s_pv.v)
                    wait("gpsimd", s_pw[sl], s_pw[sl].v)
                    dma("gpsimd", dscr2.ap()[fb * 128:(fb + 1) * 128, sbk * TB:sbk * TB + TB],
                        p3s, s_pw[sl])
                    pw3free[sl] = s_pw[sl].v
                rlfree[par] = s_ts.v
            P3W = [s_pw[0].v, s_pw[1].v]
            s_mxf3 = S_("fmx")
            op("vector", "tensor_copy", mxp[:, 1:2], mxp[:, 5:6], inc=s_mxf3)
            pD, _ = emit_ptmax(5, s_mxf3, pb[3])
            qsem3, qcnt3 = emit_armax(jp3, [mxr[0:1, 17:18]], pD)
            cbo = 6 * jp3

            wdstage = [xf_c[0][:, 0:1024], xf_c[1][:, 0:1024]]
            p3rl = [wblo[:, 0:TB], wblo[:, 512:512 + TB] if TB == 512 else wblo[:, TB:2 * TB]]
            obufs = [wbhi[:, 0:TB], wbhi[:, 512:512 + TB] if TB == 512 else wbhi[:, TB:2 * TB]]
            s_dw = [S_("fwl0"), S_("fwl1")]
            s_or = [S_("frl0"), S_("frl1")]
            s_vr = [S_("vrl0"), S_("vrl1")]
            s_dmm = S_("fdmm"); s_fres = S_("ffres")
            dmmv = {}; tqfree = {}; obfree = {}; fresv = {}
            EVFULL = s_ev.v
            VQFULL = s_vq.v
            PVFULL = s_pv.v
            TSFULL = s_ts.v
            for tb in range(NSB):
                for fb in range(FBLK):
                    par = fb % 2
                    n = tb * FBLK + fb
                    wd = wdstage[par]; p3b = p3rl[par]; ob = obufs[par]; tbf = tbufs[par]
                    sw = s_dw[par]; so = s_or[par]
                    wait("sync", sw, sw.v)
                    if n == 0:
                        wait("sync", s_pv, PVFULL)
                    if n >= 2:
                        wait("sync", s_dmm, dmmv[n - 2])
                    for j in range(8):
                        dma("sync", wd[:, j * 128:(j + 1) * 128],
                            agout_wdq[lsel].ap()[j:j + 1,
                                6 * D * 128 + fb * 16384: 6 * D * 128 + (fb + 1) * 16384]
                            .rearrange("o (p f) -> (o p) f", f=128), sw)
                    sv = s_vr[par]
                    wait("scalar", sv, sv.v)
                    if n == 0:
                        wait("scalar", s_pw[0], P3W[0])
                        wait("scalar", s_pw[1], P3W[1])
                        wait("scalar", s_vq, VQFULL)
                        wait("scalar", s_ts, TSFULL)
                    if n >= 2:
                        wait("scalar", s_dmm, dmmv[n - 2])
                        wait("scalar", s_vq, tqfree[par])
                    dma("scalar", p3b,
                        dscr2.ap()[fb * 128:(fb + 1) * 128, tb * TB:tb * TB + TB], sv)
                    wait("scalar", sv, sv.v)
                    if n == 0:
                        wait("scalar", qsem3, qcnt3)
                    if n >= 2:
                        wait("scalar", s_vq, tqfree[par])
                    op("scalar", "activation", tbf, p3b, AF.Identity,
                       scale=qcols[:, cbo + 1:cbo + 2], bias=np.float32(MAGIC), inc=s_ts)
                    wait("vector", s_ts, s_ts.v)
                    if n >= 2:
                        wait("vector", s_dmm, obfree[par])
                    op("vector", "tensor_scalar", ob, tbf, MAGIC, qcols[:, cbo:cbo + 1],
                       op0=ALU.subtract, op1=ALU.mult, inc=s_vq)
                    tqfree[par] = s_vq.v
                    wait("tensor", sw, sw.v)
                    wait("tensor", s_vq, s_vq.v)
                    if fb == 0:
                        if tb == 0:
                            wait("tensor", s_ev, EVFULL)
                        else:
                            wait("tensor", s_fres, fresv[tb - 1])
                    for j in range(8):
                        op("tensor", "matmul", pb[j][:, 0:TB], wd[:, j * 128:(j + 1) * 128],
                           ob, start=(fb == 0), stop=(fb == FBLK - 1),
                           inc=(s_dmm if j == 7 else None))
                    dmmv[n] = s_dmm.v
                    obfree[par] = s_dmm.v
                wait("vector", s_dmm, s_dmm.v)
                for j in range(8):
                    op("vector", "scalar_tensor_tensor",
                       A[:, j * TCORE + tb * TB: j * TCORE + tb * TB + TB],
                       pb[j][:, 0:TB], modT[:, gbcol + j:gbcol + j + 1],
                       A[:, j * TCORE + tb * TB: j * TCORE + tb * TB + TB],
                       op0=ALU.mult, op1=ALU.add, inc=s_fres)
                fresv[tb] = s_fres.v
            return s_fres, s_fres.v

        pn_ready = [("scalar", s_res, 8), ("vector", s_res, 8),
                    ("tensor", s_vev, NSPAN),
                    ("tensor", s_uev, NCH),
                    ("tensor", s_rev[0], s_rev[0].v), ("tensor", s_rev[1], s_rev[1].v),
                    ("scalar", s_hp[0], 16 * (NTILE + 1))]
        if dbg_hfm is not None:
            pn_ready.append(("scalar", B.sems["s_dh"], B.sems["s_dh"].v))
        s_hq0, hq0 = emit_prenorm(32, 24, 0, pn_ready)
        if dbg_h2 is not None:
            s_dbg = S_("s_dbg")
            wait("gpsimd", s_hq0, hq0)
            for _dc in range(0, 8 * TCORE, 2048):
                dma("gpsimd", dbg_h2[:, _dc:_dc + 2048], hfm[:, _dc:_dc + 2048], s_dbg)
        if "s_dbg" in B.sems:
            wait("vector", B.sems["s_dbg"], B.sems["s_dbg"].v)
        s_f0, f0cnt = emit_ffn(0, 40, 1, 2, 3, s_hq0, hq0)
        if dbg_x2 is not None:
            s_dbg = S_("s_dbg")
            wait("gpsimd", s_f0, f0cnt)
            for _dc in range(0, 8 * TCORE, 2048):
                dma("gpsimd", dbg_x2[:, _dc:_dc + 2048], A[:, _dc:_dc + 2048], s_dbg)

    if phases >= 4:
        NKB = TCORE // 128
        HED = 16
        s_at = S_("hat"); s_av = S_("hav"); s_as = S_("has")
        s_ag2 = S_("hag"); s_hl = S_("hlo")
        if "s_dbg" in B.sems:
            wait("vector", B.sems["s_dbg"], B.sems["s_dbg"].v)
        s_qw = [S_("gsp0"), S_("gsp1")]

        pn_ready1 = [("scalar", s_f0, f0cnt), ("vector", s_f0, f0cnt),
                     ("tensor", s_f0, f0cnt),
                     ("scalar", B.sems["fmm"], B.sems["fmm"].v),
                     ("vector", B.sems["fmm"], B.sems["fmm"].v)]
        s_h3, h3cnt = emit_prenorm(56, 48, None, pn_ready1)

        h1b = gpub[0:64, 0:TCORE]
        vbuf = gpub[0:64, 0:TCORE]
        kbuf = tt1.bitcast(BF16)[0:64, 0:TCORE]
        sqb64 = tt1.bitcast(BF16)[0:64, 0:TCORE]
        pbuf = vxf[0]
        kaug = vxf[1][0:65, 0:TCORE]
        qaug = vxf[1][0:65, 4 * VSPAN:4 * VSPAN + TCORE]
        vtokaug = w0t_sb.bitcast(BF16)[:, 0:NKB * 65]
        lamr0 = xhat.bitcast(BF16)[0:1, 0:TCORE]

        wait("sync", s_hl, s_hl.v)
        wait("sync", s_f0, f0cnt)
        wait("sync", B.sems["pnxt"], B.sems["pnxt"].v)
        dma("sync", rows2[0:1, 0:64], wl1t[:, :], s_hl)
        dma("sync", rowsflat[0:1, 0:TCORE], pos[:, :], s_hl)
        wait("scalar", s_hl, s_hl.v)
        wait("scalar", s_f0, f0cnt)
        wait("scalar", B.sems["pna"], B.sems["pna"].v)
        wait("scalar", B.sems["pnq"], B.sems["pnq"].v)
        fop("scalar", "copy", lamr0[0:1, 0:64], rows2[0:1, 0:64])
        fop("scalar", "copy", rows2[0:1, 64:128], lamr0[0:1, 0:64])
        op("scalar", "activation", rowsflat[0:1, 0:TCORE], rowsflat[0:1, 0:TCORE],
           AF.Ln, inc=s_as)
        for qb in range(NTB):
            wait("tensor", s_as, s_as.v)
            if qb == 0:
                wait("tensor", s_f0, f0cnt)
                wait("tensor", B.sems["pnxv"], B.sems["pnxv"].v)
                wait("tensor", B.sems["pnxs"], B.sems["pnxs"].v)
                wait("tensor", B.sems["pnv"], B.sems["pnv"].v)
                wait("tensor", s_b3, s_b3.v)
                wait("tensor", B.sems["s_gld"], B.sems["s_gld"].v)
            op("tensor", "matmul", pb[6][0:64, 0:TB], rows2[0:1, 64:128],
               rowsflat[0:1, qb * TB:qb * TB + TB], start=True, stop=True, inc=s_at)
            wait("scalar", s_at, s_at.v)
            op("scalar", "activation", h1b[:, qb * TB:qb * TB + TB], pb[6][0:64, 0:TB],
               AF.Relu, scale=1.0, bias=bl1c_sb[:, 0:1], inc=s_as)
            wait("tensor", s_as, s_as.v)
            if qb == 0:
                wait("tensor", S_("hag"), 16)
            op("tensor", "matmul", pb[7][0:16, 0:TB], wl2b[:, :],
               h1b[:, qb * TB:qb * TB + TB], start=True, stop=True, inc=s_at)
            wait("scalar", s_at, s_at.v)
            op("scalar", "activation", xhat[0:16, 512:512 + TB] if TB == 512 else
               xhat[0:16, TB:2 * TB], pb[7][0:16, 0:TB],
               AF.Sigmoid, scale=1.0, bias=bl2c_sb[:, 0:1], inc=s_as)
            wait("vector", s_as, s_as.v)
            op("vector", "tensor_scalar", lamsc[:, qb * TB:qb * TB + TB],
               xhat[0:16, 512:512 + TB] if TB == 512 else xhat[0:16, TB:2 * TB],
               float(64 ** -0.5), None, op0=ALU.mult, inc=s_av)
        LAMDONE = s_av.v

        qw = [carep.bitcast(BF16)[:, 0:1024], sarep.bitcast(BF16)[:, 0:1024]]
        qstg = [wbhi[:, 0:TB], wbhi[:, 512:512 + TB] if TB == 512 else wbhi[:, TB:2 * TB]]
        s_qm = S_("fmm"); s_qe = S_("pnv"); s_qs = [S_("fsp0"), S_("fsp1")]
        qmv = {}
        qevv = {}
        for fb3 in range(24 if "noqkv" not in debug else 0):
            par = fb3 % 2
            w = qw[par]; sw = s_qw[par]
            wait("gpsimd", sw, sw.v)
            if fb3 >= 2:
                wait("gpsimd", s_qm, qmv[fb3 - 2])
            for q in range(8):
                dma("gpsimd", w[:, q * 128:(q + 1) * 128],
                    wqkvt[q * 128:(q + 1) * 128, fb3 * 128:(fb3 + 1) * 128], sw)
            wait("tensor", sw, sw.v)
            if fb3 == 0:
                wait("tensor", s_h3, h3cnt)
            for tb in range(NTB):
                if fb3 > 0:
                    wait("tensor", s_qe, qevv.get((fb3 - 1, tb), 0))
                for q in range(8):
                    op("tensor", "matmul", pb[tb][:, 0:TB], w[:, q * 128:(q + 1) * 128],
                       hfm[:, q * TCORE + tb * TB: q * TCORE + tb * TB + TB],
                       start=(q == 0), stop=(q == 7), inc=(s_qm if q == 7 else None))
            qmv[fb3] = s_qm.v
            for tb in range(NTB):
                sl = (fb3 * NTB + tb) % 2
                wait("vector", s_qm, qmv[fb3])
                wait("vector", s_qs[sl], s_qs[sl].v)
                op("vector", "tensor_scalar", qstg[sl], pb[tb][:, 0:TB], 1.0, None,
                   op0=ALU.mult, inc=s_qe)
                qevv[(fb3, tb)] = s_qe.v
                wait("gpsimd", s_qe, s_qe.v)
                wait("gpsimd", s_qs[sl], s_qs[sl].v)
                dma("gpsimd", dscr1.ap()[fb3 * 128:(fb3 + 1) * 128, tb * TB:tb * TB + TB],
                    qstg[sl], s_qs[sl])
        QKVMM = s_qm.v

        klv = {}
        pvrec = {}
        if "noheads" in debug:
            HED = 0
        for h in range(HED):
            wait("sync", s_hl, s_hl.v)
            if h == 0:
                wait("sync", s_qs[0], s_qs[0].v)
                wait("sync", s_qs[1], s_qs[1].v)
                wait("sync", B.sems["fts"], B.sems["fts"].v)
                wait("sync", s_av, LAMDONE)
            else:
                wait("sync", s_av, s_av.v)
                wait("sync", s_at, s_at.v)
            dma("sync", kbuf[0:64, :], dscr1.ap()[1024 + 64 * h:1024 + 64 * (h + 1), :], s_hl)
            dma("sync", qaug[0:64, :], dscr1.ap()[64 * h:64 * (h + 1), :], s_hl)
            dma("sync", vbuf[0:64, :], dscr1.ap()[2048 + 64 * h:2048 + 64 * (h + 1), :], s_hl)
            wait("gpsimd", s_ag2, s_ag2.v)
            wait("gpsimd", s_av, LAMDONE if h == 0 else s_av.v)
            if h > 0:
                wait("gpsimd", s_at, s_at.v)
            dma("gpsimd", lamr0[0:1, 0:TCORE], lamsc[h:h + 1, :], s_ag2)
            if h == 0:
                wait("scalar", s_ag2, s_ag2.v)
                op("scalar", "activation", kaug[64:65, :], lamr0[0:1, 0:TCORE],
                   AF.Identity, scale=np.float32(0.0), bias=np.float32(1.0), inc=s_as)
            for tb in range(NTB):
                wait("tensor", s_ag2, s_ag2.v)
                if h > 0 or tb > 0:
                    wait("tensor", s_av, klv.get("b6", 0))
                op("tensor", "matmul", pb[6][0:64, 0:TB], ones_rb[0:1, 0:64],
                   lamr0[0:1, tb * TB:tb * TB + TB], start=True, stop=True, inc=s_at)
                wait("vector", s_at, s_at.v)
                wait("vector", s_hl, s_hl.v)
                op("vector", "tensor_tensor", kaug[0:64, tb * TB:tb * TB + TB],
                   kbuf[0:64, tb * TB:tb * TB + TB], pb[6][0:64, 0:TB],
                   op=ALU.mult, inc=s_av)
                klv["b6"] = s_av.v
            if "klonly" in debug:
                continue
            if "nomk" not in debug:
                wait("vector", s_av, s_av.v)
                wait("vector", s_at, s_at.v)
                op("vector", "tensor_tensor", sqb64[:, 0:TCORE], kaug[0:64, 0:TCORE],
                   kaug[0:64, 0:TCORE], op=ALU.mult, inc=s_av)
                for qb in range(NTB):
                    wait("tensor", s_av, s_av.v)
                    wait("tensor", s_as, s_as.v)
                    if h == 0 and qb == 0:
                        wait("tensor", B.sems["pnxs"], B.sems["pnxs"].v)
                    op("tensor", "matmul", pb[7][0:1, 0:TB], ones_cb[0:64, 0:1],
                       sqb64[:, qb * TB:qb * TB + TB], start=True, stop=True, inc=s_at)
                    wait("scalar", s_at, s_at.v)
                    if h == 0 and qb == 0:
                        wait("scalar", s_as, s_as.v)
                    op("scalar", "copy", rowsflat[0:1, qb * TB:qb * TB + TB],
                       pb[7][0:1, 0:TB], inc=s_as)
                wait("vector", s_as, s_as.v)
                op("vector", "tensor_reduce", mxr[0:1, 18:19],
                   rowsflat[0:1, 0:TCORE].rearrange("o (g k) -> o g k", g=1),
                   axis=AX.X, op=ALU.max, inc=s_av)
                wait("scalar", s_av, s_av.v)
                op("scalar", "activation", mxr[0:1, 18:19], mxr[0:1, 18:19], AF.Sqrt, inc=s_as)
                wait("vector", s_as, s_as.v)
                op("vector", "tensor_scalar", mxr[0:1, 19:20], mxr[0:1, 18:19], -1.0, None,
                   op0=ALU.mult, inc=s_av)
                wait("vector", s_at, s_at.v)
                op("vector", "tensor_tensor", sqb64[:, 0:TCORE], qaug[0:64, 0:TCORE],
                   qaug[0:64, 0:TCORE], op=ALU.mult, inc=s_av)
                for qb in range(NTB):
                    wait("tensor", s_av, s_av.v)
                    wait("tensor", s_as, s_as.v)
                    op("tensor", "matmul", pb[7][0:1, 0:TB], ones_cb[0:64, 0:1],
                       sqb64[:, qb * TB:qb * TB + TB], start=True, stop=True, inc=s_at)
                    wait("scalar", s_at, s_at.v)
                    op("scalar", "copy", rows2[0:1, qb * TB:qb * TB + TB],
                       pb[7][0:1, 0:TB], inc=s_as)
                wait("scalar", s_as, s_as.v)
                op("scalar", "activation", rows2[0:1, 0:TCORE], rows2[0:1, 0:TCORE],
                   AF.Sqrt, inc=s_as)
                wait("vector", s_as, s_as.v)
                op("vector", "tensor_scalar", rows2[0:1, 0:TCORE], rows2[0:1, 0:TCORE],
                   mxr[0:1, 19:20], 75.0, op0=ALU.mult, op1=ALU.add, inc=s_av)
                wait("scalar", s_av, s_av.v)
                op("scalar", "copy", qaug[64:65, 0:TCORE], rows2[0:1, 0:TCORE], inc=s_as)
            QAUGR = s_as.v
            for kb2 in range(NKB // 2):
                wait("tensor", s_hl, s_hl.v)
                wait("tensor", s_av, klv["b6"])
                wait("tensor", s_as, s_as.v)
                op("tensor", "transpose", pbb[6][:, 0:64],
                   vbuf[:, 2 * kb2 * 128:(2 * kb2 + 1) * 128], ident_b[0:64, 0:64])
                op("tensor", "transpose", pbb[6][:, 64:128],
                   vbuf[:, (2 * kb2 + 1) * 128:(2 * kb2 + 2) * 128], ident_b[0:64, 0:64],
                   inc=s_at)
                wait("scalar", s_at, s_at.v)
                op("scalar", "copy", vtokaug[:, 2 * kb2 * 65:2 * kb2 * 65 + 64],
                   pbb[6][:, 0:64])
                op("scalar", "copy", vtokaug[:, (2 * kb2 + 1) * 65:(2 * kb2 + 1) * 65 + 64],
                   pbb[6][:, 64:128], inc=s_as)
            if "no3d" not in debug:
                op("scalar", "activation",
                   vtokaug.rearrange("p (b c) -> p b c", c=65)[:, :, 64:65],
                   ident_b[:, 0:NKB].rearrange("p (b c) -> p b c", c=1),
                   AF.Identity, scale=np.float32(0.0), bias=np.float32(1.0), inc=s_as)
            VTOKR = s_as.v
            for kb in range(NKB if "noqk" not in debug else 0):
                par = kb % 2
                for qb in range(NTB):
                    wait("tensor", s_as, max(QAUGR, VTOKR))
                    wait("tensor", s_as, s_as.v)
                    wait("tensor", s_av, klv["b6"])
                    if h == 0 and kb == 0 and qb == 0:
                        wait("tensor", s_qe, s_qe.v)
                    op("tensor", "matmul", pb[4 + qb][:, 0:TB],
                       kaug[:, kb * 128:(kb + 1) * 128],
                       qaug[:, qb * TB:qb * TB + TB], start=True, stop=True, inc=s_at)
                    wait("scalar", s_at, s_at.v)
                    if kb >= 2:
                        wait("scalar", s_at, pvrec.get(par, 0))
                    op("scalar", "activation",
                       pbuf[:, par * TCORE + qb * TB: par * TCORE + qb * TB + TB],
                       pb[4 + qb][:, 0:TB], AF.Exp, inc=s_as)
                    wait("tensor", s_as, s_as.v)
                    op("tensor", "matmul", pb[qb][0:65, 0:TB],
                       vtokaug[:, kb * 65:kb * 65 + 65],
                       pbuf[:, par * TCORE + qb * TB: par * TCORE + qb * TB + TB],
                       start=(kb == 0), stop=(kb == NKB - 1),
                       inc=(s_at if qb == NTB - 1 else None))
                pvrec[par] = s_at.v
            for qb in range(NTB if "noqk" not in debug else 0):
                wait("scalar", s_at, s_at.v)
                op("scalar", "copy", rowsflat[0:1, qb * TB:qb * TB + TB],
                   pb[qb][64:65, 0:TB], inc=s_as)
            wait("vector", s_as, s_as.v)
            fop("vector", "tensor_scalar", rowsflat[0:1, 0:TCORE],
                rowsflat[0:1, 0:TCORE], 1e-30, None, op0=ALU.max)
            op("vector", "reciprocal", rows2[0:1, 0:TCORE], rowsflat[0:1, 0:TCORE],
               inc=s_av)
            rrsb = wkvvt_sb.bitcast(F32)[0:64, 0:TB]
            for qb in range(NTB if "noqk" not in debug else 0):
                wait("tensor", s_av, s_av.v)
                op("tensor", "matmul", pb[4 + qb][0:64, 0:TB], ones_f[0:1, 0:64],
                   rows2[0:1, qb * TB:qb * TB + TB], start=True, stop=True, inc=s_at)
                wait("scalar", s_at, s_at.v)
                if h == 0 and qb == 0:
                    wait("scalar", B.sems["fvq"], B.sems["fvq"].v)
                else:
                    wait("scalar", s_av, s_av.v)
                op("scalar", "copy", rrsb, pb[4 + qb][0:64, 0:TB], inc=s_as)
                wait("vector", s_as, s_as.v)
                if h == 0 and qb == 0:
                    wait("vector", s_qm, QKVMM)
                op("vector", "tensor_tensor",
                   hfm[64 * (h % 2):64 * (h % 2) + 64,
                       (h // 2) * TCORE + qb * TB: (h // 2) * TCORE + qb * TB + TB],
                   pb[qb][0:64, 0:TB], rrsb, op=ALU.mult, inc=s_av)
        ATTNV = s_av.v

        s_om = S_("fmm"); s_oev2 = S_("pnv")
        omv = {}
        oevv = {}
        for ob in range(8 if "noheads" not in debug else 0):
            par = ob % 2
            w = qw[par]; sw = s_qw[par]
            wait("gpsimd", sw, sw.v)
            if ob == 0:
                wait("gpsimd", s_qm, QKVMM)
            if ob >= 2:
                wait("gpsimd", s_om, omv[ob - 2])
            for q in range(8):
                dma("gpsimd", w[:, q * 128:(q + 1) * 128],
                    woutt[q * 128:(q + 1) * 128, ob * 128:(ob + 1) * 128], sw)
            wait("tensor", sw, sw.v)
            if ob == 0:
                wait("tensor", s_av, ATTNV)
            for tb in range(NTB):
                if ob > 0:
                    wait("tensor", s_oev2, oevv.get((ob - 1, tb), 0))
                for q in range(8):
                    op("tensor", "matmul", pb[tb][:, 0:TB], w[:, q * 128:(q + 1) * 128],
                       hfm[:, q * TCORE + tb * TB: q * TCORE + tb * TB + TB],
                       start=(q == 0), stop=(q == 7), inc=(s_om if q == 7 else None))
            omv[ob] = s_om.v
            for tb in range(NTB):
                sl = (ob * NTB + tb) % 2
                ab = xhat[:, sl * TB:sl * TB + TB] if TB == 512 else \
                     xhat[:, sl * TB:sl * TB + TB]
                wait("scalar", s_om, omv[ob])
                if ob >= 1:
                    wait("scalar", s_oev2, oevv.get((ob - 1, tb), 0))
                op("scalar", "activation", ab, pb[tb][:, 0:TB], AF.Identity,
                   scale=1.0, bias=boutc_sb[:, ob:ob + 1], inc=s_as)
                wait("vector", s_as, s_as.v)
                op("vector", "scalar_tensor_tensor",
                   A[:, ob * TCORE + tb * TB: ob * TCORE + tb * TB + TB],
                   ab, modT[:, 64 + ob:65 + ob],
                   A[:, ob * TCORE + tb * TB: ob * TCORE + tb * TB + TB],
                   op0=ALU.mult, op1=ALU.add, inc=s_oev2)
                oevv[(ob, tb)] = s_oev2.v
        s_x3, x3cnt = s_oev2, s_oev2.v
        if dbg_x3 is not None:
            s_dbg = S_("s_dbg")
            wait("gpsimd", s_x3, x3cnt)
            for _dc in range(0, 8 * TCORE, 2048):
                dma("gpsimd", dbg_x3[:, _dc:_dc + 2048], A[:, _dc:_dc + 2048], s_dbg)

    if phases >= 5:
        pn_ready2 = [("scalar", s_x3, x3cnt), ("vector", s_x3, x3cnt),
                     ("tensor", s_x3, x3cnt),
                     ("scalar", B.sems["fmm"], B.sems["fmm"].v),
                     ("vector", B.sems["fmm"], B.sems["fmm"].v),
                     ("tensor", B.sems["has"], B.sems["has"].v),
                     ("vector", B.sems["has"], B.sems["has"].v),
                     ("scalar", B.sems["has"], B.sems["has"].v),
                     ("scalar", B.sems["hav"], B.sems["hav"].v),
                     ("scalar", B.sems["hat"], B.sems["hat"].v),
                     ("vector", B.sems["hat"], B.sems["hat"].v),
                     ("vector", B.sems["pnv"], B.sems["pnv"].v)]
        s_hq1, hq1 = emit_prenorm(80, 72, 4, pn_ready2)
        if "s_dbg" in B.sems:
            wait("vector", B.sems["s_dbg"], B.sems["s_dbg"].v)
        s_f1, f1cnt = emit_ffn(1, 88, 5, 6, 7, s_hq1, hq1)

    if phases >= 6:
        s_otr = S_("hat"); s_oe = S_("has"); s_out = S_("s_out")
        outv = {}
        for i in range(NTILE):
            par = i % 2
            ostg = xtile[par]
            for kk in range(4):
                if i == 0 and kk == 0:
                    wait("tensor", s_f1, f1cnt)
                    wait("tensor", B.sems["pnv"], B.sems["pnv"].v)
                wait("tensor", s_oe, s_oe.v)
                op("tensor", "transpose", pb[5][:, 0:128],
                   A[:, 2 * kk * TCORE + i * 128: 2 * kk * TCORE + (i + 1) * 128],
                   ident_f[:])
                op("tensor", "transpose", pb[5][:, 128:256],
                   A[:, (2 * kk + 1) * TCORE + i * 128: (2 * kk + 1) * TCORE + (i + 1) * 128],
                   ident_f[:], inc=s_otr)
                wait("scalar", s_otr, s_otr.v)
                if i >= 2 and kk == 0:
                    wait("scalar", s_out, outv[i - 2])
                op("scalar", "copy", ostg[:, 2 * kk * 128:(2 * kk + 1) * 128],
                   pb[5][:, 0:128])
                op("scalar", "copy", ostg[:, (2 * kk + 1) * 128:(2 * kk + 2) * 128],
                   pb[5][:, 128:256], inc=s_oe)
            wait("gpsimd", s_oe, s_oe.v)
            wait("gpsimd", s_out, s_out.v)
            dma("gpsimd", out_h[i * 128:(i + 1) * 128, :], ostg[:], s_out)
            outv[i] = s_out.v


    for semname in ["s_agy", "s_yp20", "s_yp21", "s_hp0", "s_agw", "s_wqp0", "s_wqp1", "s_res", "s_out", "fsp0", "fsp1", "gsp0", "gsp1"]:
        if semname in B.sems:
            sem = B.sems[semname]
            wait("gpsimd", sem, sem.v)
    for dbgsem in ("s_dmod", "s_dh", "s_dv2", "s_dyag", "s_dres", "s_dwq", "s_dbg"):
        if dbgsem in B.sems:
            wait("gpsimd", B.sems[dbgsem], B.sems[dbgsem].v)

    with nc.Block() as block:
        block.sync(lambda e: B.run("sync", e))
        block.scalar(lambda e: B.run("scalar", e))
        block.vector(lambda e: B.run("vector", e))
        block.tensor(lambda e: B.run("tensor", e))
        block.gpsimd(lambda e: B.run("gpsimd", e))

    return nc, douts


Builder.run = lambda self, name, e: [fn(e) for fn in self.items[name]] and None


def prep_inputs(ins, S=16384):
    TCORE = S // NCORES
    MODS = 6 * D // NCORES
    x = np.asarray(ins["L0_attn_x"], np.float32).reshape(-1, D)[:S]
    W0 = np.asarray(ins["L0_attn_W0"], np.float32)
    Wkv = np.asarray(ins["L0_attn_Wkv"], np.float32)
    bkv = np.asarray(ins["L0_attn_bkv"], np.float32)
    eta = np.asarray(ins["L0_attn_eta"], np.float32).reshape(1, 1)
    c0 = np.asarray(ins["L0_c"], np.float32).reshape(-1)
    wmod0 = np.asarray(ins["L0_wmod"], np.float32)
    c1 = np.asarray(ins["L1_c"], np.float32).reshape(-1)
    wmod1 = np.asarray(ins["L1_wmod"], np.float32)
    g0 = np.asarray(ins["L0_ffn_w_gate"], np.float32)
    u0 = np.asarray(ins["L0_ffn_w_up"], np.float32)
    d0 = np.asarray(ins["L0_ffn_w_down"], np.float32)
    g1 = np.asarray(ins["L1_ffn_w_gate"], np.float32)
    u1 = np.asarray(ins["L1_ffn_w_up"], np.float32)
    d1 = np.asarray(ins["L1_ffn_w_down"], np.float32)
    wqkvt = np.ascontiguousarray(np.asarray(ins["L1_attn_Wqkv"], np.float32).T)
    woutt = np.ascontiguousarray(np.asarray(ins["L1_attn_Wout"], np.float32).T)
    wl1t = np.ascontiguousarray(np.asarray(ins["L1_attn_Wl1"], np.float32).T)
    bl1c = np.ascontiguousarray(np.asarray(ins["L1_attn_bl1"], np.float32).reshape(64, 1))
    wl2t = np.ascontiguousarray(np.asarray(ins["L1_attn_Wl2"], np.float32).T)
    bl2c = np.ascontiguousarray(np.asarray(ins["L1_attn_bl2"], np.float32).reshape(16, 1))
    boutc = np.ascontiguousarray(np.asarray(ins["L1_attn_bout"], np.float32).reshape(8, 128).T)

    import ml_dtypes
    identbf = np.eye(128, dtype=ml_dtypes.bfloat16)
    identf = np.eye(128, dtype=np.float32)
    onesf = np.ones((1, 128), np.float32)
    trium = np.triu(np.ones((128, 128), np.float32), 1)
    onescolbf = np.ones((128, 1), ml_dtypes.bfloat16)

    maps = []
    for i in range(NCORES):
        fo = slice(128 * i, 128 * (i + 1))
        f3 = slice(384 * i, 384 * (i + 1))
        maps.append({
            "xin": np.ascontiguousarray(x[TCORE * i:TCORE * (i + 1)]),
            "w0t": np.ascontiguousarray(W0[fo, :].T),
            "wkvvt": np.ascontiguousarray(Wkv[D:, :][fo, :].T),
            "bkvv": np.ascontiguousarray(bkv[D:][fo].reshape(128, 1)),
            "eta": eta,
            "c0t": np.ascontiguousarray(c0.reshape(8, 128).T),
            "wmod0s": np.ascontiguousarray(wmod0[:, MODS * i:MODS * (i + 1)]),
            "c1t": np.ascontiguousarray(c1.reshape(8, 128).T),
            "wmod1s": np.ascontiguousarray(wmod1[:, MODS * i:MODS * (i + 1)]),
            "g0s": np.ascontiguousarray(g0[f3, :]),
            "u0s": np.ascontiguousarray(u0[f3, :]),
            "d0s": np.ascontiguousarray(d0[fo, :]),
            "g1s": np.ascontiguousarray(g1[f3, :]),
            "u1s": np.ascontiguousarray(u1[f3, :]),
            "d1s": np.ascontiguousarray(d1[fo, :]),
            "wqkvt": wqkvt, "woutt": woutt, "wl1t": wl1t, "bl1c": bl1c,
            "wl2t": wl2t, "bl2c": bl2c, "boutc": boutc,
            "pos": (np.arange(TCORE * i + 1, TCORE * (i + 1) + 1, dtype=np.float32)
                    .reshape(1, TCORE)),
            "identbf": identbf, "identf": identf, "onesf": onesf, "trium": trium,
            "onescolbf": onescolbf,
        })
    return maps


build_p01 = None


if __name__ == "__main__":
    nc, _ = build_full(S=2048, debug=("h", "mod", "y", "v"))
    print("built ok")


def kernel(**inputs):
    S = 16384
    maps = prep_inputs(inputs, S=S)
    nc, _ = build_full(S=S, debug=(), phases=6)
    res = run_bass_kernel_spmd(nc, maps, core_ids=list(range(NCORES)))
    out = np.concatenate([np.asarray(res.results[i]["out"], np.float32)
                          for i in range(NCORES)], axis=0)
    return out.reshape(1, S, 1024)


# version 07
# speedup vs baseline: 1.0421x; 1.0267x over previous
import numpy as np
import concourse.bass as bass
import concourse.mybir as mybir
from concourse.bass_utils import run_bass_kernel_spmd

F32 = mybir.dt.float32
BF16 = mybir.dt.bfloat16
AF = mybir.ActivationFunctionType
ALU = mybir.AluOpType
AX = mybir.AxisListType

NCORES = 8
D = 1024
CH = 128
MAGIC = float(np.float32(12582912.0))
ENGINES = ("sync", "scalar", "vector", "tensor", "gpsimd")


class Sem:
    def __init__(self, nc, name):
        self.h = nc.alloc_semaphore(name)
        self.v = 0
    def inc(self, k=1):
        self.v += k
        return self.v


class Builder:
    def __init__(self, nc):
        self.nc = nc
        self.items = {e: [] for e in ENGINES}
        self.sems = {}
    ALIASES = {
        "s_ld2": "s_ld", "s_prep2": "s_prep", "s_mod2": "s_mod", "s_mev2": "s_mev",
        "s_mpub2": "s_mpub", "s_ccm2": "s_ccm", "s_mld2": "s_mld", "s_gab2": "s_gab",
        "s_adv2": "s_adv",
        "s_cmm": "s_umm", "s_ytr": "s_vtr",
        "fev": "pnv", "fdmm": "fmm", "ffres": "s_res", "fpw0": "fsp0", "fpw1": "fsp1",
        "frl0": "fwl0", "frl1": "fwl1", "arpub": "arA", "arrb": "arA",
        "arpv": "pnxv", "arrpm": "pnxt", "pnq1": "pnc", "pnq2": "pnq",
        "pmx": "pnxt", "fmx": "pnxv",
        "gsp0": "fsp0", "gsp1": "fsp1", "hag": "fsp0", "hlo": "fwl0",
    }
    def S(self, name):
        name = self.ALIASES.get(name, name)
        if name not in self.sems:
            self.sems[name] = Sem(self.nc, name)
        return self.sems[name]
    def emit(self, eng, fn):
        self.items[eng].append(fn)
    def dma(self, eng, out, in_, sem, **kw):
        sem.inc(16)
        self.emit(eng, lambda e, out=out, in_=in_, h=sem.h, kw=kw:
                  e.dma_start(out=out, in_=in_, **kw).then_inc(h, 16))
        return sem.v
    def wait(self, eng, sem, val=None):
        v = sem.v if val is None else val
        if v > 0:
            self.emit(eng, lambda e, h=sem.h, v=v: e.wait_ge(h, v))
    def op(self, eng, method, *args, inc=None, **kw):
        if inc is not None:
            inc.inc(1)
        def fn(e, method=method, args=args, kw=kw, inc=inc):
            r = getattr(e, method)(*args, **kw)
            if inc is not None:
                r.then_inc(inc.h, 1)
        self.emit(eng, fn)
        return None if inc is None else inc.v
    def fop(self, eng, method, *args, **kw):
        f = self.S("fence_" + eng)
        self.op(eng, method, *args, inc=f, **kw)
        self.wait(eng, f, f.v)
    def cc(self, kind, op_, ins_, outs_, sem):
        sem.inc(1)
        self.emit("gpsimd", lambda e, kind=kind, op_=op_, i=ins_, o=outs_, h=sem.h:
                  e.collective_compute(kind, op_, replica_groups=[list(range(NCORES))],
                                       ins=i, outs=o).then_inc(h, 1))
        return sem.v


def build_full(S=16384, debug=(), phases=6):
    FF = 3072
    TCORE = S // NCORES
    NTILE = TCORE // 128
    NCH = S // CH
    OWNCH = NTILE
    NPIECE = min(4, TCORE // 128)
    PTOK = S // NPIECE
    PCORE = TCORE // NPIECE
    PTILE = NTILE // NPIECE
    VSPAN = min(512, PCORE)
    MODS = 6 * D // NCORES
    MFS = MODS // 128

    nc = bass.Bass("TRN2", target_bir_lowering=False)
    B = Builder(nc)
    S_, dma, wait, op, cc = B.S, B.dma, B.wait, B.op, B.cc
    fop = B.fop

    def din(name, shape, dt=F32):
        return nc.dram_tensor(name, shape, dt, kind="ExternalInput").ap()
    xin = din("xin", [TCORE, D])
    w0t = din("w0t", [D, 128])
    wkvvt = din("wkvvt", [D, 128])
    bkvv = din("bkvv", [128, 1])
    eta_h = din("eta", [1, 1])
    c0t = din("c0t", [128, 8])
    wmod0s = din("wmod0s", [D, MODS])
    c1t = din("c1t", [128, 8])
    wmod1s = din("wmod1s", [D, MODS])
    g0s = din("g0s", [3 * 128, D])
    u0s = din("u0s", [3 * 128, D])
    d0s = din("d0s", [128, 3 * D])
    g1s = din("g1s", [3 * 128, D])
    u1s = din("u1s", [3 * 128, D])
    d1s = din("d1s", [128, 3 * D])
    wqkvt = din("wqkvt", [D, 3 * D])
    woutt = din("woutt", [D, D])
    wl1t = din("wl1t", [1, 64])
    bl1c = din("bl1c", [64, 1])
    wl2t = din("wl2t", [64, 16])
    bl2c = din("bl2c", [16, 1])
    pos = din("pos", [1, TCORE])
    boutc = din("boutc", [128, 8])
    onescolbf = din("onescolbf", [128, 1], BF16)
    identbf = din("identbf", [128, 128], BF16)
    identf = din("identf", [128, 128], F32)
    onesf = din("onesf", [1, 128], F32)
    trium = din("trium", [128, 128], F32)

    douts = {}
    def dout(name, shape, dt=F32):
        douts[name] = nc.dram_tensor(name, shape, dt, kind="ExternalOutput").ap()
        return douts[name]
    dbg_y = dout("dbg_y", [128, S], BF16) if "y" in debug else None
    dbg_hfm = dout("dbg_hfm", [128, 8 * TCORE], BF16) if "h" in debug else None
    dbg_mod = dout("dbg_mod", [128, 48], F32) if "mod" in debug else None
    dbg_v = dout("dbg_v", [128, S], F32) if "v" in debug else None
    dbg_c1 = dout("dbg_c1", [128, 8 * 128 + 1024 + 256 + 128 + 128 + 1024 + 512], F32) if "c1" in debug else None
    dbg_res = dout("dbg_res", [128, 8 * TCORE], F32) if "res" in debug else None
    dbg_wq = dout("dbg_wq", [128, 256], BF16) if "wq" in debug else None
    dbg_h2 = dout("dbg_h2", [128, 8 * TCORE], BF16) if "h2" in debug else None
    dbg_x2 = dout("dbg_x2", [128, 8 * TCORE], F32) if "x2" in debug else None
    dbg_x3 = dout("dbg_x3", [128, 8 * TCORE], F32) if "x3" in debug else None

    agin_mod = nc.dram_tensor("agin_mod", [1, MODS], F32)
    agout_mod = nc.dram_tensor("agout_mod", [NCORES, MODS], F32, addr_space="Shared")
    agin_h = nc.dram_tensor("agin_h", [1, D * TCORE * 2], BF16)
    agout_h = nc.dram_tensor("agout_h", [NCORES, D * TCORE * 2], BF16, addr_space="Shared")
    agin_g = nc.dram_tensor("agin_g", [1, OWNCH * 2 * 128 * 128], BF16)
    agout_g = nc.dram_tensor("agout_g", [NCORES, OWNCH * 2 * 128 * 128], BF16, addr_space="Shared")
    agin_y = [nc.dram_tensor(f"agin_y{p}", [128, PTOK], BF16) for p in range(NPIECE)]
    agout_yall = nc.dram_tensor("agout_yall", [NPIECE * NCORES * 128, PTOK], BF16, addr_space="Shared")
    agout_y = [agout_yall.ap()[p * NCORES * 128:(p + 1) * NCORES * 128, :] for p in range(NPIECE)]
    WDQN = 3 * FF * 128
    agin_wdq = [nc.dram_tensor(f"agin_wdq{l}", [1, WDQN], BF16) for l in range(2)]
    agout_wdq = [nc.dram_tensor(f"agout_wdq{l}", [NCORES, WDQN], BF16, addr_space="Shared") for l in range(2)]
    MXW = [1, 2, 1, 1, 1, 2, 1, 1]
    agin_mx = [nc.dram_tensor(f"agin_mx{j}", [1, MXW[j]], F32) for j in range(8)]
    agout_mx = [nc.dram_tensor(f"agout_mx{j}", [NCORES, MXW[j]], F32, addr_space="Shared") for j in range(8)]
    dscr1 = nc.dram_tensor("dscr1", [FF, TCORE], BF16)
    dscr2 = nc.dram_tensor("dscr2", [FF, TCORE], BF16)
    out_h = nc.dram_tensor("out", [TCORE, D], F32, kind="ExternalOutput").ap() if phases >= 6 else None

    sb = lambda name, shape, dt=F32: nc.alloc_sbuf_tensor(name, shape, dt).ap()
    A = sb("arenaA", [128, S], F32)
    hfm = sb("hfm", [128, 8 * TCORE], BF16)
    if S >= 16384:
        htok = A.bitcast(BF16)[:, 2 * MODS * 8:2 * MODS * 8 + NTILE * D]
    else:
        htok = sb("htok", [128, NTILE * D], BF16)
    xtile = [sb(f"xtile{j}", [128, D]) for j in range(2)]
    xhat = sb("xhat", [128, D])
    tt1 = sb("tt1", [128, D])
    carep = sb("carep", [128, D])
    sarep = sb("sarep", [128, D])
    rows8 = sb("rows8", [64, 128])
    rowsflat = sb("rowsflat", [1, 16 * 128])
    stats = sb("stats", [128, 8 * NTILE])
    modT = sb("modT", [128, 96])
    csc = sb("csc", [128, 16])
    wmods = A[:, 0:8 * MODS] if S >= 16384 else sb("wmods", [128, 8 * MODS])
    cval = sb("cval", [128, 1])
    ngc = sb("ngc", [128, 1])
    ident_b = sb("ident_b", [128, 128], BF16)
    ones_cb = sb("ones_cb", [128, 1], BF16)
    ones_rb = sb("ones_rb", [1, 128], BF16)
    boutc_sb = sb("boutc_sb", [128, 8])
    bl1c_sb = sb("bl1c_sb", [64, 1])
    bl2c_sb = sb("bl2c_sb", [16, 1])
    wl2b = sb("wl2b", [64, 16], BF16)
    lamsc = sb("lamsc", [16, TCORE], BF16)
    rows2 = sb("rows2", [1, max(TCORE, 1024)])
    mxp = sb("mxp", [128, 8])
    mxr = sb("mxr", [1, 24])
    qcols = sb("qcols", [128, 48])
    ident_f = sb("ident_f", [128, 128], F32)
    ones_f = sb("ones_f", [1, 128], F32)
    triu_f = sb("triu_f", [128, 128], F32)
    w0t_sb = sb("w0t_sb", [128, 8 * 128])
    wkvvt_sb = sb("wkvvt_sb", [128, 8 * 128], BF16)
    bkvv_sb = sb("bkvv_sb", [128, 1])
    gup = sb("gup", [128, 128])
    nlo = sb("nlo", [128, 128], BF16)
    nup = sb("nup", [128, 128], BF16)
    ilq = sb("ilq", [128, 128], BF16)
    rbuf = sb("rbuf", [128, 128], BF16)
    gpub = sb("gpub", [128, OWNCH * 256], BF16)
    wbhi = sb("wbhi", [128, 8 * 128], BF16)
    wblo = sb("wblo", [128, 8 * 128], BF16)
    xf_c = [sb(f"xf_c{j}", [128, 8 * 128], BF16) for j in range(2)]
    xt_c = [sb(f"xt_c{j}", [128, D], BF16) for j in range(2)]
    gt_c = [sb(f"gt_c{j}", [128, 2 * 128], BF16) for j in range(2)]
    vxf = [sb(f"vxf{j}", [128, 8 * VSPAN], BF16) for j in range(2)]
    d_c = sb("d_c", [128, 128], BF16)
    u_c = sb("u_c", [128, 128], BF16)
    y_c = sb("y_c", [128, 128], BF16)
    yt_c2 = [sb(f"yt_c2{j}", [128, 128], BF16) for j in range(2)]
    vstage = sb("vstage", [128, VSPAN])

    pb = [nc.alloc_psum_tensor(f"pb{k}", [128, 512], F32).ap() for k in range(8)]
    pbb = [p.bitcast(BF16) for p in pb]
    def WP(k):
        return pb[k // 4][:, (k % 4) * 128:(k % 4) * 128 + 128]

    s_ld = S_("s_ld")
    dma("sync", ident_b[:], identbf[:, :], s_ld)
    dma("sync", ident_f[:], identf[:, :], s_ld)
    dma("sync", ones_f[:], onesf[:, :], s_ld)
    dma("sync", triu_f[:], trium[:, :], s_ld)
    dma("sync", w0t_sb[:].rearrange("p (k f) -> p k f", f=128), w0t.rearrange("(k p) f -> p k f", p=128), s_ld)
    s_gld = S_("s_gld")
    dma("gpsimd", wkvvt_sb[:].rearrange("p (k f) -> p k f", f=128), wkvvt.rearrange("(k p) f -> p k f", p=128), s_gld)
    dma("sync", bkvv_sb[:], bkvv[:, :], s_ld)
    dma("sync", ones_cb[:], onescolbf[:, :], s_ld)
    dma("sync", boutc_sb[:], boutc[:, :], s_ld)
    dma("sync", bl1c_sb[:], bl1c[:, :], s_ld)
    dma("sync", bl2c_sb[:], bl2c[:, :], s_ld)
    s_gld2 = S_("hag")
    dma("gpsimd", wl2b[:], wl2t[:, :], s_gld2)
    dma("sync", csc[:, 0:8], c0t[:, :], s_ld)
    dma("sync", wmods[:].rearrange("p (k f) -> p k f", f=768), wmod0s.rearrange("(k p) f -> p k f", p=128), s_ld)
    dma("sync", cval[0:1, 0:1], eta_h[:, :], s_ld)
    LD = s_ld.v

    s_prep = S_("s_prep")
    wait("scalar", s_ld, LD)
    op("scalar", "mul", cval[0:1, 0:1], cval[0:1, 0:1], float(2.0 / D))
    op("scalar", "copy", ones_rb[:], ones_f[:])
    op("scalar", "activation", csc[:, 0:8], csc[:, 0:8], AF.Silu, inc=s_prep)

    s_cv = S_("s_cv")
    wait("tensor", s_prep)
    op("tensor", "matmul", pb[7][:, 0:1], ones_f[0:1, :], cval[0:1, 0:1], start=True, stop=True, inc=s_cv)
    s_cv2 = S_("s_cv2")
    wait("scalar", s_cv)
    op("scalar", "copy", cval[:, 0:1], pb[7][:, 0:1], inc=s_cv2)
    wait("scalar", s_cv2, s_cv2.v)
    op("scalar", "mul", ngc[:, 0:1], cval[:, 0:1], -1.0, inc=s_cv2)
    CV2 = s_cv2.v

    s_mod = S_("s_mod")
    wait("tensor", s_cv2, CV2)
    for mf in range(MFS):
        for hc in range(8):
            op("tensor", "matmul", pb[7][:, 128 + mf:129 + mf],
               wmods[:, hc * MODS + mf * 128: hc * MODS + mf * 128 + 128],
               csc[:, hc:hc + 1],
               start=(hc == 0), stop=(hc == 7), inc=(s_mod if hc == 7 else None))
    s_mev = S_("s_mev")
    wait("scalar", s_mod, MFS)
    op("scalar", "copy", modT[:, 0:MFS], pb[7][:, 128:128 + MFS], inc=s_mev)
    s_mpub = S_("s_mpub")
    wait("gpsimd", s_mev)
    dma("gpsimd", agin_mod.ap().rearrange("o (p k) -> p (o k)", p=128), modT[:, 0:MFS], s_mpub)
    wait("gpsimd", s_mpub)
    s_ccm = S_("s_ccm")
    cc("AllGather", ALU.bypass, [agin_mod.ap()], [agout_mod.ap()], s_ccm)
    wait("gpsimd", s_ccm)
    s_mld = S_("s_mld")
    dma("gpsimd", modT[:, 0:48].rearrange("p (r k) -> p r k", k=MFS),
        agout_mod.ap().rearrange("r (p k) -> p r k", p=128), s_mld)
    MLD = s_mld.v

    s_adv = S_("s_adv")
    wait("vector", s_mld, MLD)
    op("vector", "tensor_scalar", modT[:, 8:16], modT[:, 8:16], 1.0, None, op0=ALU.add)
    op("vector", "tensor_scalar", modT[:, 32:40], modT[:, 32:40], 1.0, None, op0=ALU.add, inc=s_adv)
    s_gab = S_("s_gab")
    wait("scalar", s_mld, MLD)
    op("scalar", "activation", tt1[:, 0:8], modT[:, 16:24], AF.Abs)
    op("scalar", "activation", tt1[:, 8:16], modT[:, 16:24], AF.Sign)
    op("scalar", "activation", tt1[:, 16:24], modT[:, 40:48], AF.Abs)
    op("scalar", "activation", tt1[:, 24:32], modT[:, 40:48], AF.Sign, inc=s_gab)
    wait("vector", s_gab)
    fop("vector", "tensor_scalar", tt1[:, 0:8], tt1[:, 0:8], 0.25, None, op0=ALU.max)
    fop("vector", "tensor_scalar", tt1[:, 16:24], tt1[:, 16:24], 0.25, None, op0=ALU.max)
    op("vector", "tensor_tensor", modT[:, 16:24], tt1[:, 0:8], tt1[:, 8:16], op=ALU.mult)
    op("vector", "tensor_tensor", modT[:, 40:48], tt1[:, 16:24], tt1[:, 24:32], op=ALU.mult, inc=s_adv)
    ADV = s_adv.v
    if dbg_mod is not None:
        s_dmod = S_("s_dmod")
        wait("gpsimd", s_adv, ADV)
        dma("gpsimd", dbg_mod[:, :], modT[:, 0:48], s_dmod)

    agin_mod2 = nc.dram_tensor("agin_mod2", [1, MODS], F32)
    agout_mod2 = nc.dram_tensor("agout_mod2", [NCORES, MODS], F32, addr_space="Shared")
    s_ld2 = S_("s_ld2")
    wait("sync", s_mod, MFS)
    dma("sync", wmods[:].rearrange("p (k f) -> p k f", f=MODS), wmod1s.rearrange("(k p) f -> p k f", p=128), s_ld2)
    dma("sync", csc[:, 8:16], c1t[:, :], s_ld2)
    s_prep2 = S_("s_prep2")
    wait("scalar", s_ld2, s_ld2.v)
    op("scalar", "activation", csc[:, 8:16], csc[:, 8:16], AF.Silu, inc=s_prep2)
    s_mod2 = S_("s_mod2")
    wait("tensor", s_prep2, s_prep2.v)
    wait("tensor", s_mev, 1)
    for mf in range(MFS):
        for hc in range(8):
            op("tensor", "matmul", pb[7][:, 128 + mf:129 + mf],
               wmods[:, hc * MODS + mf * 128: hc * MODS + mf * 128 + 128],
               csc[:, 8 + hc:9 + hc],
               start=(hc == 0), stop=(hc == 7), inc=(s_mod2 if hc == 7 else None))
    s_mev2 = S_("s_mev2")
    wait("scalar", s_mod2, s_mod2.v)
    op("scalar", "copy", modT[:, 48:48 + MFS], pb[7][:, 128:128 + MFS], inc=s_mev2)
    s_mpub2 = S_("s_mpub2")
    wait("gpsimd", s_mev2)
    dma("gpsimd", agin_mod2.ap().rearrange("o (p k) -> p (o k)", p=128), modT[:, 48:48 + MFS], s_mpub2)
    wait("gpsimd", s_mpub2)
    s_ccm2 = S_("s_ccm2")
    cc("AllGather", ALU.bypass, [agin_mod2.ap()], [agout_mod2.ap()], s_ccm2)
    wait("gpsimd", s_ccm2)
    s_mld2 = S_("s_mld2")
    dma("gpsimd", modT[:, 48:96].rearrange("p (r k) -> p r k", k=MFS),
        agout_mod2.ap().rearrange("r (p k) -> p r k", p=128), s_mld2)
    s_adv2 = S_("s_adv2")
    wait("vector", s_mld2, s_mld2.v)
    op("vector", "tensor_scalar", modT[:, 56:64], modT[:, 56:64], 1.0, None, op0=ALU.add)
    op("vector", "tensor_scalar", modT[:, 80:88], modT[:, 80:88], 1.0, None, op0=ALU.add, inc=s_adv2)
    s_gab2 = S_("s_gab2")
    wait("scalar", s_mld2, s_mld2.v)
    wait("scalar", s_adv, ADV)
    op("scalar", "activation", tt1[:, 32:40], modT[:, 64:72], AF.Abs)
    op("scalar", "activation", tt1[:, 40:48], modT[:, 64:72], AF.Sign)
    op("scalar", "activation", tt1[:, 48:56], modT[:, 88:96], AF.Abs)
    op("scalar", "activation", tt1[:, 56:64], modT[:, 88:96], AF.Sign, inc=s_gab2)
    wait("vector", s_gab2, s_gab2.v)
    fop("vector", "tensor_scalar", tt1[:, 32:40], tt1[:, 32:40], 0.25, None, op0=ALU.max)
    fop("vector", "tensor_scalar", tt1[:, 48:56], tt1[:, 48:56], 0.25, None, op0=ALU.max)
    op("vector", "tensor_tensor", modT[:, 64:72], tt1[:, 32:40], tt1[:, 40:48], op=ALU.mult)
    op("vector", "tensor_tensor", modT[:, 88:96], tt1[:, 48:56], tt1[:, 56:64], op=ALU.mult, inc=s_adv2)
    ADV2 = s_adv2.v

    s_tr = S_("s_tr")
    wait("tensor", s_adv, ADV)
    op("tensor", "transpose", pb[6][0:8, 0:128], modT[:, 8:16], ident_f[:])
    op("tensor", "transpose", pb[6][0:8, 128:256], modT[:, 0:8], ident_f[:], inc=s_tr)
    s_tr2 = S_("s_tr2")
    wait("scalar", s_tr)
    op("scalar", "copy", rows8[0:8, 0:128], pb[6][0:8, 0:128])
    op("scalar", "copy", rows8[32:40, 0:128], pb[6][0:8, 128:256], inc=s_tr2)
    s_rflat = S_("s_rflat")
    wait("gpsimd", s_tr2)
    dma("gpsimd", rowsflat[0:1, 0:8 * 128], rows8[0:8, :], s_rflat)
    dma("gpsimd", rowsflat[0:1, 8 * 128:16 * 128], rows8[32:40, :], s_rflat)
    s_rep = S_("s_rep")
    s_repev = S_("s_repev")
    wait("tensor", s_rflat)
    wait("tensor", s_mev, 1)
    for k in range(8):
        if k > 0:
            wait("tensor", s_repev, s_repev.v)
        op("tensor", "matmul", pb[7][:, 0:128], ones_f[0:1, :], rowsflat[0:1, k * 128:(k + 1) * 128], start=True, stop=True)
        op("tensor", "matmul", pb[7][:, 128:256], ones_f[0:1, :], rowsflat[0:1, (8 + k) * 128:(9 + k) * 128], start=True, stop=True, inc=s_rep)
        wait("scalar", s_rep, s_rep.v)
        op("scalar", "copy", carep[:, k * 128:k * 128 + 128], pb[7][:, 0:128])
        op("scalar", "copy", sarep[:, k * 128:k * 128 + 128], pb[7][:, 128:256], inc=s_repev)
    REPDONE = s_repev.v

    s_hst = S_("s_hst")
    s_hsc = S_("s_hsc")
    s_hv = S_("s_hv")
    s_htr = S_("s_htr")
    s_hev = S_("s_hev")
    s_gmm = S_("s_gmm")
    s_ovec = S_("s_ovec")
    s_osq = S_("s_osq")
    s_oev = S_("s_oev")
    s_hp = [S_("s_hp0")]
    s_gpubd = S_("s_gpubd")
    s_agdone = S_("s_agdone")

    s_xp = [S_("s_xp0"), S_("s_xp1")]
    dma("sync", xtile[0][:], xin[0:128, :], s_xp[0])
    for i in range(NTILE):
        buf = xtile[i % 2]
        if i + 1 < NTILE:
            if i >= 1:
                wait("sync", s_hsc, s_hsc.v)
                wait("sync", s_xp[(i + 1) % 2], s_xp[(i + 1) % 2].v)
            dma("sync", xtile[(i + 1) % 2][:], xin[(i + 1) * 128:(i + 2) * 128, :], s_xp[(i + 1) % 2])
        sc = stats[:, i * 8:(i + 1) * 8]
        wait("vector", s_xp[i % 2], 16 * (i // 2 + 1))
        if i == 0:
            wait("vector", s_repev, REPDONE)
            wait("vector", s_adv2, ADV2)
        op("vector", "tensor_reduce", sc[:, 0:1], buf[:], axis=AX.X, op=ALU.add)
        fop("vector", "tensor_tensor", tt1[:], buf[:], buf[:], op=ALU.mult)
        fop("vector", "tensor_reduce", sc[:, 1:2], tt1[:], axis=AX.X, op=ALU.add)
        fop("vector", "tensor_scalar", sc[:, 2:3], sc[:, 0:1], float(1.0 / D), None, op0=ALU.mult)
        fop("vector", "tensor_scalar", sc[:, 3:4], sc[:, 1:2], float(1.0 / D), None, op0=ALU.mult)
        fop("vector", "tensor_tensor", sc[:, 4:5], sc[:, 2:3], sc[:, 2:3], op=ALU.mult)
        fop("vector", "tensor_tensor", sc[:, 4:5], sc[:, 3:4], sc[:, 4:5], op=ALU.subtract)
        fop("vector", "tensor_scalar", sc[:, 4:5], sc[:, 4:5], 1e-5, None, op0=ALU.add)
        op("vector", "reciprocal", sc[:, 5:6], sc[:, 4:5], inc=s_hst)
        wait("scalar", s_hst, s_hst.v)
        op("scalar", "activation", sc[:, 6:7], sc[:, 5:6], AF.Sqrt, inc=s_hsc)
        wait("vector", s_hsc, s_hsc.v)
        fop("vector", "tensor_tensor", sc[:, 7:8], sc[:, 2:3], sc[:, 6:7], op=ALU.mult)
        op("vector", "tensor_scalar", sc[:, 7:8], sc[:, 7:8], -1.0, None, op0=ALU.mult, inc=s_hst)
        wait("scalar", s_hst, s_hst.v)
        op("scalar", "activation", xhat[:], buf[:], AF.Identity, scale=sc[:, 6:7], bias=sc[:, 7:8], inc=s_hsc)
        wait("vector", s_hsc, s_hsc.v)
        fop("vector", "tensor_tensor", tt1[:], xhat[:], carep[:], op=ALU.mult)
        op("vector", "tensor_tensor", htok[:, i * D:(i + 1) * D], tt1[:], sarep[:], op=ALU.add, inc=s_hv)
        wait("tensor", s_hv, s_hv.v)
        for kk in range(4):
            if not (i == 0 and kk == 0):
                wait("tensor", s_hev, s_hev.v)
            op("tensor", "transpose", pbb[6][:, 0:128], htok[:, i * D + 2 * kk * 128: i * D + (2 * kk + 1) * 128], ident_b[:])
            op("tensor", "transpose", pbb[6][:, 128:256], htok[:, i * D + (2 * kk + 1) * 128: i * D + (2 * kk + 2) * 128], ident_b[:], inc=s_htr)
            wait("scalar", s_htr, s_htr.v)
            op("scalar", "copy", hfm[:, 2 * kk * TCORE + i * 128: 2 * kk * TCORE + (i + 1) * 128], pbb[6][:, 0:128])
            op("scalar", "copy", hfm[:, (2 * kk + 1) * TCORE + i * 128: (2 * kk + 1) * TCORE + (i + 1) * 128], pbb[6][:, 128:256], inc=s_hev)
        ofs_tok = D * TCORE
        wait("gpsimd", s_hv, s_hv.v)
        dma("gpsimd",
            agin_h.ap()[0:1, ofs_tok + i * 128 * D: ofs_tok + (i + 1) * 128 * D]
            .rearrange("o (t f) -> (o t) f", f=D),
            htok[:, i * D:(i + 1) * D], s_hp[0])
        if i + 1 == NTILE:
            wait("gpsimd", s_hev, s_hev.v)
            dma("gpsimd",
                agin_h.ap()[0:1, 0:D * TCORE].rearrange("o (k pp t) -> (o pp) k t", pp=128, t=TCORE),
                hfm.rearrange("pp (k tc) -> pp k tc", tc=TCORE)[:, :, :],
                s_hp[0])
            wait("gpsimd", s_hp[0], 16 * (NTILE + 1))
            cc("AllGather", ALU.bypass, [agin_h.ap()], [agout_h.ap()], s_agdone)

        wait("tensor", s_hev, s_hev.v)
        wait("tensor", s_cv2, CV2)
        wait("tensor", s_ovec, s_ovec.v)
        for k in range(8):
            op("tensor", "matmul", pb[2][:, 0:128],
               hfm[:, k * TCORE + i * 128: k * TCORE + (i + 1) * 128],
               hfm[:, k * TCORE + i * 128: k * TCORE + (i + 1) * 128],
               start=(k == 0), stop=(k == 7), inc=(s_gmm if k == 7 else None))
        wait("vector", s_gmm, s_gmm.v)
        wait("vector", s_adv, ADV)
        fop("vector", "tensor_tensor", gup[:], pb[2][:, 0:128], triu_f[:], op=ALU.mult)
        B.sems["s_ovec"].inc(1) if False else None
        op("vector", "tensor_copy", gpub[:, i * 256 + 128: i * 256 + 256], gup[:], inc=s_ovec)
        fop("vector", "tensor_scalar", nup[:], gup[:], ngc[:, 0:1], None, op0=ALU.mult)
        op("vector", "tensor_tensor", rbuf[:], nup[:], ident_b[:], op=ALU.add, inc=s_ovec)
        wait("tensor", s_ovec, s_ovec.v)
        op("tensor", "transpose", pbb[3][:, 0:128], nup[:], ident_b[:], inc=s_osq)
        wait("scalar", s_osq, s_osq.v)
        op("scalar", "copy", nlo[:], pbb[3][:, 0:128], inc=s_oev)
        for j in range(6):
            wait("tensor", s_oev, s_oev.v)
            op("tensor", "matmul", pb[3][:, 0:128], nlo[:], nup[:], start=True, stop=True)
            op("tensor", "matmul", pb[3][:, 128:256], nup[:], nlo[:], start=True, stop=True, inc=s_osq)
            wait("scalar", s_osq, s_osq.v)
            op("scalar", "copy", nup[:], pb[3][:, 0:128])
            op("scalar", "copy", nlo[:], pb[3][:, 128:256], inc=s_oev)
            wait("vector", s_oev, s_oev.v)
            op("vector", "tensor_tensor", ilq[:], nlo[:], ident_b[:], op=ALU.add, inc=s_ovec)
            wait("tensor", s_ovec, s_ovec.v)
            op("tensor", "matmul", pb[4][:, 0:128], ilq[:], rbuf[:], start=True, stop=True, inc=s_osq)
            wait("scalar", s_osq, s_osq.v)
            op("scalar", "copy", rbuf[:], pb[4][:, 0:128], inc=s_oev)
        wait("vector", s_oev, s_oev.v)
        op("vector", "tensor_scalar", gpub[:, i * 256: i * 256 + 128], rbuf[:], cval[:, 0:1], None,
           op0=ALU.mult, inc=s_ovec)

    wait("gpsimd", s_ovec, s_ovec.v)
    dma("gpsimd", agin_g.ap().rearrange("o (pp f) -> (o pp) f", pp=128), gpub[:, :], s_gpubd)
    wait("gpsimd", s_gpubd, s_gpubd.v)
    cc("AllGather", ALU.bypass, [agin_g.ap()], [agout_g.ap()], s_agdone)
    AG_G = s_agdone.v

    if dbg_hfm is not None:
        s_dh = S_("s_dh")
        wait("gpsimd", s_hev, s_hev.v)
        for _dc in range(0, 8 * TCORE, 2048):
            dma("gpsimd", dbg_hfm[:, _dc:_dc + 2048], hfm[:, _dc:_dc + 2048], s_dh)

    s_wi = S_("s_wi")
    wait("tensor", s_ld, LD)
    for k in range(8):
        op("tensor", "matmul", WP(k), ident_f[:], w0t_sb[:, k * 128:(k + 1) * 128],
           start=(k % 4 == 0), stop=(k % 4 == 3), skip_group_check=True, inc=(s_wi if k == 7 else None))

    s_vx2 = [S_("s_vx20"), S_("s_vx21")]
    s_vmm = S_("s_vmm")
    s_vev = S_("s_vev")
    s_vtr = S_("s_vtr")
    s_vta = S_("s_vta")
    s_xf2 = [S_("s_xf20"), S_("s_xf21")]
    s_wcast = S_("s_wcast")
    s_wlo = S_("s_wlo")
    s_zmm = S_("s_zmm")
    s_dv = S_("s_dv")
    s_umm = S_("s_umm")
    s_uev = S_("s_uev")
    s_cmm = S_("s_cmm")
    s_yevac = S_("s_yevac")
    s_ytr = S_("s_ytr")
    s_ytev = S_("s_ytev")
    s_upd = S_("s_upd")
    s_yp2 = [S_("s_yp20"), S_("s_yp21")]
    YINC = 32 if dbg_y is not None else 16

    def emit_vspan(v):
        r = (v * VSPAN) // TCORE
        tok0 = (v * VSPAN) % TCORE
        buf = vxf[v % 2]
        svx = s_vx2[v % 2]
        wait("sync", s_agdone, 1)
        if v >= 2:
            wait("sync", s_vmm, v - 1)
        wait("sync", svx, svx.v)
        dma("sync", buf[:],
            agout_h.ap()[r:r + 1, 0:D * TCORE]
            .rearrange("o (k pp t) -> (o pp) k t", pp=128, t=TCORE)[:, :, tok0:tok0 + VSPAN], svx)
        wait("tensor", svx, svx.v)
        wait("tensor", B.sems["s_gld"], 16)
        wait("tensor", B.sems["s_repev"], B.sems["s_repev"].v)
        wait("tensor", s_ytev, s_ytev.v)
        wait("tensor", s_vta, s_vta.v)
        wait("tensor", s_vev, s_vev.v)
        for k in range(8):
            op("tensor", "matmul", pb[7][:, 0:VSPAN], wkvvt_sb[:, k * 128:(k + 1) * 128],
               buf[:, k * VSPAN:(k + 1) * VSPAN],
               start=(k == 0), stop=(k == 7), inc=(s_vmm if k == 7 else None))
        wait("scalar", s_vmm, s_vmm.v)
        op("scalar", "activation", vstage[:], pb[7][:, 0:VSPAN], AF.Identity,
           scale=1.0, bias=bkvv_sb[:, 0:1], inc=s_vev)
        wait("tensor", s_vev, s_vev.v)
        for tq in range(VSPAN // 128):
            if tq > 0:
                wait("tensor", s_vta, s_vta.v)
            op("tensor", "transpose", pb[6][:, 0:128], vstage[:, tq * 128:(tq + 1) * 128], ident_f[:], inc=s_vtr)
            wait("scalar", s_vtr, s_vtr.v)
            op("scalar", "copy", A[:, (v * VSPAN // 128 + tq) * 128:(v * VSPAN // 128 + tq + 1) * 128],
               pb[6][:, 0:128], inc=s_vta)

    def emit_chunk(c):
        par = c % 2
        zb = pb[2 + par]
        ub = pb[4 + par]
        xf, xt, gt = xf_c[par], xt_c[par], gt_c[par]
        r = (c * CH) // TCORE
        tokp = (c * CH) % TCORE

        sxf = s_xf2[par]
        if "s_dc1" in B.sems and c == 1:
            for _e in ("scalar", "vector", "tensor"):
                wait(_e, B.sems["s_dc1"], B.sems["s_dc1"].v)
        if "s_dc1b" in B.sems and c == 2:
            for _e in ("scalar", "vector", "tensor"):
                wait(_e, B.sems["s_dc1b"], B.sems["s_dc1b"].v)
        wait("sync", s_agdone, 1)
        if c == 0:
            wait("sync", s_agdone, AG_G)
        if c >= 2:
            wait("sync", s_upd, c - 1)
        wait("sync", sxf, sxf.v)
        if "s_dc1" in B.sems:
            wait("sync", B.sems["s_dc1"], B.sems["s_dc1"].v)
        dma("sync", xf[:],
            agout_h.ap()[r:r + 1, 0:D * TCORE]
            .rearrange("o (k pp t) -> (o pp) k t", pp=128, t=TCORE)[:, :, tokp:tokp + 128], sxf)
        dma("sync", xt[:],
            agout_h.ap()[r:r + 1, D * TCORE:2 * D * TCORE]
            .rearrange("o (t f) -> (o t) f", f=D)[tokp:tokp + 128, :], sxf)
        dma("sync", gt[:],
            agout_g.ap()[c // OWNCH:c // OWNCH + 1, :]
            .rearrange("o (pp f) -> (o pp) f", pp=128)[:, (c % OWNCH) * 256:((c % OWNCH) + 1) * 256], sxf)
        XL = sxf.v

        if c > 0:
            wait("scalar", s_upd, c)
        else:
            wait("scalar", s_wi, 1)
        op("scalar", "activation", wbhi[:, 0:512], pb[0][:, 0:512], AF.Copy)
        op("scalar", "activation", wbhi[:, 512:1024], pb[1][:, 0:512], AF.Copy, inc=s_wcast)
        wait("vector", s_wcast, s_wcast.v)
        op("vector", "scalar_tensor_tensor", wblo[:, 0:512], pb[0][:, 0:512], 1.0, wbhi[:, 0:512],
           op0=ALU.mult, op1=ALU.subtract)
        op("vector", "scalar_tensor_tensor", wblo[:, 512:1024], pb[1][:, 0:512], 1.0, wbhi[:, 512:1024],
           op0=ALU.mult, op1=ALU.subtract, inc=s_wlo)

        wait("tensor", sxf, XL)
        wait("tensor", s_wlo, s_wlo.v)
        if c >= 2:
            wait("tensor", s_yevac, c - 1)
        for k in range(8):
            op("tensor", "matmul", zb[:, 0:128], xf[:, k * 128:(k + 1) * 128], wbhi[:, k * 128:(k + 1) * 128],
               start=(k == 0), stop=False)
        for k in range(8):
            op("tensor", "matmul", zb[:, 0:128], xf[:, k * 128:(k + 1) * 128], wblo[:, k * 128:(k + 1) * 128],
               start=False, stop=(k == 7), inc=(s_zmm if k == 7 else None))
        wait("vector", s_zmm, s_zmm.v)
        wait("vector", s_vta, s_vta.v)
        op("vector", "tensor_tensor", d_c[:], A[:, c * 128:(c + 1) * 128], zb[:, 0:128], op=ALU.subtract, inc=s_dv)
        wait("tensor", s_dv, s_dv.v)
        op("tensor", "matmul", ub[:, 0:128], gt[:, 0:128], d_c[:], start=True, stop=True, inc=s_umm)
        wait("scalar", s_umm, s_umm.v)
        op("scalar", "copy", u_c[:], ub[:, 0:128], inc=s_uev)
        wait("tensor", s_uev, s_uev.v)
        op("tensor", "matmul", zb[:, 0:128], gt[:, 128:256], u_c[:], start=False, stop=False,
           skip_group_check=True, inc=s_cmm)
        for k in range(8):
            op("tensor", "matmul", WP(k), xt[:, k * 128:(k + 1) * 128], u_c[:],
               start=False, stop=(c == NCH - 1), skip_group_check=True,
               inc=(s_upd if k == 7 else None))
        wait("scalar", s_cmm, s_cmm.v)
        if c >= 2:
            wait("scalar", s_yp2[par], YINC * (c // 2))
        op("scalar", "copy", y_c[:], zb[:, 0:128], inc=s_yevac)
        wait("tensor", s_yevac, c + 1)
        wait("tensor", s_vta, s_vta.v)
        op("tensor", "transpose", pbb[6][:, 0:128], y_c[:], ident_b[:], inc=s_ytr)
        wait("scalar", s_ytr, s_ytr.v)
        op("scalar", "copy", yt_c2[par][:], pbb[6][:, 0:128], inc=s_ytev)
        if dbg_c1 is not None and c == 0:
            s_dc1 = S_("s_dc1")
            wait("gpsimd", s_uev, s_uev.v)
            wait("gpsimd", s_wlo, s_wlo.v)
            o = 0
            dma("gpsimd", dbg_c1[:, 0:1024].bitcast(BF16)[:, 0:1024], xf[:], s_dc1)
            dma("gpsimd", dbg_c1[:, 1024:1536].bitcast(BF16)[:, 0:1024], xt[:], s_dc1)
            dma("gpsimd", dbg_c1[:, 1536:1664].bitcast(BF16)[:, 0:256], gt[:], s_dc1)
            dma("gpsimd", dbg_c1[:, 1664:1728].bitcast(BF16)[:, 0:128], d_c[:], s_dc1)
            dma("gpsimd", dbg_c1[:, 1728:1792].bitcast(BF16)[:, 0:128], u_c[:], s_dc1)
            dma("gpsimd", dbg_c1[:, 1792:2304].bitcast(BF16)[:, 0:1024], wbhi[:], s_dc1)
        if dbg_c1 is not None and c == 1:
            s_dc1b = S_("s_dc1b")
            wait("gpsimd", s_wlo, s_wlo.v)
            dma("gpsimd", dbg_c1[:, 2304:2816].bitcast(BF16)[:, 0:1024], wbhi[:], s_dc1b)
        wait("gpsimd", s_ytev, s_ytev.v)
        wait("gpsimd", s_yp2[par], s_yp2[par].v)
        pY = (c * CH) // PTOK
        dma("gpsimd", agin_y[pY].ap()[:, (c * CH) % PTOK: (c * CH) % PTOK + 128], yt_c2[par][:], s_yp2[par])
        if dbg_y is not None:
            dma("gpsimd", dbg_y[:, c * 128:(c + 1) * 128], yt_c2[par][:], s_yp2[par])

    SP_PER_PIECE = PTOK // VSPAN
    CH_PER_PIECE = PTOK // CH
    for p in range(NPIECE):
        for v in range(SP_PER_PIECE):
            emit_vspan(p * SP_PER_PIECE + v)
        for j in range(CH_PER_PIECE):
            emit_chunk(p * CH_PER_PIECE + j)

    if dbg_v is not None:
        s_dv2 = S_("s_dv2")
        wait("gpsimd", s_vta, s_vta.v)
        for _dc in range(0, S, 2048):
            dma("gpsimd", dbg_v[:, _dc:_dc + 2048], A[:, _dc:_dc + 2048], s_dv2)

    s_agy = S_("s_agy")
    for p in range(NPIECE):
        nch_done = (p + 1) * CH_PER_PIECE
        wait("gpsimd", s_yp2[0], YINC * ((nch_done + 1) // 2))
        wait("gpsimd", s_yp2[1], YINC * (nch_done // 2))
        cc("AllGather", ALU.bypass, [agin_y[p].ap()], [agout_y[p]], s_agy)

    if "yag" in debug:
        dbg_yag = dout("dbg_yag", [128, 8 * TCORE], BF16)
        s_dyag = S_("s_dyag")
        myp = (0 * 1)
        i_core_tok0 = 0
        wait("gpsimd", s_agy, s_agy.v)
        for rr in range(8):
            dma("gpsimd", dbg_yag[:, rr * TCORE:(rr + 1) * TCORE],
                agout_y[0][rr * 128:(rr + 1) * 128, 0:TCORE], s_dyag)

    NSPAN = S // VSPAN
    if phases >= 2:
        PCC = PTOK // TCORE
        pid = nc.partition_id()
        s_xr = [S_("s_xr0"), S_("s_xr1")]
        s_rtr = S_("s_rtr")
        s_rev = [S_("s_rev0"), S_("s_rev1")]
        xrv = {}
        wait("tensor", s_uev, NCH)
        for i in range(NTILE):
            buf = xtile[i % 2]
            sx = s_xr[i % 2]
            wait("sync", sx, sx.v)
            if i >= 2:
                wait("sync", s_rtr, xrv[i - 2])
            dma("sync", buf[:], xin[i * 128:(i + 1) * 128, :], sx)
            for kk in range(4):
                wait("tensor", sx, sx.v)
                wait("tensor", s_rev[0], s_rev[0].v)
                op("tensor", "transpose", pb[5][:, 0:128],
                   buf[:, 2 * kk * 128:(2 * kk + 1) * 128], ident_f[:])
                op("tensor", "transpose", pb[5][:, 128:256],
                   buf[:, (2 * kk + 1) * 128:(2 * kk + 2) * 128], ident_f[:], inc=s_rtr)
                wait("scalar", s_rtr, s_rtr.v)
                if i == 0 and kk == 0:
                    wait("scalar", s_dv, NCH)
                    if dbg_v is not None:
                        wait("scalar", B.sems["s_dv2"], B.sems["s_dv2"].v)
                op("scalar", "copy",
                   A[:, 2 * kk * TCORE + i * 128: 2 * kk * TCORE + (i + 1) * 128],
                   pb[5][:, 0:128])
                op("scalar", "copy",
                   A[:, (2 * kk + 1) * TCORE + i * 128: (2 * kk + 1) * TCORE + (i + 1) * 128],
                   pb[5][:, 128:256], inc=s_rev[0])
            xrv[i] = s_rtr.v

        ybuf = [gpub[:, 0:TCORE], gpub[:, TCORE:2 * TCORE]]
        s_yb = [S_("s_yb0"), S_("s_yb1")]
        s_res = S_("s_res")
        ybv = {}
        for k in range(8 if "nop2b" not in debug else 0):
            j = k % 2
            sy = s_yb[j]
            if k == 0:
                wait("gpsimd", s_agy, NPIECE)
                wait("gpsimd", s_gpubd, s_gpubd.v)
            wait("gpsimd", sy, sy.v)
            if k >= 2:
                wait("gpsimd", s_res, ybv[k - 2])
            dma("gpsimd", ybuf[j],
                agout_yall.ap()[bass.ds((pid // PCC) * NCORES * 128 + k * 128, 128),
                                bass.ds((pid % PCC) * TCORE, TCORE)], sy)
            wait("vector", sy, sy.v)
            if k == 0:
                wait("vector", s_rev[0], s_rev[0].v)
                wait("vector", s_rev[1], s_rev[1].v)
            op("vector", "scalar_tensor_tensor", A[:, k * TCORE:(k + 1) * TCORE],
               ybuf[j], modT[:, 16 + k:17 + k], A[:, k * TCORE:(k + 1) * TCORE],
               op0=ALU.mult, op1=ALU.add, inc=s_res)
            ybv[k] = s_res.v
        if dbg_res is not None:
            s_dres = S_("s_dres")
            wait("gpsimd", s_res, 8)
            for _dc in range(0, 8 * TCORE, 2048):
                dma("gpsimd", dbg_res[:, _dc:_dc + 2048], A[:, _dc:_dc + 2048], s_dres)

    if phases >= 2 and "nop0d" not in debug:
        if S >= 16384:
            wq_in = vxf[0].bitcast(F32)[:, 0:D]
            wq_sg = vxf[0].bitcast(F32)[:, D:D + 64]
            wq_bf = vxf[1][:, 0:D]
            wq_st = vxf[1][:, D:D + 256]
            wqdbg = vxf[1][:, 512:768]
        else:
            wq_in = sb("wq_in", [128, D])
            wq_sg = sb("wq_sg", [128, 64])
            wq_bf = sb("wq_bf", [128, D], BF16)
            wq_st = sb("wq_st", [128, 256], BF16)
            wqdbg = sb("wqdbg", [128, 256], BF16)
        s_wql = [S_("s_wql0"), S_("s_wql1")]
        s_wqv = S_("s_wqv")
        s_wqt = S_("s_wqt")
        s_wqe = [S_("s_wqe0"), S_("s_wqe1")]
        s_wqp = [S_("s_wqp0"), S_("s_wqp1")]
        s_agw = S_("s_agw")
        NG = 32
        wq_tiles = []
        for (gsl, usl, dsl) in ((g0s, u0s, d0s), (g1s, u1s, d1s)):
            tl = []
            for t3 in range(3):
                tl.append((gsl[t3 * 128:(t3 + 1) * 128, :], t3 * D * 128))
            for t3 in range(3):
                tl.append((usl[t3 * 128:(t3 + 1) * 128, :], (3 + t3) * D * 128))
            for s3 in range(3):
                tl.append((dsl[:, s3 * D:(s3 + 1) * D], (6 + s3) * D * 128))
            wq_tiles.append(tl)

        wait("tensor", s_ytev, s_ytev.v)
        wait("tensor", s_vta, s_vta.v)
        g32 = lambda ap: ap.rearrange("p (g k) -> p g k", k=32)
        sgb = lambda c0: wq_sg[:, c0:c0 + NG].rearrange("p (g o) -> p g o", o=1).to_broadcast((128, NG, 32))
        tc = 0
        for lsel in range(2):
            for (srcap, pubofs) in wq_tiles[lsel]:
                par = tc % 2
                svl = s_wql[par]
                wait("sync", svl, svl.v)
                wait("sync", s_wqv, s_wqv.v)
                if tc == 0:
                    wait("sync", s_vmm, NSPAN)
                dma("sync", wq_in[:, :], srcap, svl)
                wait("vector", svl, svl.v)
                fop("vector", "tensor_reduce", wq_sg[:, 0:NG].rearrange("p (g o) -> p g o", o=1),
                    g32(wq_in), axis=AX.X, op=ALU.max, apply_absolute_value=True)
                fop("vector", "tensor_scalar", wq_sg[:, 0:NG], wq_sg[:, 0:NG],
                    float(1.0 / 7.0), 1e-12, op0=ALU.mult, op1=ALU.max)
                fop("vector", "reciprocal", wq_sg[:, NG:2 * NG], wq_sg[:, 0:NG])
                fop("vector", "tensor_tensor", g32(wq_in), g32(wq_in), sgb(NG), op=ALU.mult)
                fop("vector", "tensor_scalar", wq_in[:, :], wq_in[:, :], MAGIC, None, op0=ALU.add)
                fop("vector", "tensor_scalar", wq_in[:, :], wq_in[:, :], MAGIC, None, op0=ALU.subtract)
                wait("vector", s_wqt, s_wqt.v)
                op("vector", "tensor_tensor", g32(wq_bf), g32(wq_in), sgb(0), op=ALU.mult, inc=s_wqv)
                for bb in range(4):
                    wait("tensor", s_wqv, s_wqv.v)
                    wait("tensor", s_wqe[0], s_wqe[0].v)
                    wait("tensor", s_wqe[1], s_wqe[1].v)
                    op("tensor", "transpose", pbb[6][:, 0:128],
                       wq_bf[:, 2 * bb * 128:(2 * bb + 1) * 128], ident_b[:])
                    op("tensor", "transpose", pbb[6][:, 128:256],
                       wq_bf[:, (2 * bb + 1) * 128:(2 * bb + 2) * 128], ident_b[:], inc=s_wqt)
                    wait("scalar", s_wqt, s_wqt.v)
                    wait("scalar", s_wqp[0], s_wqp[0].v)
                    wait("scalar", s_wqp[1], s_wqp[1].v)
                    op("scalar", "copy", wq_st[:, 0:128], pbb[6][:, 0:128], inc=s_wqe[0])
                    op("scalar", "copy", wq_st[:, 128:256], pbb[6][:, 128:256], inc=s_wqe[1])
                    for half in range(2):
                        b = 2 * bb + half
                        wait("gpsimd", s_wqe[half], s_wqe[half].v)
                        wait("gpsimd", s_wqp[half], s_wqp[half].v)
                        dma("gpsimd",
                            agin_wdq[lsel].ap()[0:1, pubofs + b * 16384: pubofs + (b + 1) * 16384]
                            .rearrange("o (r q) -> (o r) q", q=128),
                            wq_st[:, half * 128:half * 128 + 128], s_wqp[half])
                tc += 1
            wait("gpsimd", s_wqp[0], s_wqp[0].v)
            wait("gpsimd", s_wqp[1], s_wqp[1].v)
            cc("AllGather", ALU.bypass, [agin_wdq[lsel].ap()], [agout_wdq[lsel].ap()], s_agw)

        if dbg_wq is not None:
            s_dwq = S_("s_dwq")
            s_wqd = S_("s_wqd")
            wait("sync", s_agw, 2)
            dma("sync", wqdbg[:, 0:128],
                agout_wdq[0].ap()[0:1, 0:16384].rearrange("o (p f) -> (o p) f", f=128), s_wqd)
            dma("sync", wqdbg[:, 128:256],
                agout_wdq[0].ap()[0:1, 6 * D * 128: 6 * D * 128 + 16384]
                .rearrange("o (p f) -> (o p) f", f=128), s_wqd)
            wait("gpsimd", s_wqd, s_wqd.v)
            dma("gpsimd", dbg_wq[:, :], wqdbg[:, :], s_dwq)

    if phases >= 3:
        TB = min(512, TCORE)
        NTB = TCORE // TB
        FBLK = FF // 128
        s_b3 = S_("s_b3")

        def emit_armax(j, srcs, psem):
            s_pub = S_("arpub"); s_cc = S_("arcc"); s_rb = S_("arrb")
            s_pv = S_("arpv"); s_rpm = S_("arrpm")
            wait("gpsimd", psem, psem.v)
            for m, sap in enumerate(srcs):
                wait("gpsimd", s_pub, s_pub.v)
                dma("gpsimd", agin_mx[j].ap()[0:1, m:m + 1], sap, s_pub)
            wait("gpsimd", s_pub, s_pub.v)
            cc("AllGather", ALU.bypass, [agin_mx[j].ap()], [agout_mx[j].ap()], s_cc)
            wait("gpsimd", s_cc, s_cc.v)
            for m in range(len(srcs)):
                wait("gpsimd", s_rb, s_rb.v)
                if m > 0:
                    wait("gpsimd", s_pv, s_pv.v)
                dma("gpsimd", mxr[0:1, 0:NCORES],
                    agout_mx[j].ap()[:, m:m + 1].rearrange("r o -> o r"), s_rb,
                    allow_slow_non_contiguous=True)
                wait("vector", s_rb, s_rb.v)
                wait("vector", s_rpm, s_rpm.v)
                fop("vector", "tensor_reduce", mxr[0:1, 8:9],
                    mxr[0:1, 0:NCORES].rearrange("o (g k) -> o g k", g=1),
                    axis=AX.X, op=ALU.max)
                fop("vector", "tensor_scalar", mxr[0:1, 9:10], mxr[0:1, 8:9],
                    float(1.0 / 32767.0), 1e-12, op0=ALU.mult, op1=ALU.max)
                fop("vector", "reciprocal", mxr[0:1, 10:11], mxr[0:1, 9:10])
                op("vector", "tensor_scalar", mxr[0:1, 11:12], mxr[0:1, 9:10],
                   -MAGIC, None, op0=ALU.mult, inc=s_pv)
                cb0 = 6 * j + 3 * m
                wait("tensor", s_pv, s_pv.v)
                wait("tensor", s_b3, s_b3.v)
                op("tensor", "matmul", pb[3][:, 0:1], ones_f[0:1, :], mxr[0:1, 9:10],
                   start=True, stop=True)
                op("tensor", "matmul", pb[3][:, 1:2], ones_f[0:1, :], mxr[0:1, 10:11],
                   start=True, stop=True)
                op("tensor", "matmul", pb[3][:, 2:3], ones_f[0:1, :], mxr[0:1, 11:12],
                   start=True, stop=True, inc=s_rpm)
                wait("scalar", s_rpm, s_rpm.v)
                op("scalar", "copy", qcols[:, cb0:cb0 + 3], pb[3][:, 0:3], inc=s_b3)
            return s_b3, s_b3.v

        def emit_ptmax(srccol, psem, bank):
            s_t = S_("pnxt"); s_vv = S_("pnxv")
            wait("tensor", psem, psem.v)
            wait("tensor", s_b3, s_b3.v)
            op("tensor", "transpose", bank[0:1, 0:128], mxp[:, srccol:srccol + 1],
               ident_f[:], inc=s_t)
            wait("vector", s_t, s_t.v)
            op("vector", "tensor_reduce", mxr[0:1, 12 + srccol:13 + srccol],
               bank[0:1, 0:128].rearrange("o (g k) -> o g k", g=1),
               axis=AX.X, op=ALU.max, inc=s_vv)
            return s_vv, s_vv.v

        def emit_prenorm(ccol, scol, qdq_j, ready):
            s_c = S_("pnc"); s_q = S_("pnq")
            s_m = S_("pnm")
            s_xv = S_("pnxv"); s_xs = S_("pnxs"); s_xt = S_("pnxt")
            s_v = S_("pnv"); s_a = S_("pna")
            for eng, sem, cnt in ready:
                wait(eng, sem, cnt)
            stfree = {}
            for tb in range(NTB):
                for k in range(8):
                    slot = k % 2
                    n = tb * 8 + k
                    rb = A[:, k * TCORE + tb * TB: k * TCORE + tb * TB + TB]
                    sta = xf_c[0][:, slot * TB: slot * TB + TB]
                    stb = xf_c[1][:, slot * TB: slot * TB + TB]
                    if n >= 2:
                        wait("scalar", s_m, stfree[slot])
                        wait("vector", s_m, stfree[slot])
                    op("scalar", "copy", sta, rb, inc=s_c)
                    op("vector", "tensor_tensor", stb, rb, rb, op=ALU.mult, inc=s_q)
                    wait("tensor", s_c, s_c.v)
                    wait("tensor", s_q, s_q.v)
                    if k == 0 and tb > 0:
                        wait("tensor", s_xv, s_xv.v)
                        wait("tensor", s_xs, s_xs.v)
                    op("tensor", "matmul", pb[7][0:1, 0:TB], ones_cb[:, 0:1], sta,
                       start=(k == 0), stop=(k == 7), inc=s_m)
                    op("tensor", "matmul", pb[7][32:33, 0:TB], ones_cb[:, 0:1], stb,
                       start=(k == 0), stop=(k == 7), inc=s_m)
                    stfree[slot] = s_m.v
                r0 = pb[7][0:1, 0:TB]; r32 = pb[7][32:33, 0:TB]
                mu = rows2[0:1, 0:TB]; e2 = rows2[0:1, 512:512 + TB]
                wait("scalar", s_m, s_m.v)
                op("scalar", "copy", mu, r0)
                op("scalar", "copy", e2, r32, inc=s_xs)
                wait("vector", s_xs, s_xs.v)
                fop("vector", "tensor_scalar", mu, mu, float(1.0 / D), None, op0=ALU.mult)
                fop("vector", "tensor_scalar", e2, e2, float(1.0 / D), None, op0=ALU.mult)
                fop("vector", "tensor_tensor", rowsflat[0:1, 0:TB], mu, mu, op=ALU.mult)
                fop("vector", "tensor_tensor", e2, e2, rowsflat[0:1, 0:TB], op=ALU.subtract)
                op("vector", "tensor_scalar", e2, e2, 1e-5, None, op0=ALU.add, inc=s_xv)
                wait("scalar", s_xv, s_xv.v)
                op("scalar", "activation", rowsflat[0:1, 0:TB], e2, AF.Sqrt, inc=s_xs)
                wait("vector", s_xs, s_xs.v)
                fop("vector", "reciprocal", e2, rowsflat[0:1, 0:TB])
                fop("vector", "tensor_tensor", rowsflat[0:1, 512:512 + TB], mu, e2, op=ALU.mult)
                fop("vector", "tensor_scalar", rowsflat[0:1, 512:512 + TB],
                    rowsflat[0:1, 512:512 + TB], -1.0, None, op0=ALU.mult)
                op("vector", "tensor_copy", rowsflat[0:1, 0:TB], e2, inc=s_xv)
                wait("tensor", s_xv, s_xv.v)
                if tb > 0:
                    wait("tensor", s_v, s_v.v)
                op("tensor", "matmul", pb[4][:, 0:TB], ones_f[0:1, :],
                   rowsflat[0:1, 0:TB], start=True, stop=True)
                op("tensor", "matmul", pb[5][:, 0:TB], ones_f[0:1, :],
                   rowsflat[0:1, 512:512 + TB], start=True, stop=True, inc=s_xt)
                t2free = {}
                for k in range(8):
                    slot = k % 2
                    n = tb * 8 + k
                    rb = A[:, k * TCORE + tb * TB: k * TCORE + tb * TB + TB]
                    t2 = xhat[:, slot * TB: slot * TB + TB] if TB == 512 else \
                         xhat[:, slot * TB: slot * TB + TB]
                    wait("vector", s_xt, s_xt.v)
                    if k >= 2:
                        wait("vector", s_a, t2free[slot])
                    fop("vector", "tensor_tensor", t2, rb, pb[4][:, 0:TB], op=ALU.mult)
                    op("vector", "tensor_tensor", t2, t2, pb[5][:, 0:TB], op=ALU.add, inc=s_v)
                    if qdq_j is not None:
                        wait("vector", s_v, s_v.v)
                        if n == 0:
                            fop("vector", "tensor_reduce", mxp[:, 0:1], t2, axis=AX.X,
                                op=ALU.max, apply_absolute_value=True)
                        else:
                            fop("vector", "tensor_reduce", mxp[:, 1:2], t2, axis=AX.X,
                                op=ALU.max, apply_absolute_value=True)
                            fop("vector", "tensor_tensor", mxp[:, 0:1], mxp[:, 0:1],
                                mxp[:, 1:2], op=ALU.max)
                    wait("scalar", s_v, s_v.v)
                    op("scalar", "activation",
                       hfm[:, k * TCORE + tb * TB: k * TCORE + tb * TB + TB],
                       t2, AF.Identity, scale=modT[:, ccol + k:ccol + k + 1],
                       bias=modT[:, scol + k:scol + k + 1], inc=s_a)
                    t2free[slot] = s_a.v
            if qdq_j is None:
                return s_a, s_a.v
            s_mxf = S_("pnxv")
            op("vector", "tensor_copy", mxp[:, 1:2], mxp[:, 0:1], inc=s_mxf)
            psem, _ = emit_ptmax(0, s_mxf, pb[3])
            qsem, qcnt = emit_armax(qdq_j, [mxr[0:1, 12:13]], psem)
            cb0 = 6 * qdq_j
            s_q1 = S_("pnq1"); s_q2 = S_("pnq2")
            hafinal = s_a.v
            qfree = {}
            for tb in range(NTB):
                for k in range(8):
                    slot = (tb * 8 + k) % 2
                    n = tb * 8 + k
                    hb = hfm[:, k * TCORE + tb * TB: k * TCORE + tb * TB + TB]
                    t2 = xhat[:, slot * TB: slot * TB + TB]
                    if n == 0:
                        wait("scalar", qsem, qcnt)
                        wait("scalar", s_a, hafinal)
                    if n >= 2:
                        wait("scalar", s_q2, qfree[slot])
                    op("scalar", "activation", t2, hb, AF.Identity,
                       scale=qcols[:, cb0 + 1:cb0 + 2], bias=np.float32(MAGIC), inc=s_q1)
                    wait("vector", s_q1, s_q1.v)
                    op("vector", "tensor_scalar", hb, t2, MAGIC, qcols[:, cb0:cb0 + 1],
                       op0=ALU.subtract, op1=ALU.mult, inc=s_q2)
                    qfree[slot] = s_q2.v
            return s_q2, s_q2.v

        def emit_ffn(lsel, gbcol, jug, jp2, jp3, s_hq, hqcnt):
            NSB = NTB
            wstage = [carep.bitcast(BF16), sarep.bitcast(BF16)]
            ustg = [wbhi[:, 0:TB], wbhi[:, 512:512 + TB] if TB == 512 else wbhi[:, TB:2 * TB]]
            gstg = [wblo[:, 0:TB], wblo[:, 512:512 + TB] if TB == 512 else wblo[:, TB:2 * TB]]
            s_wl = [S_("fwl0"), S_("fwl1")]
            s_mm = S_("fmm"); s_ev = S_("fev"); s_mx = S_("fmx")
            s_sp = [S_("fsp0"), S_("fsp1")]
            s_sp2 = [S_("ssp0"), S_("ssp1")]
            mmv = {}
            evfree = {}
            wsfree = {}
            spfree = [(s_sp[0].v, s_sp2[0].v), (s_sp[1].v, s_sp2[1].v)]
            MM0 = s_mm.v
            for fb in range(FBLK):
                rank, t3 = fb // 3, fb % 3
                par = fb % 2
                ws = wstage[par]; sw = s_wl[par]
                wait("sync", sw, sw.v)
                if fb == 0:
                    wait("sync", s_agw, lsel + 1)
                    wait("sync", s_mm, MM0)
                if fb >= 2:
                    wait("sync", s_mm, wsfree[fb - 2])
                for q in range(8):
                    dma("sync", ws[:, q * 128:(q + 1) * 128],
                        agout_wdq[lsel].ap()[rank:rank + 1,
                            t3 * D * 128 + q * 16384: t3 * D * 128 + (q + 1) * 16384]
                        .rearrange("o (p f) -> (o p) f", f=128), sw)
                for q in range(8):
                    dma("sync", ws[:, 1024 + q * 128:1024 + (q + 1) * 128],
                        agout_wdq[lsel].ap()[rank:rank + 1,
                            (3 + t3) * D * 128 + q * 16384: (3 + t3) * D * 128 + (q + 1) * 16384]
                        .rearrange("o (p f) -> (o p) f", f=128), sw)
                wait("tensor", sw, sw.v)
                if fb == 0:
                    wait("tensor", s_hq, hqcnt)
                    wait("tensor", s_wqe[0], s_wqe[0].v)
                    wait("tensor", s_wqe[1], s_wqe[1].v)
                for tb in range(NSB):
                    if fb > 0:
                        uv, gv = evfree[(fb - 1, tb)]
                        wait("tensor", s_ev, uv)
                    for q in range(8):
                        op("tensor", "matmul", pb[tb][:, 0:TB],
                           ws[:, 1024 + q * 128:1024 + (q + 1) * 128],
                           hfm[:, q * TCORE + tb * TB: q * TCORE + tb * TB + TB],
                           start=(q == 0), stop=(q == 7))
                    if fb > 0:
                        wait("tensor", s_ev, evfree[(fb - 1, tb)][1])
                    for q in range(8):
                        op("tensor", "matmul", pb[4 + tb][:, 0:TB],
                           ws[:, q * 128:(q + 1) * 128],
                           hfm[:, q * TCORE + tb * TB: q * TCORE + tb * TB + TB],
                           start=(q == 0), stop=(q == 7), inc=(s_mm if q == 7 else None))
                    mmv[(fb, tb)] = s_mm.v
                wsfree[fb] = s_mm.v
                for tb in range(NSB):
                    sl = (fb * NSB + tb) % 2
                    first = (fb == 0 and tb == 0)
                    wait("vector", s_mm, mmv[(fb, tb)])
                    wait("vector", s_sp[sl], spfree[sl][0])
                    wait("vector", s_sp2[sl], spfree[sl][1])
                    op("vector", "tensor_scalar", ustg[sl], pb[tb][:, 0:TB], 1.0, None,
                       op0=ALU.mult, inc=s_ev)
                    u_ev = s_ev.v
                    op("vector", "tensor_reduce", mxp[:, 1:2], pb[tb][:, 0:TB],
                       axis=AX.X, op=ALU.max, apply_absolute_value=True, inc=s_ev)
                    uv = s_ev.v
                    wait("vector", s_ev, s_ev.v)
                    if first:
                        fop("vector", "tensor_copy", mxp[:, 2:3], mxp[:, 1:2])
                    else:
                        fop("vector", "tensor_tensor", mxp[:, 2:3], mxp[:, 2:3],
                            mxp[:, 1:2], op=ALU.max)
                    op("vector", "tensor_scalar", gstg[sl], pb[4 + tb][:, 0:TB], 1.0, None,
                       op0=ALU.mult, inc=s_ev)
                    g_ev = s_ev.v
                    op("vector", "tensor_reduce", mxp[:, 1:2], pb[4 + tb][:, 0:TB],
                       axis=AX.X, op=ALU.max, apply_absolute_value=True, inc=s_ev)
                    gv = s_ev.v
                    wait("vector", s_ev, s_ev.v)
                    if first:
                        fop("vector", "tensor_copy", mxp[:, 3:4], mxp[:, 1:2])
                    else:
                        fop("vector", "tensor_tensor", mxp[:, 3:4], mxp[:, 3:4],
                            mxp[:, 1:2], op=ALU.max)
                    evfree[(fb, tb)] = (uv, gv)
                    wait("gpsimd", s_ev, u_ev)
                    wait("gpsimd", s_sp[sl], s_sp[sl].v)
                    dma("gpsimd", dscr2.ap()[fb * 128:(fb + 1) * 128, tb * TB:tb * TB + TB],
                        ustg[sl], s_sp[sl])
                    wait("scalar", s_ev, g_ev)
                    wait("scalar", s_sp2[sl], s_sp2[sl].v)
                    dma("scalar", dscr1.ap()[fb * 128:(fb + 1) * 128, tb * TB:tb * TB + TB],
                        gstg[sl], s_sp2[sl])
                    spfree[sl] = (s_sp[sl].v, s_sp2[sl].v)
            ALLSP = [s_sp[0].v, s_sp[1].v]
            ALLMM = s_mm.v
            s_mxf = S_("fmx")
            op("vector", "tensor_copy", mxp[:, 1:2], mxp[:, 2:3], inc=s_mxf)
            pA, _ = emit_ptmax(2, s_mxf, pb[3])
            op("vector", "tensor_copy", mxp[:, 1:2], mxp[:, 3:4], inc=s_mxf)
            pB, _ = emit_ptmax(3, s_mxf, pb[2])
            qsem, qcnt = emit_armax(jug, [mxr[0:1, 14:15], mxr[0:1, 15:16]], pB)
            cbu, cbg = 6 * jug, 6 * jug + 3

            gbufs = [w0t_sb.bitcast(BF16)[:, 0:TCORE], gpub[:, 0:TCORE]]
            tbufs = [xt_c[0].bitcast(F32)[:, 0:TB], xt_c[1].bitcast(F32)[:, 0:TB]]
            sgbufs = [xhat[:, 0:TB], xhat[:, 512:512 + TB] if TB == 512 else xhat[:, TB:2 * TB]]
            w32 = wkvvt_sb.bitcast(F32)
            gqbufs = ([vstage[:, 0:TB], w32[:, 0:TB]] if VSPAN >= TB
                      else [w32[:, 0:TB], w32[:, TB:2 * TB]])
            p2stg = [xf_c[0][:, 0:TB], xf_c[0][:, 512:512 + TB] if TB == 512 else xf_c[0][:, TB:2 * TB]]
            s_rl = [S_("frl0"), S_("frl1")]
            s_ts = S_("fts"); s_vq = S_("fvq"); s_pv = S_("fpv")
            s_pw = [S_("fpw0"), S_("fpw1")]
            tbfree = {}; sgfree = {}; gqfree = {}; p2free = {}; gbfree = {}
            pwfree = [s_pw[0].v, s_pw[1].v]
            for fb in range(FBLK):
                par = fb % 2
                gb = gbufs[par]; sg = s_rl[par]
                wait("sync", sg, sg.v)
                if fb == 0:
                    wait("sync", s_sp2[0], s_sp2[0].v)
                    wait("sync", s_sp2[1], s_sp2[1].v)
                    wait("sync", s_res, 8)
                if fb >= 2:
                    wait("sync", s_ts, gbfree[par])
                dma("sync", gb, dscr1.ap()[fb * 128:(fb + 1) * 128, :], sg)
                for sbk in range(NSB):
                    sl = (fb * NSB + sbk) % 2
                    n = fb * NSB + sbk
                    tbf = tbufs[sl]; sgb = sgbufs[sl]; gq = gqbufs[sl]; p2s = p2stg[sl]
                    gblk = gb[:, sbk * TB: sbk * TB + TB]
                    wait("scalar", sg, sg.v)
                    if n == 0:
                        wait("scalar", qsem, qcnt)
                    if n >= 2:
                        wait("scalar", s_vq, tbfree[sl])
                    op("scalar", "activation", tbf, gblk, AF.Identity,
                       scale=qcols[:, cbg + 1:cbg + 2], bias=np.float32(MAGIC), inc=s_ts)
                    wait("scalar", s_ts, s_ts.v)
                    if n >= 2:
                        wait("scalar", s_pv, sgfree[sl])
                    op("scalar", "activation", sgb, tbf, AF.Sigmoid,
                       scale=qcols[:, cbg:cbg + 1], bias=qcols[:, cbg + 2:cbg + 3], inc=s_ts)
                    wait("vector", s_ts, s_ts.v)
                    if n >= 2:
                        wait("vector", s_pv, gqfree[sl])
                    op("vector", "tensor_scalar", gq, tbf, MAGIC, qcols[:, cbg:cbg + 1],
                       op0=ALU.subtract, op1=ALU.mult, inc=s_vq)
                    tbfree[sl] = s_vq.v
                    fop("vector", "tensor_scalar", sgb, sgb, 192.0,
                        float(192.0 + 65535.0 / 65536.0), op0=ALU.add, op1=ALU.min)
                    wait("vector", s_pw[sl], pwfree[sl])
                    wait("vector", s_vq, s_vq.v)
                    op("vector", "scalar_tensor_tensor", p2s, sgb, 192.0, gq,
                       op0=ALU.subtract, op1=ALU.mult, inc=s_pv)
                    sgfree[sl] = s_pv.v
                    gqfree[sl] = s_pv.v
                    wait("vector", s_pv, s_pv.v)
                    op("vector", "tensor_reduce", mxp[:, 1:2], p2s, axis=AX.X,
                       op=ALU.max, apply_absolute_value=True, inc=s_pv)
                    wait("vector", s_pv, s_pv.v)
                    if n == 0:
                        fop("vector", "tensor_copy", mxp[:, 4:5], mxp[:, 1:2])
                    else:
                        fop("vector", "tensor_tensor", mxp[:, 4:5], mxp[:, 4:5],
                            mxp[:, 1:2], op=ALU.max)
                    wait("gpsimd", s_pv, s_pv.v)
                    wait("gpsimd", s_pw[sl], s_pw[sl].v)
                    dma("gpsimd", dscr1.ap()[fb * 128:(fb + 1) * 128, sbk * TB:sbk * TB + TB],
                        p2s, s_pw[sl])
                    pwfree[sl] = s_pw[sl].v
                gbfree[par] = s_ts.v
            P2W = [s_pw[0].v, s_pw[1].v]
            s_mxf2 = S_("fmx")
            op("vector", "tensor_copy", mxp[:, 1:2], mxp[:, 4:5], inc=s_mxf2)
            pC, _ = emit_ptmax(4, s_mxf2, pb[3])
            qsem2, qcnt2 = emit_armax(jp2, [mxr[0:1, 16:17]], pC)
            cbp = 6 * jp2

            upbufs = [carep.bitcast(BF16)[:, 0:TCORE], sarep.bitcast(BF16)[:, 0:TCORE]]
            tbufs2 = [wbhi.bitcast(F32)[:, 0:TB], wblo.bitcast(F32)[:, 0:TB]]
            tb2free = {}; p3free = {}; uqfree = {}; g2free = {}; rlfree = {}
            pw3free = [s_pw[0].v, s_pw[1].v]
            s_ur = [S_("vrl0"), S_("vrl1")]
            for fb in range(FBLK):
                par = fb % 2
                p2b = gbufs[par]; upb = upbufs[par]
                sp = s_rl[par]
                su = s_ur[par]
                wait("sync", sp, sp.v)
                if fb == 0:
                    wait("sync", s_pw[0], P2W[0])
                    wait("sync", s_pw[1], P2W[1])
                if fb >= 2:
                    wait("sync", s_ts, rlfree[par])
                dma("sync", p2b, dscr1.ap()[fb * 128:(fb + 1) * 128, :], sp)
                wait("scalar", su, su.v)
                if fb == 0:
                    wait("scalar", s_sp[0], ALLSP[0])
                    wait("scalar", s_sp[1], ALLSP[1])
                    wait("scalar", s_mm, ALLMM)
                if fb >= 2:
                    wait("scalar", s_ts, rlfree[par])
                dma("scalar", upb, dscr2.ap()[fb * 128:(fb + 1) * 128, :], su)
                for sbk in range(NSB):
                    sl = (fb * NSB + sbk) % 2
                    n = fb * NSB + sbk
                    tbf = tbufs[sl]; tbf2 = tbufs2[sl]
                    g2 = gqbufs[sl]; uq = sgbufs[sl]; p3s = p2stg[sl]
                    wait("scalar", sp, sp.v)
                    wait("scalar", su, su.v)
                    if n == 0:
                        wait("scalar", qsem2, qcnt2)
                        wait("scalar", s_vq, s_vq.v)
                        wait("scalar", s_pv, s_pv.v)
                    if n >= 2:
                        wait("scalar", s_vq, tb2free[sl])
                    op("scalar", "activation", tbf, p2b[:, sbk * TB: sbk * TB + TB],
                       AF.Identity, scale=qcols[:, cbp + 1:cbp + 2],
                       bias=np.float32(MAGIC), inc=s_ts)
                    op("scalar", "activation", tbf2, upb[:, sbk * TB: sbk * TB + TB],
                       AF.Identity, scale=qcols[:, cbu + 1:cbu + 2],
                       bias=np.float32(MAGIC), inc=s_ts)
                    wait("vector", s_ts, s_ts.v)
                    if n >= 2:
                        wait("vector", s_pv, g2free[sl])
                    op("vector", "tensor_scalar", g2, tbf, MAGIC, qcols[:, cbp:cbp + 1],
                       op0=ALU.subtract, op1=ALU.mult, inc=s_vq)
                    op("vector", "tensor_scalar", uq, tbf2, MAGIC, qcols[:, cbu:cbu + 1],
                       op0=ALU.subtract, op1=ALU.mult, inc=s_vq)
                    tb2free[sl] = s_vq.v
                    wait("vector", s_pw[sl], pw3free[sl])
                    wait("vector", s_vq, s_vq.v)
                    op("vector", "tensor_tensor", p3s, g2, uq, op=ALU.mult, inc=s_pv)
                    g2free[sl] = s_pv.v
                    wait("vector", s_pv, s_pv.v)
                    op("vector", "tensor_reduce", mxp[:, 1:2], p3s, axis=AX.X,
                       op=ALU.max, apply_absolute_value=True, inc=s_pv)
                    wait("vector", s_pv, s_pv.v)
                    if n == 0:
                        fop("vector", "tensor_copy", mxp[:, 5:6], mxp[:, 1:2])
                    else:
                        fop("vector", "tensor_tensor", mxp[:, 5:6], mxp[:, 5:6],
                            mxp[:, 1:2], op=ALU.max)
                    wait("gpsimd", s_pv, s_pv.v)
                    wait("gpsimd", s_pw[sl], s_pw[sl].v)
                    dma("gpsimd", dscr2.ap()[fb * 128:(fb + 1) * 128, sbk * TB:sbk * TB + TB],
                        p3s, s_pw[sl])
                    pw3free[sl] = s_pw[sl].v
                rlfree[par] = s_ts.v
            P3W = [s_pw[0].v, s_pw[1].v]
            s_mxf3 = S_("fmx")
            op("vector", "tensor_copy", mxp[:, 1:2], mxp[:, 5:6], inc=s_mxf3)
            pD, _ = emit_ptmax(5, s_mxf3, pb[3])
            qsem3, qcnt3 = emit_armax(jp3, [mxr[0:1, 17:18]], pD)
            cbo = 6 * jp3

            wdstage = [xf_c[0][:, 0:1024], xf_c[1][:, 0:1024]]
            p3rl = [wblo[:, 0:TB], wblo[:, 512:512 + TB] if TB == 512 else wblo[:, TB:2 * TB]]
            obufs = [wbhi[:, 0:TB], wbhi[:, 512:512 + TB] if TB == 512 else wbhi[:, TB:2 * TB]]
            s_dw = [S_("fwl0"), S_("fwl1")]
            s_or = [S_("frl0"), S_("frl1")]
            s_vr = [S_("vrl0"), S_("vrl1")]
            s_dmm = S_("fdmm"); s_fres = S_("ffres")
            dmmv = {}; tqfree = {}; obfree = {}; fresv = {}
            EVFULL = s_ev.v
            VQFULL = s_vq.v
            PVFULL = s_pv.v
            TSFULL = s_ts.v
            for tb in range(NSB):
                for fb in range(FBLK):
                    par = fb % 2
                    n = tb * FBLK + fb
                    wd = wdstage[par]; p3b = p3rl[par]; ob = obufs[par]; tbf = tbufs[par]
                    sw = s_dw[par]; so = s_or[par]
                    wait("sync", sw, sw.v)
                    if n == 0:
                        wait("sync", s_pv, PVFULL)
                    if n >= 2:
                        wait("sync", s_dmm, dmmv[n - 2])
                    for j in range(8):
                        dma("sync", wd[:, j * 128:(j + 1) * 128],
                            agout_wdq[lsel].ap()[j:j + 1,
                                6 * D * 128 + fb * 16384: 6 * D * 128 + (fb + 1) * 16384]
                            .rearrange("o (p f) -> (o p) f", f=128), sw)
                    sv = s_vr[par]
                    wait("scalar", sv, sv.v)
                    if n == 0:
                        wait("scalar", s_pw[0], P3W[0])
                        wait("scalar", s_pw[1], P3W[1])
                        wait("scalar", s_vq, VQFULL)
                        wait("scalar", s_ts, TSFULL)
                    if n >= 2:
                        wait("scalar", s_dmm, dmmv[n - 2])
                        wait("scalar", s_vq, tqfree[par])
                    dma("scalar", p3b,
                        dscr2.ap()[fb * 128:(fb + 1) * 128, tb * TB:tb * TB + TB], sv)
                    wait("scalar", sv, sv.v)
                    if n == 0:
                        wait("scalar", qsem3, qcnt3)
                    if n >= 2:
                        wait("scalar", s_vq, tqfree[par])
                    op("scalar", "activation", tbf, p3b, AF.Identity,
                       scale=qcols[:, cbo + 1:cbo + 2], bias=np.float32(MAGIC), inc=s_ts)
                    wait("vector", s_ts, s_ts.v)
                    if n >= 2:
                        wait("vector", s_dmm, obfree[par])
                    op("vector", "tensor_scalar", ob, tbf, MAGIC, qcols[:, cbo:cbo + 1],
                       op0=ALU.subtract, op1=ALU.mult, inc=s_vq)
                    tqfree[par] = s_vq.v
                    wait("tensor", sw, sw.v)
                    wait("tensor", s_vq, s_vq.v)
                    if fb == 0:
                        if tb == 0:
                            wait("tensor", s_ev, EVFULL)
                        else:
                            wait("tensor", s_fres, fresv[tb - 1])
                    for j in range(8):
                        op("tensor", "matmul", pb[j][:, 0:TB], wd[:, j * 128:(j + 1) * 128],
                           ob, start=(fb == 0), stop=(fb == FBLK - 1),
                           inc=(s_dmm if j == 7 else None))
                    dmmv[n] = s_dmm.v
                    obfree[par] = s_dmm.v
                wait("vector", s_dmm, s_dmm.v)
                for j in range(8):
                    op("vector", "scalar_tensor_tensor",
                       A[:, j * TCORE + tb * TB: j * TCORE + tb * TB + TB],
                       pb[j][:, 0:TB], modT[:, gbcol + j:gbcol + j + 1],
                       A[:, j * TCORE + tb * TB: j * TCORE + tb * TB + TB],
                       op0=ALU.mult, op1=ALU.add, inc=s_fres)
                fresv[tb] = s_fres.v
            return s_fres, s_fres.v

        pn_ready = [("scalar", s_res, 8), ("vector", s_res, 8),
                    ("tensor", s_vev, NSPAN),
                    ("tensor", s_uev, NCH),
                    ("tensor", s_rev[0], s_rev[0].v), ("tensor", s_rev[1], s_rev[1].v),
                    ("scalar", s_hp[0], 16 * (NTILE + 1))]
        if dbg_hfm is not None:
            pn_ready.append(("scalar", B.sems["s_dh"], B.sems["s_dh"].v))
        s_hq0, hq0 = emit_prenorm(32, 24, 0, pn_ready)
        if dbg_h2 is not None:
            s_dbg = S_("s_dbg")
            wait("gpsimd", s_hq0, hq0)
            for _dc in range(0, 8 * TCORE, 2048):
                dma("gpsimd", dbg_h2[:, _dc:_dc + 2048], hfm[:, _dc:_dc + 2048], s_dbg)
        if "s_dbg" in B.sems:
            wait("vector", B.sems["s_dbg"], B.sems["s_dbg"].v)
        s_f0, f0cnt = emit_ffn(0, 40, 1, 2, 3, s_hq0, hq0)
        if dbg_x2 is not None:
            s_dbg = S_("s_dbg")
            wait("gpsimd", s_f0, f0cnt)
            for _dc in range(0, 8 * TCORE, 2048):
                dma("gpsimd", dbg_x2[:, _dc:_dc + 2048], A[:, _dc:_dc + 2048], s_dbg)

    if phases >= 4:
        NKB = TCORE // 128
        HED = 16
        s_at = S_("hat"); s_av = S_("hav"); s_as = S_("has")
        s_ag2 = S_("hag"); s_hl = S_("hlo")
        if "s_dbg" in B.sems:
            wait("vector", B.sems["s_dbg"], B.sems["s_dbg"].v)
        s_qw = [S_("gsp0"), S_("gsp1")]

        pn_ready1 = [("scalar", s_f0, f0cnt), ("vector", s_f0, f0cnt),
                     ("tensor", s_f0, f0cnt),
                     ("scalar", B.sems["fmm"], B.sems["fmm"].v),
                     ("vector", B.sems["fmm"], B.sems["fmm"].v)]
        s_h3, h3cnt = emit_prenorm(56, 48, None, pn_ready1)

        h1b = gpub[0:64, 0:TCORE]
        vbuf = gpub[0:64, 0:TCORE]
        kbuf = tt1.bitcast(BF16)[0:64, 0:TCORE]
        sqb64 = tt1.bitcast(BF16)[0:64, 0:TCORE]
        pbuf = vxf[0]
        kaug = vxf[1][0:65, 0:TCORE]
        qaug = vxf[1][0:65, 4 * VSPAN:4 * VSPAN + TCORE]
        vtokaug = w0t_sb.bitcast(BF16)[:, 0:NKB * 65]
        lamr0 = xhat.bitcast(BF16)[0:1, 0:TCORE]

        wait("sync", s_hl, s_hl.v)
        wait("sync", s_f0, f0cnt)
        wait("sync", B.sems["pnxt"], B.sems["pnxt"].v)
        dma("sync", rows2[0:1, 0:64], wl1t[:, :], s_hl)
        dma("sync", rowsflat[0:1, 0:TCORE], pos[:, :], s_hl)
        wait("scalar", s_hl, s_hl.v)
        wait("scalar", s_f0, f0cnt)
        wait("scalar", B.sems["pna"], B.sems["pna"].v)
        wait("scalar", B.sems["pnq"], B.sems["pnq"].v)
        fop("scalar", "copy", lamr0[0:1, 0:64], rows2[0:1, 0:64])
        fop("scalar", "copy", rows2[0:1, 64:128], lamr0[0:1, 0:64])
        op("scalar", "activation", rowsflat[0:1, 0:TCORE], rowsflat[0:1, 0:TCORE],
           AF.Ln, inc=s_as)
        for qb in range(NTB):
            wait("tensor", s_as, s_as.v)
            if qb == 0:
                wait("tensor", s_f0, f0cnt)
                wait("tensor", B.sems["pnxv"], B.sems["pnxv"].v)
                wait("tensor", B.sems["pnxs"], B.sems["pnxs"].v)
                wait("tensor", B.sems["pnv"], B.sems["pnv"].v)
                wait("tensor", s_b3, s_b3.v)
                wait("tensor", B.sems["s_gld"], B.sems["s_gld"].v)
            op("tensor", "matmul", pb[6][0:64, 0:TB], rows2[0:1, 64:128],
               rowsflat[0:1, qb * TB:qb * TB + TB], start=True, stop=True, inc=s_at)
            wait("scalar", s_at, s_at.v)
            op("scalar", "activation", h1b[:, qb * TB:qb * TB + TB], pb[6][0:64, 0:TB],
               AF.Relu, scale=1.0, bias=bl1c_sb[:, 0:1], inc=s_as)
            wait("tensor", s_as, s_as.v)
            if qb == 0:
                wait("tensor", S_("hag"), 16)
            op("tensor", "matmul", pb[7][0:16, 0:TB], wl2b[:, :],
               h1b[:, qb * TB:qb * TB + TB], start=True, stop=True, inc=s_at)
            wait("scalar", s_at, s_at.v)
            op("scalar", "activation", xhat[0:16, 512:512 + TB] if TB == 512 else
               xhat[0:16, TB:2 * TB], pb[7][0:16, 0:TB],
               AF.Sigmoid, scale=1.0, bias=bl2c_sb[:, 0:1], inc=s_as)
            wait("vector", s_as, s_as.v)
            op("vector", "tensor_scalar", lamsc[:, qb * TB:qb * TB + TB],
               xhat[0:16, 512:512 + TB] if TB == 512 else xhat[0:16, TB:2 * TB],
               float(64 ** -0.5), None, op0=ALU.mult, inc=s_av)
        LAMDONE = s_av.v

        qw = [carep.bitcast(BF16)[:, 0:1024], sarep.bitcast(BF16)[:, 0:1024]]
        qstg = [wbhi[:, 0:TB], wbhi[:, 512:512 + TB] if TB == 512 else wbhi[:, TB:2 * TB]]
        s_qm = S_("fmm"); s_qe = S_("pnv"); s_qs = [S_("ssp0"), S_("ssp1")]
        qmv = {}
        qevv = {}
        for fb3 in range(24 if "noqkv" not in debug else 0):
            par = fb3 % 2
            w = qw[par]; sw = s_qw[par]
            wait("gpsimd", sw, sw.v)
            if fb3 >= 2:
                wait("gpsimd", s_qm, qmv[fb3 - 2])
            for q in range(8):
                dma("gpsimd", w[:, q * 128:(q + 1) * 128],
                    wqkvt[q * 128:(q + 1) * 128, fb3 * 128:(fb3 + 1) * 128], sw)
            wait("tensor", sw, sw.v)
            if fb3 == 0:
                wait("tensor", s_h3, h3cnt)
            for tb in range(NTB):
                if fb3 > 0:
                    wait("tensor", s_qe, qevv.get((fb3 - 1, tb), 0))
                for q in range(8):
                    op("tensor", "matmul", pb[tb][:, 0:TB], w[:, q * 128:(q + 1) * 128],
                       hfm[:, q * TCORE + tb * TB: q * TCORE + tb * TB + TB],
                       start=(q == 0), stop=(q == 7), inc=(s_qm if q == 7 else None))
            qmv[fb3] = s_qm.v
            for tb in range(NTB):
                sl = (fb3 * NTB + tb) % 2
                wait("vector", s_qm, qmv[fb3])
                wait("vector", s_qs[sl], s_qs[sl].v)
                op("vector", "tensor_scalar", qstg[sl], pb[tb][:, 0:TB], 1.0, None,
                   op0=ALU.mult, inc=s_qe)
                qevv[(fb3, tb)] = s_qe.v
                wait("scalar", s_qe, s_qe.v)
                wait("scalar", s_qs[sl], s_qs[sl].v)
                dma("scalar", dscr1.ap()[fb3 * 128:(fb3 + 1) * 128, tb * TB:tb * TB + TB],
                    qstg[sl], s_qs[sl])
        QKVMM = s_qm.v

        klv = {}
        pvrec = {}
        if "noheads" in debug:
            HED = 0
        for h in range(HED):
            wait("sync", s_hl, s_hl.v)
            if h == 0:
                wait("sync", s_qs[0], s_qs[0].v)
                wait("sync", s_qs[1], s_qs[1].v)
                wait("sync", B.sems["fts"], B.sems["fts"].v)
                wait("sync", s_av, LAMDONE)
            else:
                wait("sync", s_av, s_av.v)
                wait("sync", s_at, s_at.v)
            dma("sync", kbuf[0:64, :], dscr1.ap()[1024 + 64 * h:1024 + 64 * (h + 1), :], s_hl)
            dma("sync", qaug[0:64, :], dscr1.ap()[64 * h:64 * (h + 1), :], s_hl)
            dma("sync", vbuf[0:64, :], dscr1.ap()[2048 + 64 * h:2048 + 64 * (h + 1), :], s_hl)
            wait("gpsimd", s_ag2, s_ag2.v)
            wait("gpsimd", s_av, LAMDONE if h == 0 else s_av.v)
            if h > 0:
                wait("gpsimd", s_at, s_at.v)
            dma("gpsimd", lamr0[0:1, 0:TCORE], lamsc[h:h + 1, :], s_ag2)
            if h == 0:
                wait("scalar", s_ag2, s_ag2.v)
                op("scalar", "activation", kaug[64:65, :], lamr0[0:1, 0:TCORE],
                   AF.Identity, scale=np.float32(0.0), bias=np.float32(1.0), inc=s_as)
            for tb in range(NTB):
                wait("tensor", s_ag2, s_ag2.v)
                if h > 0 or tb > 0:
                    wait("tensor", s_av, klv.get("b6", 0))
                op("tensor", "matmul", pb[6][0:64, 0:TB], ones_rb[0:1, 0:64],
                   lamr0[0:1, tb * TB:tb * TB + TB], start=True, stop=True, inc=s_at)
                wait("vector", s_at, s_at.v)
                wait("vector", s_hl, s_hl.v)
                op("vector", "tensor_tensor", kaug[0:64, tb * TB:tb * TB + TB],
                   kbuf[0:64, tb * TB:tb * TB + TB], pb[6][0:64, 0:TB],
                   op=ALU.mult, inc=s_av)
                klv["b6"] = s_av.v
            if "klonly" in debug:
                continue
            if "nomk" not in debug:
                wait("vector", s_av, s_av.v)
                wait("vector", s_at, s_at.v)
                op("vector", "tensor_tensor", sqb64[:, 0:TCORE], kaug[0:64, 0:TCORE],
                   kaug[0:64, 0:TCORE], op=ALU.mult, inc=s_av)
                for qb in range(NTB):
                    wait("tensor", s_av, s_av.v)
                    wait("tensor", s_as, s_as.v)
                    if h == 0 and qb == 0:
                        wait("tensor", B.sems["pnxs"], B.sems["pnxs"].v)
                    op("tensor", "matmul", pb[7][0:1, 0:TB], ones_cb[0:64, 0:1],
                       sqb64[:, qb * TB:qb * TB + TB], start=True, stop=True, inc=s_at)
                    wait("scalar", s_at, s_at.v)
                    if h == 0 and qb == 0:
                        wait("scalar", s_as, s_as.v)
                    op("scalar", "copy", rowsflat[0:1, qb * TB:qb * TB + TB],
                       pb[7][0:1, 0:TB], inc=s_as)
                wait("vector", s_as, s_as.v)
                op("vector", "tensor_reduce", mxr[0:1, 18:19],
                   rowsflat[0:1, 0:TCORE].rearrange("o (g k) -> o g k", g=1),
                   axis=AX.X, op=ALU.max, inc=s_av)
                wait("scalar", s_av, s_av.v)
                op("scalar", "activation", mxr[0:1, 18:19], mxr[0:1, 18:19], AF.Sqrt, inc=s_as)
                wait("vector", s_as, s_as.v)
                op("vector", "tensor_scalar", mxr[0:1, 19:20], mxr[0:1, 18:19], -1.0, None,
                   op0=ALU.mult, inc=s_av)
                wait("vector", s_at, s_at.v)
                op("vector", "tensor_tensor", sqb64[:, 0:TCORE], qaug[0:64, 0:TCORE],
                   qaug[0:64, 0:TCORE], op=ALU.mult, inc=s_av)
                for qb in range(NTB):
                    wait("tensor", s_av, s_av.v)
                    wait("tensor", s_as, s_as.v)
                    op("tensor", "matmul", pb[7][0:1, 0:TB], ones_cb[0:64, 0:1],
                       sqb64[:, qb * TB:qb * TB + TB], start=True, stop=True, inc=s_at)
                    wait("scalar", s_at, s_at.v)
                    op("scalar", "copy", rows2[0:1, qb * TB:qb * TB + TB],
                       pb[7][0:1, 0:TB], inc=s_as)
                wait("scalar", s_as, s_as.v)
                op("scalar", "activation", rows2[0:1, 0:TCORE], rows2[0:1, 0:TCORE],
                   AF.Sqrt, inc=s_as)
                wait("vector", s_as, s_as.v)
                op("vector", "tensor_scalar", rows2[0:1, 0:TCORE], rows2[0:1, 0:TCORE],
                   mxr[0:1, 19:20], 75.0, op0=ALU.mult, op1=ALU.add, inc=s_av)
                wait("scalar", s_av, s_av.v)
                op("scalar", "copy", qaug[64:65, 0:TCORE], rows2[0:1, 0:TCORE], inc=s_as)
            QAUGR = s_as.v
            for kb2 in range(NKB // 2):
                wait("tensor", s_hl, s_hl.v)
                wait("tensor", s_av, klv["b6"])
                wait("tensor", s_as, s_as.v)
                op("tensor", "transpose", pbb[6][:, 0:64],
                   vbuf[:, 2 * kb2 * 128:(2 * kb2 + 1) * 128], ident_b[0:64, 0:64])
                op("tensor", "transpose", pbb[6][:, 64:128],
                   vbuf[:, (2 * kb2 + 1) * 128:(2 * kb2 + 2) * 128], ident_b[0:64, 0:64],
                   inc=s_at)
                wait("scalar", s_at, s_at.v)
                op("scalar", "copy", vtokaug[:, 2 * kb2 * 65:2 * kb2 * 65 + 64],
                   pbb[6][:, 0:64])
                op("scalar", "copy", vtokaug[:, (2 * kb2 + 1) * 65:(2 * kb2 + 1) * 65 + 64],
                   pbb[6][:, 64:128], inc=s_as)
            if "no3d" not in debug:
                op("scalar", "activation",
                   vtokaug.rearrange("p (b c) -> p b c", c=65)[:, :, 64:65],
                   ident_b[:, 0:NKB].rearrange("p (b c) -> p b c", c=1),
                   AF.Identity, scale=np.float32(0.0), bias=np.float32(1.0), inc=s_as)
            VTOKR = s_as.v
            for kb in range(NKB if "noqk" not in debug else 0):
                par = kb % 2
                for qb in range(NTB):
                    wait("tensor", s_as, max(QAUGR, VTOKR))
                    wait("tensor", s_as, s_as.v)
                    wait("tensor", s_av, klv["b6"])
                    if h == 0 and kb == 0 and qb == 0:
                        wait("tensor", s_qe, s_qe.v)
                    op("tensor", "matmul", pb[4 + qb][:, 0:TB],
                       kaug[:, kb * 128:(kb + 1) * 128],
                       qaug[:, qb * TB:qb * TB + TB], start=True, stop=True, inc=s_at)
                    wait("scalar", s_at, s_at.v)
                    if kb >= 2:
                        wait("scalar", s_at, pvrec.get(par, 0))
                    op("scalar", "activation",
                       pbuf[:, par * TCORE + qb * TB: par * TCORE + qb * TB + TB],
                       pb[4 + qb][:, 0:TB], AF.Exp, inc=s_as)
                    wait("tensor", s_as, s_as.v)
                    op("tensor", "matmul", pb[qb][0:65, 0:TB],
                       vtokaug[:, kb * 65:kb * 65 + 65],
                       pbuf[:, par * TCORE + qb * TB: par * TCORE + qb * TB + TB],
                       start=(kb == 0), stop=(kb == NKB - 1),
                       inc=(s_at if qb == NTB - 1 else None))
                pvrec[par] = s_at.v
            for qb in range(NTB if "noqk" not in debug else 0):
                wait("scalar", s_at, s_at.v)
                op("scalar", "copy", rowsflat[0:1, qb * TB:qb * TB + TB],
                   pb[qb][64:65, 0:TB], inc=s_as)
            wait("vector", s_as, s_as.v)
            fop("vector", "tensor_scalar", rowsflat[0:1, 0:TCORE],
                rowsflat[0:1, 0:TCORE], 1e-30, None, op0=ALU.max)
            op("vector", "reciprocal", rows2[0:1, 0:TCORE], rowsflat[0:1, 0:TCORE],
               inc=s_av)
            rrsb = wkvvt_sb.bitcast(F32)[0:64, 0:TB]
            for qb in range(NTB if "noqk" not in debug else 0):
                wait("tensor", s_av, s_av.v)
                op("tensor", "matmul", pb[4 + qb][0:64, 0:TB], ones_f[0:1, 0:64],
                   rows2[0:1, qb * TB:qb * TB + TB], start=True, stop=True, inc=s_at)
                wait("scalar", s_at, s_at.v)
                if h == 0 and qb == 0:
                    wait("scalar", B.sems["fvq"], B.sems["fvq"].v)
                else:
                    wait("scalar", s_av, s_av.v)
                op("scalar", "copy", rrsb, pb[4 + qb][0:64, 0:TB], inc=s_as)
                wait("vector", s_as, s_as.v)
                if h == 0 and qb == 0:
                    wait("vector", s_qm, QKVMM)
                op("vector", "tensor_tensor",
                   hfm[64 * (h % 2):64 * (h % 2) + 64,
                       (h // 2) * TCORE + qb * TB: (h // 2) * TCORE + qb * TB + TB],
                   pb[qb][0:64, 0:TB], rrsb, op=ALU.mult, inc=s_av)
        ATTNV = s_av.v

        s_om = S_("fmm"); s_oev2 = S_("pnv")
        omv = {}
        oevv = {}
        for ob in range(8 if "noheads" not in debug else 0):
            par = ob % 2
            w = qw[par]; sw = s_qw[par]
            wait("gpsimd", sw, sw.v)
            if ob == 0:
                wait("gpsimd", s_qm, QKVMM)
            if ob >= 2:
                wait("gpsimd", s_om, omv[ob - 2])
            for q in range(8):
                dma("gpsimd", w[:, q * 128:(q + 1) * 128],
                    woutt[q * 128:(q + 1) * 128, ob * 128:(ob + 1) * 128], sw)
            wait("tensor", sw, sw.v)
            if ob == 0:
                wait("tensor", s_av, ATTNV)
            for tb in range(NTB):
                if ob > 0:
                    wait("tensor", s_oev2, oevv.get((ob - 1, tb), 0))
                for q in range(8):
                    op("tensor", "matmul", pb[tb][:, 0:TB], w[:, q * 128:(q + 1) * 128],
                       hfm[:, q * TCORE + tb * TB: q * TCORE + tb * TB + TB],
                       start=(q == 0), stop=(q == 7), inc=(s_om if q == 7 else None))
            omv[ob] = s_om.v
            for tb in range(NTB):
                sl = (ob * NTB + tb) % 2
                ab = xhat[:, sl * TB:sl * TB + TB] if TB == 512 else \
                     xhat[:, sl * TB:sl * TB + TB]
                wait("scalar", s_om, omv[ob])
                if ob >= 1:
                    wait("scalar", s_oev2, oevv.get((ob - 1, tb), 0))
                op("scalar", "activation", ab, pb[tb][:, 0:TB], AF.Identity,
                   scale=1.0, bias=boutc_sb[:, ob:ob + 1], inc=s_as)
                wait("vector", s_as, s_as.v)
                op("vector", "scalar_tensor_tensor",
                   A[:, ob * TCORE + tb * TB: ob * TCORE + tb * TB + TB],
                   ab, modT[:, 64 + ob:65 + ob],
                   A[:, ob * TCORE + tb * TB: ob * TCORE + tb * TB + TB],
                   op0=ALU.mult, op1=ALU.add, inc=s_oev2)
                oevv[(ob, tb)] = s_oev2.v
        s_x3, x3cnt = s_oev2, s_oev2.v
        if dbg_x3 is not None:
            s_dbg = S_("s_dbg")
            wait("gpsimd", s_x3, x3cnt)
            for _dc in range(0, 8 * TCORE, 2048):
                dma("gpsimd", dbg_x3[:, _dc:_dc + 2048], A[:, _dc:_dc + 2048], s_dbg)

    if phases >= 5:
        pn_ready2 = [("scalar", s_x3, x3cnt), ("vector", s_x3, x3cnt),
                     ("tensor", s_x3, x3cnt),
                     ("scalar", B.sems["fmm"], B.sems["fmm"].v),
                     ("vector", B.sems["fmm"], B.sems["fmm"].v),
                     ("tensor", B.sems["has"], B.sems["has"].v),
                     ("vector", B.sems["has"], B.sems["has"].v),
                     ("scalar", B.sems["has"], B.sems["has"].v),
                     ("scalar", B.sems["hav"], B.sems["hav"].v),
                     ("scalar", B.sems["hat"], B.sems["hat"].v),
                     ("vector", B.sems["hat"], B.sems["hat"].v),
                     ("vector", B.sems["pnv"], B.sems["pnv"].v)]
        s_hq1, hq1 = emit_prenorm(80, 72, 4, pn_ready2)
        if "s_dbg" in B.sems:
            wait("vector", B.sems["s_dbg"], B.sems["s_dbg"].v)
        s_f1, f1cnt = emit_ffn(1, 88, 5, 6, 7, s_hq1, hq1)

    if phases >= 6:
        s_otr = S_("hat"); s_oe = S_("has"); s_out = S_("s_out")
        outv = {}
        for i in range(NTILE):
            par = i % 2
            ostg = xtile[par]
            for kk in range(4):
                if i == 0 and kk == 0:
                    wait("tensor", s_f1, f1cnt)
                    wait("tensor", B.sems["pnv"], B.sems["pnv"].v)
                wait("tensor", s_oe, s_oe.v)
                op("tensor", "transpose", pb[5][:, 0:128],
                   A[:, 2 * kk * TCORE + i * 128: 2 * kk * TCORE + (i + 1) * 128],
                   ident_f[:])
                op("tensor", "transpose", pb[5][:, 128:256],
                   A[:, (2 * kk + 1) * TCORE + i * 128: (2 * kk + 1) * TCORE + (i + 1) * 128],
                   ident_f[:], inc=s_otr)
                wait("scalar", s_otr, s_otr.v)
                if i >= 2 and kk == 0:
                    wait("scalar", s_out, outv[i - 2])
                op("scalar", "copy", ostg[:, 2 * kk * 128:(2 * kk + 1) * 128],
                   pb[5][:, 0:128])
                op("scalar", "copy", ostg[:, (2 * kk + 1) * 128:(2 * kk + 2) * 128],
                   pb[5][:, 128:256], inc=s_oe)
            wait("gpsimd", s_oe, s_oe.v)
            wait("gpsimd", s_out, s_out.v)
            dma("gpsimd", out_h[i * 128:(i + 1) * 128, :], ostg[:], s_out)
            outv[i] = s_out.v


    for semname in ["s_agy", "s_yp20", "s_yp21", "s_hp0", "s_agw", "s_wqp0", "s_wqp1", "s_res", "s_out", "fsp0", "fsp1", "gsp0", "gsp1"]:
        if semname in B.sems:
            sem = B.sems[semname]
            wait("gpsimd", sem, sem.v)
    for dbgsem in ("s_dmod", "s_dh", "s_dv2", "s_dyag", "s_dres", "s_dwq", "s_dbg"):
        if dbgsem in B.sems:
            wait("gpsimd", B.sems[dbgsem], B.sems[dbgsem].v)

    with nc.Block() as block:
        block.sync(lambda e: B.run("sync", e))
        block.scalar(lambda e: B.run("scalar", e))
        block.vector(lambda e: B.run("vector", e))
        block.tensor(lambda e: B.run("tensor", e))
        block.gpsimd(lambda e: B.run("gpsimd", e))

    return nc, douts


Builder.run = lambda self, name, e: [fn(e) for fn in self.items[name]] and None


def prep_inputs(ins, S=16384):
    TCORE = S // NCORES
    MODS = 6 * D // NCORES
    x = np.asarray(ins["L0_attn_x"], np.float32).reshape(-1, D)[:S]
    W0 = np.asarray(ins["L0_attn_W0"], np.float32)
    Wkv = np.asarray(ins["L0_attn_Wkv"], np.float32)
    bkv = np.asarray(ins["L0_attn_bkv"], np.float32)
    eta = np.asarray(ins["L0_attn_eta"], np.float32).reshape(1, 1)
    c0 = np.asarray(ins["L0_c"], np.float32).reshape(-1)
    wmod0 = np.asarray(ins["L0_wmod"], np.float32)
    c1 = np.asarray(ins["L1_c"], np.float32).reshape(-1)
    wmod1 = np.asarray(ins["L1_wmod"], np.float32)
    g0 = np.asarray(ins["L0_ffn_w_gate"], np.float32)
    u0 = np.asarray(ins["L0_ffn_w_up"], np.float32)
    d0 = np.asarray(ins["L0_ffn_w_down"], np.float32)
    g1 = np.asarray(ins["L1_ffn_w_gate"], np.float32)
    u1 = np.asarray(ins["L1_ffn_w_up"], np.float32)
    d1 = np.asarray(ins["L1_ffn_w_down"], np.float32)
    wqkvt = np.ascontiguousarray(np.asarray(ins["L1_attn_Wqkv"], np.float32).T)
    woutt = np.ascontiguousarray(np.asarray(ins["L1_attn_Wout"], np.float32).T)
    wl1t = np.ascontiguousarray(np.asarray(ins["L1_attn_Wl1"], np.float32).T)
    bl1c = np.ascontiguousarray(np.asarray(ins["L1_attn_bl1"], np.float32).reshape(64, 1))
    wl2t = np.ascontiguousarray(np.asarray(ins["L1_attn_Wl2"], np.float32).T)
    bl2c = np.ascontiguousarray(np.asarray(ins["L1_attn_bl2"], np.float32).reshape(16, 1))
    boutc = np.ascontiguousarray(np.asarray(ins["L1_attn_bout"], np.float32).reshape(8, 128).T)

    import ml_dtypes
    identbf = np.eye(128, dtype=ml_dtypes.bfloat16)
    identf = np.eye(128, dtype=np.float32)
    onesf = np.ones((1, 128), np.float32)
    trium = np.triu(np.ones((128, 128), np.float32), 1)
    onescolbf = np.ones((128, 1), ml_dtypes.bfloat16)

    maps = []
    for i in range(NCORES):
        fo = slice(128 * i, 128 * (i + 1))
        f3 = slice(384 * i, 384 * (i + 1))
        maps.append({
            "xin": np.ascontiguousarray(x[TCORE * i:TCORE * (i + 1)]),
            "w0t": np.ascontiguousarray(W0[fo, :].T),
            "wkvvt": np.ascontiguousarray(Wkv[D:, :][fo, :].T),
            "bkvv": np.ascontiguousarray(bkv[D:][fo].reshape(128, 1)),
            "eta": eta,
            "c0t": np.ascontiguousarray(c0.reshape(8, 128).T),
            "wmod0s": np.ascontiguousarray(wmod0[:, MODS * i:MODS * (i + 1)]),
            "c1t": np.ascontiguousarray(c1.reshape(8, 128).T),
            "wmod1s": np.ascontiguousarray(wmod1[:, MODS * i:MODS * (i + 1)]),
            "g0s": np.ascontiguousarray(g0[f3, :]),
            "u0s": np.ascontiguousarray(u0[f3, :]),
            "d0s": np.ascontiguousarray(d0[fo, :]),
            "g1s": np.ascontiguousarray(g1[f3, :]),
            "u1s": np.ascontiguousarray(u1[f3, :]),
            "d1s": np.ascontiguousarray(d1[fo, :]),
            "wqkvt": wqkvt, "woutt": woutt, "wl1t": wl1t, "bl1c": bl1c,
            "wl2t": wl2t, "bl2c": bl2c, "boutc": boutc,
            "pos": (np.arange(TCORE * i + 1, TCORE * (i + 1) + 1, dtype=np.float32)
                    .reshape(1, TCORE)),
            "identbf": identbf, "identf": identf, "onesf": onesf, "trium": trium,
            "onescolbf": onescolbf,
        })
    return maps


build_p01 = None


if __name__ == "__main__":
    nc, _ = build_full(S=2048, debug=("h", "mod", "y", "v"))
    print("built ok")


def kernel(**inputs):
    S = 16384
    maps = prep_inputs(inputs, S=S)
    nc, _ = build_full(S=S, debug=(), phases=6)
    res = run_bass_kernel_spmd(nc, maps, core_ids=list(range(NCORES)))
    out = np.concatenate([np.asarray(res.results[i]["out"], np.float32)
                          for i in range(NCORES)], axis=0)
    return out.reshape(1, S, 1024)
